# version 11
# baseline (speedup 1.0000x reference)
import numpy as np
import ml_dtypes
import concourse.bass as bass
import concourse.mybir as mybir
from concourse.bass_utils import run_bass_kernel_spmd
from contextlib import ExitStack

F32 = mybir.dt.float32
BF16 = mybir.dt.bfloat16
I32 = mybir.dt.int32
ALU = mybir.AluOpType
AFT = mybir.ActivationFunctionType
AX = mybir.AxisListType

B, T, C = 16, 2048, 1024
I_FF = 2816
NCORE = 8
SEQ = 2
TT = 4
NCH = 8
NIC = 22
SCALE = 0.125
C_RND = float(np.float32(3 * 2.0 ** 22))
EPS = 1e-5

STOP_AFTER = None
DOWN_ORDER = None
MARKS = []

ENGS = ("sync", "tensor", "vector", "scalar", "gpsimd")
SEMS = ("t", "v", "s", "g", "cc", "dc", "dxx", "dcl", "dgw", "dwc", "dar",
        "dwg", "dwu", "dwd0", "dwd1", "dsp0", "dsp1", "drb0", "drb1",
        "dou0", "dou1", "dxf0", "dxf1", "dxh", "dxw")
SEM_ENG = {"t": "tensor", "v": "vector", "s": "scalar", "g": "gpsimd"}


class Sched:
    def __init__(self):
        self.ops = {e: [] for e in ENGS}
        self.cnt = {s: 0 for s in SEMS}

    def emit(self, eng, fn, waits=()):
        w = {}
        for f in waits:
            if f is None:
                continue
            s, vv = f
            if vv <= 0:
                continue
            w[s] = max(w.get(s, 0), vv)
        self.ops[eng].append([tuple(w.items()), fn, None, 0])

    def inc_last(self, eng, sem, k=1):
        op = self.ops[eng][-1]
        assert op[2] is None
        op[2] = sem
        op[3] = k
        self.cnt[sem] += k
        return (sem, self.cnt[sem])

    def c(self, sem):
        return (sem, self.cnt[sem])


def build_core():
    nc = bass.Bass("TRN2", target_bir_lowering=False, debug=False,
                   detect_race_conditions=False)
    S = Sched()
    E = S.emit
    IL = S.inc_last
    cc = S.c
    MARKS.clear()
    MK = lambda lbl: MARKS.append((lbl, S.cnt["t"], S.cnt["v"], S.cnt["s"]))

    d_x = nc.dram_tensor("x", [SEQ, C, T], F32, kind="ExternalInput").ap()
    d_wqdT = nc.dram_tensor("wqdT", [C, C], F32, kind="ExternalInput").ap()
    d_wkdT = nc.dram_tensor("wkdT", [C, C], F32, kind="ExternalInput").ap()
    d_wvdT = nc.dram_tensor("wvdT", [C, C], F32, kind="ExternalInput").ap()
    d_ball = nc.dram_tensor("ball", [C, 4096], F32, kind="ExternalInput").ap()
    d_wc = nc.dram_tensor("wc", [C, C], F32, kind="ExternalInput").ap()
    d_wg = nc.dram_tensor("wg", [I_FF, C], F32, kind="ExternalInput").ap()
    d_wu = nc.dram_tensor("wu", [I_FF, C], F32, kind="ExternalInput").ap()
    d_wd = nc.dram_tensor("wd", [C, I_FF], F32, kind="ExternalInput").ap()
    d_cos = nc.dram_tensor("cos_t", [128, T], BF16, kind="ExternalInput").ap()
    d_sin = nc.dram_tensor("sin_t", [128, T], BF16, kind="ExternalInput").ap()
    d_dmask = nc.dram_tensor("dmask", [128, 4 * 512], BF16, kind="ExternalInput").ap()
    d_ident = nc.dram_tensor("ident", [128, 128], BF16, kind="ExternalInput").ap()
    d_clipx = nc.dram_tensor("clipx", [1, 1], F32, kind="ExternalInput").ap()
    d_clipd = nc.dram_tensor("clipd", [1, 1], F32, kind="ExternalInput").ap()
    d_out = nc.dram_tensor("out", [SEQ, C, T], F32, kind="ExternalOutput").ap()
    d_dbg = nc.dram_tensor("dbg", [16, 256], F32, kind="ExternalOutput").ap()
    HAVE_DBG = STOP_AFTER == "c1"
    if HAVE_DBG:
        d_dbg2 = nc.dram_tensor("dbg2", [128, 16384], F32, kind="ExternalOutput").ap()

    d_spill = nc.dram_tensor("h_spill", [SEQ * TT, 128, NIC * 512], F32)
    d_xh = nc.dram_tensor("xh", [SEQ, C, T], F32).ap()
    d_ar1i = nc.dram_tensor("ar1i", [1, 128], F32)
    d_ar1o = nc.dram_tensor("ar1o", [1, 128], F32)
    d_ar2i = nc.dram_tensor("ar2i", [1, 128], F32)
    d_ar2o = nc.dram_tensor("ar2o", [1, 128], F32)

    ctx = ExitStack()
    with ctx:
        sbt = lambda name, shape, dt=BF16: ctx.enter_context(nc.sbuf_tensor(name, shape, dt))
        cos_s = sbt("cos_s", [128, T])
        sin_s = sbt("sin_s", [128, T])
        dmask_s = sbt("dmask_s", [128, 4 * 512])
        ident_s = sbt("ident_s", [128, 128])
        ones_s = sbt("ones_s", [128, 128])
        onesf = sbt("onesf", [128, 1], F32)
        cbias = sbt("cbias", [128, 1], F32)
        epsb = sbt("epsb", [128, 1], F32)
        rsx_b = sbt("rsx_b", [128, 1], F32)
        rsh_b = sbt("rsh_b", [128, 1], F32)
        sh_b = sbt("sh_b", [128, 1], F32)
        sxs_b = sbt("sxs_b", [128, 1], F32)
        biasg = sbt("biasg", [128, 1], F32)
        tiny = sbt("tiny", [128, 16], F32)
        mbt = sbt("mbt", [128, 4])
        rowA = sbt("rowA", [1, 256], F32)
        rowC = sbt("rowC", [1, 16])
        gmx1 = sbt("gmx1", [1, 128], F32)
        gmx2 = sbt("gmx2", [1, 128], F32)
        sdv = sbt("sdv", [128, 8], F32)
        sgv = sbt("sgv", [128, NIC], F32)
        suv = sbt("suv", [128, NIC], F32)
        hred = sbt("hred", [128, 8 * NIC], F32)

        psA = ctx.enter_context(nc.psum_tensor("psA", [128, 2048], F32))
        psB = ctx.enter_context(nc.psum_tensor("psB", [128, 2048], F32))
        psAb = psA.ap().bitcast(BF16)
        psBb = psB.ap().bitcast(BF16)
        psAb = psA.ap().bitcast(BF16)

        sems = {}
        for s_ in SEMS:
            sems[s_] = ctx.enter_context(nc.semaphore("sem_" + s_))

        def xsv(chv, lo, n):
            return xs[:, chv * 4096 + lo: chv * 4096 + lo + n]

        E("gpsimd", lambda e: e.memset(ones_s[:, :], 1.0))
        E("gpsimd", lambda e: e.memset(onesf[:, :], 1.0))
        E("gpsimd", lambda e: e.memset(cbias[:, :], C_RND))
        E("gpsimd", lambda e: e.memset(epsb[:, :], EPS))
        E("gpsimd", lambda e: e.memset(rowA[:, :], 1.0))
        E("gpsimd", lambda e: e.memset(tiny[:, :], 0.0))
        IL("gpsimd", "g")
        g_setup = cc("g")
        E("sync", lambda e: e.dma_start(out=cos_s[:, :], in_=d_cos[:, :]))
        IL("sync", "dc", 16)
        E("sync", lambda e: e.dma_start(out=sin_s[:, :], in_=d_sin[:, :]))
        IL("sync", "dc", 16)
        E("sync", lambda e: e.dma_start(out=dmask_s[:, :], in_=d_dmask[:, :]))
        IL("sync", "dc", 16)
        E("sync", lambda e: e.dma_start(out=ident_s[:, :], in_=d_ident[:, :]))
        IL("sync", "dc", 16)
        E("gpsimd", lambda e: e.dma_start(out=rowA[0:1, 0:1], in_=d_clipx[:, :]), waits=[g_setup])
        IL("gpsimd", "dcl", 16)
        E("gpsimd", lambda e: e.dma_start(out=rowA[0:1, 1:2], in_=d_clipd[:, :]))
        IL("gpsimd", "dcl", 16)
        for s_ in range(SEQ):
            for chv in range(NCH):
                E("gpsimd", lambda e, s_=s_, chv=chv: e.dma_start(
                    out=xsv(chv, s_ * T, T),
                    in_=d_x[s_, chv * 128:(chv + 1) * 128, :]))
                IL("gpsimd", "dxx", 16)
        f_dx_x = cc("dxx")
        f_dx_const = cc("dc")

        def _recip_bf(e, out, in_):
            with nc.allow_low_precision(reason="1/rowsum to bf16; same rounding as fp32 recip + bf16 copy"):
                return e.reciprocal(out, in_)

        def EV(fn, waits=()):
            E("vector", fn, waits=list(waits) + [cc("v")])
            return IL("vector", "v")

        def bcast128(col, dst, waits_v=()):
            EV(lambda e: e.tensor_copy(rowC[0:1, 8:9], rowA[0:1, col:col + 1]),
               waits=waits_v)
            EV(lambda e: e.tensor_copy(tiny[0:1, 15:16], rowC[0:1, 8:9]))
            EV(lambda e: e.tensor_sub(tiny[0:1, 14:15], rowA[0:1, col:col + 1],
                                      tiny[0:1, 15:16]))
            f_v = EV(lambda e: e.tensor_copy(rowC[0:1, 9:10], tiny[0:1, 14:15]))
            E("tensor", lambda e: e.matmul(psB[:, 1536:1537], ones_s[0:1, :], rowC[0:1, 8:9],
                                           start=True, stop=False), waits=[f_v])
            E("tensor", lambda e: e.matmul(psB[:, 1536:1537], ones_s[0:1, :], rowC[0:1, 9:10],
                                           start=False, stop=True))
            f_t = IL("tensor", "t")
            return EV(lambda e: e.tensor_copy(dst[:, :], psB[:, 1536:1537]), waits=[f_t])

        def pmax128(tc, dstc, mc, waits_v=()):
            f_v = EV(lambda e: e.tensor_copy(mbt[:, mc:mc + 1], tiny[:, tc:tc + 1]),
                     waits=waits_v)
            E("tensor", lambda e: e.matmul(psB[0:1, 1536:1664], mbt[:, mc:mc + 1],
                                           ident_s[:, :], start=True, stop=True),
              waits=[f_v])
            f_t = IL("tensor", "t")
            EV(lambda e: e.tensor_copy(rowA[0:1, 128:256], psB[0:1, 1536:1664]),
               waits=[f_t])
            return EV(lambda e: e.tensor_reduce(rowA[0:1, dstc:dstc + 1],
                                                rowA[0:1, 128:256], AX.X, ALU.max))

        f_A_end = {}
        with ExitStack() as actx:
            sa = lambda name, shape, dt=BF16: actx.enter_context(nc.sbuf_tensor(name, shape, dt))
            xs = sa("xs", [128, NCH * 4096])
            eff = sa("eff", [128, NCH * 1024])
            qcb = sa("qcb", [128, T]); qeb = sa("qeb", [128, T])
            kcb = sa("kcb", [128, T]); keb = sa("keb", [128, T])
            erb = sa("erb", [128, T])
            vsb = sa("vsb", [128, 16 * 260])
            attu = sa("attu", [128, 2 * 2048])
            ysb = sa("ysb", [128, NCH * 2048])
            zr = sa("zr", [128, 64], F32)
            zrb = sa("zrb", [128, 2048])
            wdt3 = sa("wdt3", [128, 3 * 2048])
            xf = wdt3[:, 0:2048].bitcast(F32)
            r1f = wdt3[:, 2048:4096].bitcast(F32)
            ballg = sa("ballg", [128, 2048])
            wcs = sa("wcs", [128, 2 * 1024])
            wcf = sa("wcf", [128, 2 * 1024], F32)
            strow = sa("strow", [1, 1536], F32)
            strow_b = sa("strow_b", [1, 1536])

            E("gpsimd", lambda e: e.memset(zr[:, :], 1.0), waits=[g_setup])
            IL("gpsimd", "g")

            f_exp0_last = None
            f_exp1_last = None
            f_avnorm_last = None
            f_proj_prev = None
            f_vsb_free = None
            f_ln1_v = None

            for s_ in range(SEQ):
                t0 = s_ * T
                for g in range(4):
                    MK(f"s{s_}g{g}_start")
                    hl0 = 256 * g
                    for wi, dsrc in enumerate((d_wqdT, d_wkdT, d_wvdT)):
                        E("gpsimd", lambda e, wi=wi, dsrc=dsrc, hl0=hl0: e.dma_start(
                            out=wdt3[:, wi * 2048: (wi + 1) * 2048]
                                .rearrange("p (a n) -> p a n", a=2),
                            in_=dsrc[hl0:hl0 + 256, :].rearrange("(a p) n -> p a n", p=128)),
                          waits=[f_proj_prev, f_dx_x if (s_ == 0 and g == 0) else None,
                                 f_ln1_v, cc("dxw") if g == 0 else None])
                        IL("gpsimd", "dgw", 16)
                    E("gpsimd", lambda e, hl0=hl0, g=g: e.dma_start(
                        out=ballg[:].rearrange("p (a n) -> p a n", a=2),
                        in_=d_ball[hl0:hl0 + 256, 1024 * g:1024 * (g + 1)]
                            .rearrange("(a p) n -> p a n", p=128)))
                    IL("gpsimd", "dgw", 16)
                    f_gdma = cc("dgw")

                    f_effmm = [None] * NCH
                    f_effev = [None] * NCH
                    for chv in range(NCH):
                        cols = ((0, 384, psA), (512, 384, psA), (1024, 256, psA)) if chv % 2 == 0 \
                            else ((1536, 384, psA), (0, 384, psB), (512, 256, psB))
                        for ri, (pb, nn, ps) in enumerate(cols):
                            r0 = (0, 384, 768)[ri]
                            for a in range(2):
                                w_ = []
                                if chv == 0 and ri == 0 and a == 0:
                                    w_ = [f_gdma, f_dx_const, f_exp0_last,
                                          f_exp1_last, f_avnorm_last, f_ln1_v]
                                if chv >= 2 and ri == 0 and a == 0:
                                    w_ = [f_effev[chv - 2]]
                                E("tensor", lambda e, chv=chv, ri=ri, r0=r0, pb=pb, nn=nn,
                                  ps=ps, a=a: e.matmul(
                                    ps[:, pb:pb + nn],
                                    wdt3[:, ri * 2048 + a * 1024 + chv * 128:
                                         ri * 2048 + a * 1024 + chv * 128 + 128],
                                    ballg[:, a * 1024 + r0: a * 1024 + r0 + nn],
                                    start=(a == 0), stop=(a == 1)), waits=w_)
                        f_effmm[chv] = IL("tensor", "t")
                        for ri, (pb, nn, ps) in enumerate(cols):
                            co = (0, 384, 768)[ri]
                            E("vector", lambda e, chv=chv, pb=pb, nn=nn, ps=ps, co=co:
                              e.tensor_copy(eff[:, chv * 1024 + co: chv * 1024 + co + nn],
                                            ps[:, pb:pb + nn]),
                              waits=[f_effmm[chv]] + ([f_proj_prev] if chv == 0 and ri == 0 else []))
                        f_effev[chv] = IL("vector", "v")
                    f_eff_done = f_effev[NCH - 1]

                    dests = ((qcb, 0), (qeb, 1), (erb, 2), (kcb, 3), (keb, 4), (erb, 5))
                    f_pj = {}
                    f_rope_done = None
                    for di, (dst, ci) in enumerate(dests):
                        for tt in range(TT):
                            pb = 512 * (tt % 2)
                            w_ = [f_eff_done] if (di == 0 and tt <= 1) else []
                            prev = f_pj.get((di, tt - 2)) or f_pj.get((di - 1, tt + 2))
                            if prev is not None:
                                w_.append(prev)
                            if di == 5 and tt == 0 and f_rope_done is not None:
                                w_.append(f_rope_done)
                            for chv in range(NCH):
                                E("tensor", lambda e, ci=ci, tt=tt, chv=chv, pb=pb, t0=t0: e.matmul(
                                    psA[:, pb:pb + 512],
                                    eff[:, chv * 1024 + ci * 128: chv * 1024 + ci * 128 + 128],
                                    xsv(chv, t0 + tt * 512, 512),
                                    start=(chv == 0), stop=(chv == NCH - 1)),
                                  waits=w_ if chv == 0 else [])
                            f_mm = IL("tensor", "t")
                            E("vector", lambda e, dst=dst, tt=tt, pb=pb: e.tensor_copy(
                                dst[:, tt * 512:(tt + 1) * 512], psA[:, pb:pb + 512]),
                              waits=[f_mm])
                            f_pj[(di, tt)] = IL("vector", "v")
                        if di == 2:
                            E("vector", lambda e: e.tensor_mul(erb[:, :], erb[:, :], sin_s[:, :]),
                              waits=[f_pj[(di, TT - 1)]])
                            E("vector", lambda e: e.tensor_mul(qeb[:, :], qeb[:, :], cos_s[:, :]))
                            E("vector", lambda e: e.tensor_add(qeb[:, :], qeb[:, :], erb[:, :]))
                            f_rope_done = IL("vector", "v")
                        if di == 5:
                            E("vector", lambda e: e.tensor_mul(erb[:, :], erb[:, :], sin_s[:, :]),
                              waits=[f_pj[(di, TT - 1)]])
                            E("vector", lambda e: e.tensor_mul(keb[:, :], keb[:, :], cos_s[:, :]))
                            E("vector", lambda e: e.tensor_add(keb[:, :], keb[:, :], erb[:, :]))
                            f_ropek = IL("vector", "v")
                    f_qk_done = cc("v")

                    E("vector", lambda e: e.memset(vsb[:, :], 1.0),
                      waits=[f_vsb_free, f_eff_done])
                    f_vsb_ones = IL("vector", "v")
                    f_vp = {}
                    for tc in range(16):
                        pb = 1024 + 512 * (tc % 2)
                        w_ = [f_vsb_ones] if tc == 0 else []
                        if tc >= 2:
                            w_.append(f_vp[tc - 2])
                        for chv in range(NCH):
                            E("tensor", lambda e, tc=tc, chv=chv, pb=pb, t0=t0: e.matmul(
                                psA[:, pb:pb + 256],
                                xsv(chv, t0 + tc * 128, 128),
                                eff[:, chv * 1024 + 768: chv * 1024 + 1024],
                                start=(chv == 0), stop=(chv == NCH - 1)),
                              waits=w_ if chv == 0 else [])
                        f_mm = IL("tensor", "t")
                        E("vector", lambda e, tc=tc, pb=pb: e.tensor_copy(
                            vsb[:, 260 * tc: 260 * tc + 260]
                                .rearrange("p (j d) -> p j d", j=4)[:, :, 0:64],
                            psA[:, pb:pb + 256].rearrange("p (j d) -> p j d", j=4)),
                          waits=[f_mm])
                        f_vp[tc] = IL("vector", "v")
                    f_v_done = cc("v")

                    EV(lambda e: e.tensor_reduce(tiny[:, 0:1], qcb[:, :], AX.X,
                                                 ALU.max, apply_absolute_value=True),
                       waits=[f_qk_done])
                    EV(lambda e: e.tensor_reduce(tiny[:, 1:2], qeb[:, :], AX.X,
                                                 ALU.max, apply_absolute_value=True))
                    EV(lambda e: e.tensor_max(tiny[:, 0:1], tiny[:, 0:1], tiny[:, 1:2]))
                    EV(lambda e: e.tensor_reduce(tiny[:, 2:3], kcb[:, :], AX.X,
                                                 ALU.max, apply_absolute_value=True))
                    EV(lambda e: e.tensor_reduce(tiny[:, 3:4], keb[:, :], AX.X,
                                                 ALU.max, apply_absolute_value=True))
                    EV(lambda e: e.tensor_max(tiny[:, 2:3], tiny[:, 2:3], tiny[:, 3:4]))
                    pmax128(0, 2, 0)
                    pmax128(2, 3, 1)
                    EV(lambda e: e.tensor_mul(rowA[0:1, 4:5], rowA[0:1, 2:3],
                                              rowA[0:1, 3:4]))
                    EV(lambda e: e.tensor_scalar(rowA[0:1, 4:5], rowA[0:1, 4:5],
                                                 -8.0, None, ALU.mult))
                    f_bias_v = EV(lambda e: e.tensor_copy(rowC[0:1, 4:5], rowA[0:1, 4:5]))
                    E("tensor", lambda e: e.matmul(psB[:, 1536:1537], ones_s[0:1, :],
                                                   rowC[0:1, 4:5], start=True, stop=True),
                      waits=[f_bias_v])
                    f_bias_t = IL("tensor", "t")
                    f_bias = EV(lambda e: e.tensor_copy(biasg[:, :], psB[:, 1536:1537]),
                                waits=[f_bias_t])

                    MK(f"s{s_}g{g}_proj")
                    pend = {"qt": None, "f_evac": None}

                    def emit_norm(g=g):
                        qn = pend["qt"]
                        f_ev_ = pend["f_evac"]
                        pend["qt"] = None
                        for j in range(4):
                            if j % 2 == 0:
                                E("tensor", lambda e, j=j: e.matmul(
                                    psB[0:64, 512 * j:512 * j + 512],
                                    ones_s[64:65, 0:64],
                                    zrb[64:65, 512 * j:512 * j + 512],
                                    start=True, stop=True, tile_position=(64, 0)),
                                  waits=[f_ev_] if j == 0 else [])
                            else:
                                E("tensor", lambda e, j=j: e.matmul(
                                    psB[64:128, 512 * j:512 * j + 512],
                                    ones_s[0:1, 0:64],
                                    zrb[0:1, 512 * j:512 * j + 512],
                                    start=True, stop=True, tile_position=(0, 64)))
                        f_outer = IL("tensor", "t")
                        for j in range(4):
                            yc = 2 * g + j // 2
                            pr = 64 * (j % 2)
                            E("vector", lambda e, j=j, yc=yc, pr=pr, qn=qn: e.tensor_mul(
                                ysb[pr:pr + 64, yc * 2048 + qn * 512: yc * 2048 + qn * 512 + 512],
                                ysb[pr:pr + 64, yc * 2048 + qn * 512: yc * 2048 + qn * 512 + 512],
                                psB[pr:pr + 64, 512 * j:512 * j + 512]),
                              waits=[f_outer] if j == 0 else [])
                        return IL("vector", "v")

                    for qt in range(TT):
                        nkt = 4 * qt + 4
                        f_av = {}
                        av_args = {}

                        def emit_av(kt, nkt):
                            a0, f_dep0, f_dep1 = av_args[kt]
                            w_av = [f_dep0]
                            if kt == 0:
                                w_av.append(f_avnorm_last)
                                w_av.append(f_v_done)
                            for j in range(4):
                                w_j = w_av if j == 0 else ([f_dep1] if j == 2 else [])
                                if j % 2 == 0:
                                    E("tensor", lambda e, j=j, kt=kt, a0=a0, nkt=nkt: e.matmul(
                                        psB[0:65, 512 * j:512 * (j + 1)],
                                        vsb[:, 260 * kt + 65 * j: 260 * kt + 65 * j + 65],
                                        attu[:, a0 + 512 * j: a0 + 512 * (j + 1)],
                                        start=(kt == 0), stop=(kt == nkt - 1)),
                                      waits=w_j)
                                else:
                                    E("tensor", lambda e, j=j, kt=kt, a0=a0, nkt=nkt: e.matmul(
                                        psB[64:128, 512 * j:512 * (j + 1)],
                                        vsb[:, 260 * kt + 65 * j: 260 * kt + 65 * j + 64],
                                        attu[:, a0 + 512 * j: a0 + 512 * (j + 1)],
                                        start=(kt == 0), stop=(kt == nkt - 1)))
                                    E("tensor", lambda e, j=j, kt=kt, a0=a0, nkt=nkt: e.matmul(
                                        psB[0:1, 512 * j:512 * (j + 1)],
                                        ones_s[:, 0:1],
                                        attu[:, a0 + 512 * j: a0 + 512 * (j + 1)],
                                        start=(kt == 0), stop=(kt == nkt - 1)))
                            f_av[kt] = IL("tensor", "t")

                        for kt in range(nkt):
                            w_sc0 = [f_exp0_last, f_bias] if kt == 0 else [f_exp0_last]
                            w_sc1 = [f_exp1_last] if kt > 0 else [f_exp1_last, f_bias]
                            if qt == 0 and kt == 0:
                                w_sc0 += [f_qk_done, f_avnorm_last]
                                w_sc1 += [f_qk_done]
                            for half in range(2):
                                js = (0, 1) if half == 0 else (2, 3)
                                w_h = w_sc0 if half == 0 else w_sc1
                                for j in js:
                                    E("tensor", lambda e, j=j, kt=kt, qt=qt: e.matmul(
                                        psA[:, 512 * j:512 * (j + 1)],
                                        kcb[32 * j:32 * (j + 1), kt * 128:(kt + 1) * 128],
                                        qcb[32 * j:32 * (j + 1), qt * 512:(qt + 1) * 512],
                                        start=True, stop=False, tile_position=(32 * j, 0)),
                                      waits=w_h if j == js[0] else [])
                                for j in js:
                                    E("tensor", lambda e, j=j, kt=kt, qt=qt: e.matmul(
                                        psA[:, 512 * j:512 * (j + 1)],
                                        keb[32 * j:32 * (j + 1), kt * 128:(kt + 1) * 128],
                                        qeb[32 * j:32 * (j + 1), qt * 512:(qt + 1) * 512],
                                        start=False, stop=True, tile_position=(32 * j, 0)))
                                if half == 0:
                                    f_sc0 = IL("tensor", "t")
                                else:
                                    f_sc1 = IL("tensor", "t")
                            a0 = (kt % 2) * 2048
                            w_e0 = [f_sc0]
                            w_e1 = [f_sc1]
                            if f_av.get(kt - 2) is not None:
                                w_e0.append(f_av[kt - 2])
                                w_e1.append(f_av[kt - 2])
                            E("scalar", lambda e, a0=a0: e.activation(
                                attu[:, a0:a0 + 1024], psA[:, 0:1024], AFT.Exp,
                                bias=biasg[:, :], scale=SCALE), waits=w_e0)
                            f_exp0_last = IL("scalar", "s")
                            E("scalar", lambda e, a0=a0: e.activation(
                                attu[:, a0 + 1024:a0 + 2048], psA[:, 1024:2048], AFT.Exp,
                                bias=biasg[:, :], scale=SCALE), waits=w_e1)
                            f_exp1_last = IL("scalar", "s")
                            f_dep0, f_dep1 = f_exp0_last, f_exp1_last
                            if kt >= 4 * qt:
                                r = kt - 4 * qt
                                for j in range(4):
                                    E("vector", lambda e, a0=a0, j=j, r=r: e.tensor_mul(
                                        attu[:, a0 + 512 * j: a0 + 512 * (j + 1)],
                                        attu[:, a0 + 512 * j: a0 + 512 * (j + 1)],
                                        dmask_s[:, r * 512:(r + 1) * 512]),
                                      waits=[f_exp0_last] if j == 0 else
                                            ([f_exp1_last] if j == 2 else []))
                                f_dep1 = IL("vector", "v")
                                f_dep0 = f_dep1
                            av_args[kt] = (a0, f_dep0, f_dep1)
                            if kt == 1 and pend["qt"] is not None:
                                f_avnorm_last = emit_norm()
                            if kt >= 1:
                                emit_av(kt - 1, nkt)
                        emit_av(nkt - 1, nkt)
                        f_av_fin = f_av[nkt - 1]

                        for j in range(4):
                            yc = 2 * g + j // 2
                            prow = 64 * (j % 2)
                            if j % 2 == 0:
                                E("vector", lambda e, j=j, yc=yc, qt=qt: e.tensor_copy(
                                    ysb[0:64, yc * 2048 + qt * 512: yc * 2048 + qt * 512 + 512],
                                    psB[0:64, 512 * j:512 * j + 512]),
                                  waits=[f_av_fin] if j == 0 else [])
                                E("vector", lambda e, j=j: _recip_bf(
                                    e, zrb[64:65, 512 * j:512 * j + 512],
                                    psB[64:65, 512 * j:512 * j + 512]))
                            else:
                                E("vector", lambda e, j=j, yc=yc, qt=qt: e.tensor_copy(
                                    ysb[64:128, yc * 2048 + qt * 512: yc * 2048 + qt * 512 + 512],
                                    psB[64:128, 512 * j:512 * j + 512]))
                                E("vector", lambda e, j=j: _recip_bf(
                                    e, zrb[0:1, 512 * j:512 * j + 512],
                                    psB[0:1, 512 * j:512 * j + 512]))
                        f_evac = IL("vector", "v")
                        pend["qt"] = qt
                        pend["f_evac"] = f_evac
                        if qt == TT - 1:
                            f_avnorm_last = emit_norm()
                    f_proj_prev = cc("t")
                    f_vsb_free = cc("t")

                MK(f"s{s_}_attn_end")
                f_groups_done = cc("t")
                f_y_all = f_avnorm_last
                f_wc = {}
                f_wcd = {}
                f_xhw = {}
                for oc in range(NCH):
                    slot = (oc % 2) * 1024
                    w_l = [f_wc[oc - 2]] if oc >= 2 else \
                        ([f_A_end.get("wcs_prev")] if f_A_end.get("wcs_prev") else [])
                    if ("cast", oc - 2) in f_wcd:
                        w_l = w_l + [f_wcd[("cast", oc - 2)]]
                    E("sync", lambda e, oc=oc, slot=slot: e.dma_start(
                        out=wcf[:, slot:slot + 1024].rearrange("p (a n) -> p a n", a=8),
                        in_=d_wc[:, oc * 128:(oc + 1) * 128]
                            .rearrange("(a p) n -> p a n", p=128)), waits=w_l)
                    f_ld = IL("sync", "dwc", 16)
                    E("scalar", lambda e, slot=slot: e.copy(
                        wcs[:, slot:slot + 1024], wcf[:, slot:slot + 1024]),
                      waits=[f_ld] + ([f_wc[oc - 2]] if oc >= 2 else []))
                    f_wcd[("cast", oc)] = IL("scalar", "s")
                    f_wcd[oc] = f_wcd[("cast", oc)]
                    for tt in range(TT):
                        pb = 512 * (tt % 4)
                        par = tt % 2
                        dsem = "dxf0" if par == 0 else "dxf1"
                        prev = f_wc.get((oc, tt - 4)) or f_wc.get((oc - 1, tt))
                        prev2 = f_wc.get((oc, tt - 2)) or f_wc.get((oc - 1, tt + 2))
                        E("sync", lambda e, oc=oc, tt=tt, par=par, t0=t0, s_=s_: e.dma_start(
                            out=xf[:, par * 512:(par + 1) * 512],
                            in_=d_x[s_, oc * 128:(oc + 1) * 128, tt * 512:(tt + 1) * 512]),
                          waits=[prev2, f_groups_done] if prev2 else [f_groups_done])
                        f_xfd = IL("sync", dsem, 16)
                        w_ = [f_wcd[oc]]
                        if tt == 0 and oc == 0:
                            w_.append(f_y_all)
                        if prev:
                            w_.append(prev)
                        for chv in range(NCH):
                            E("tensor", lambda e, oc=oc, tt=tt, chv=chv, pb=pb, slot=slot:
                              e.matmul(psA[:, pb:pb + 512],
                                       wcs[:, slot + chv * 128: slot + chv * 128 + 128],
                                       ysb[:, chv * 2048 + tt * 512: chv * 2048 + tt * 512 + 512],
                                       start=(chv == 0), stop=(chv == NCH - 1)),
                              waits=w_ if chv == 0 else [])
                        f_mm = IL("tensor", "t")
                        w_stt = [f_mm, f_xfd]
                        pw = f_xhw.get((oc, tt - 2)) or f_xhw.get((oc - 1, tt + 2))
                        if pw:
                            w_stt.append(pw)
                        E("vector", lambda e, par=par, pb=pb: e.scalar_tensor_tensor(
                            r1f[:, par * 512:(par + 1) * 512], psA[:, pb:pb + 512],
                            onesf[:, :], xf[:, par * 512:(par + 1) * 512],
                            ALU.mult, ALU.add),
                          waits=w_stt)
                        E("vector", lambda e, oc=oc, tt=tt, par=par, t0=t0: e.tensor_copy(
                            xsv(oc, t0 + tt * 512, 512), r1f[:, par * 512:(par + 1) * 512]))
                        f_r1v = IL("vector", "v")
                        E("sync", lambda e, oc=oc, tt=tt, par=par, s_=s_: e.dma_start(
                            out=d_xh[s_, oc * 128:(oc + 1) * 128, tt * 512:(tt + 1) * 512],
                            in_=r1f[:, par * 512:(par + 1) * 512]), waits=[f_r1v])
                        f_xhw[(oc, tt)] = IL("sync", "dxh", 16)
                        f_wc[(oc, tt)] = f_r1v
                    f_wc[oc] = f_wc[(oc, TT - 1)]
                f_r1w_all = cc("dxh")
                f_A_end["wcs_prev"] = cc("t")
                f_r1 = cc("v")

                MK(f"s{s_}_wc_end")
                f_sq = {}
                f_st = {}
                for chv in range(NCH):
                    slot = 0
                    w_ = [f_r1] if chv == 0 else [f_st[chv - 1]]
                    E("scalar", lambda e, chv=chv, slot=slot, t0=t0: e.activation(
                        attu[:, slot:slot + 2048], xsv(chv, t0, 2048), AFT.Square), waits=w_)
                    f_sq[chv] = IL("scalar", "s")
                    for tt in range(TT):
                        E("tensor", lambda e, chv=chv, tt=tt, t0=t0: e.matmul(
                            psB[0:1, tt * 512:(tt + 1) * 512], ones_s[:, 0:1],
                            xsv(chv, t0 + tt * 512, 512),
                            start=(chv == 0), stop=(chv == NCH - 1)),
                          waits=[f_sq[chv], f_r1, f_avnorm_last] if (chv == 0 and tt == 0)
                                else [f_sq[chv]] if tt == 0 else [])
                        E("tensor", lambda e, chv=chv, tt=tt, slot=slot: e.matmul(
                            psA[0:1, tt * 512:(tt + 1) * 512], ones_s[:, 0:1],
                            attu[:, slot + tt * 512: slot + (tt + 1) * 512],
                            start=(chv == 0), stop=(chv == NCH - 1)))
                    f_st[chv] = IL("tensor", "t")
                f_stats = f_st[NCH - 1]

                f_ap = {}
                f_xw = {}
                for tt in range(TT):
                    E("vector", lambda e, tt=tt: e.tensor_scalar(
                        strow[0:1, 0:512], psB[0:1, tt * 512:(tt + 1) * 512],
                        1.0 / C, None, ALU.mult), waits=[f_stats])
                    E("vector", lambda e, tt=tt: e.tensor_scalar(
                        strow[0:1, 512:1024], psA[0:1, tt * 512:(tt + 1) * 512],
                        1.0 / C, None, ALU.mult))
                    E("vector", lambda e: e.tensor_mul(
                        strow[0:1, 1024:1536], strow[0:1, 0:512], strow[0:1, 0:512]))
                    E("vector", lambda e: e.tensor_sub(
                        strow[0:1, 512:1024], strow[0:1, 512:1024], strow[0:1, 1024:1536]))
                    f_var = IL("vector", "v")
                    E("scalar", lambda e: e.activation(
                        strow[0:1, 512:1024], strow[0:1, 512:1024], AFT.Sqrt,
                        bias=epsb[0:1, 0:1]), waits=[f_var])
                    f_sd = IL("scalar", "s")
                    E("vector", lambda e: e.reciprocal(
                        strow[0:1, 512:1024], strow[0:1, 512:1024]), waits=[f_sd])
                    E("vector", lambda e: e.tensor_mul(
                        strow[0:1, 1024:1536], strow[0:1, 0:512], strow[0:1, 512:1024]))
                    E("vector", lambda e: e.tensor_copy(strow_b[0:1, 0:512],
                                                        strow[0:1, 512:1024]))
                    E("vector", lambda e: e.tensor_copy(strow_b[0:1, 512:1024],
                                                        strow[0:1, 1024:1536]))
                    E("vector", lambda e: e.tensor_copy(strow[0:1, 0:512],
                                                        strow_b[0:1, 0:512]))
                    E("vector", lambda e: e.tensor_sub(strow[0:1, 0:512],
                                                       strow[0:1, 512:1024],
                                                       strow[0:1, 0:512]))
                    E("vector", lambda e: e.tensor_copy(strow_b[0:1, 1024:1536],
                                                        strow[0:1, 0:512]))
                    f_rows = IL("vector", "v")
                    E("tensor", lambda e, tt=tt: e.matmul(
                        psB[:, tt * 512:(tt + 1) * 512], ones_s[0:1, :],
                        strow_b[0:1, 0:512], start=True, stop=False), waits=[f_rows])
                    E("tensor", lambda e, tt=tt: e.matmul(
                        psB[:, tt * 512:(tt + 1) * 512], ones_s[0:1, :],
                        strow_b[0:1, 1024:1536], start=False, stop=True))
                    E("tensor", lambda e, tt=tt: e.matmul(
                        psA[:, tt * 512:(tt + 1) * 512], ones_s[0:1, :],
                        strow_b[0:1, 512:1024], start=True, stop=True))
                    f_bc = IL("tensor", "t")
                    for oc in range(NCH):
                        par = oc % 2
                        dsem = "dxf0" if par == 0 else "dxf1"
                        w_d = [f_r1w_all]
                        if (tt, oc - 2) in f_ap:
                            w_d.append(f_ap[(tt, oc - 2)])
                        elif tt > 0 and (tt - 1, NCH - 2 + par) in f_ap:
                            w_d.append(f_ap[(tt - 1, NCH - 2 + par)])
                        E("sync", lambda e, oc=oc, tt=tt, par=par, s_=s_: e.dma_start(
                            out=xf[:, par * 512:(par + 1) * 512],
                            in_=d_xh[s_, oc * 128:(oc + 1) * 128, tt * 512:(tt + 1) * 512]),
                          waits=w_d)
                        f_xfd = IL("sync", dsem, 16)
                        w_t = [f_xfd, f_bc]
                        pw = f_xw.get((tt, oc - 2)) or (f_xw.get((tt - 1, NCH - 2 + par))
                                                        if tt > 0 else None)
                        if pw:
                            w_t.append(pw)
                        E("vector", lambda e, par=par, tt=tt: e.tensor_mul(
                            r1f[:, par * 512:(par + 1) * 512],
                            xf[:, par * 512:(par + 1) * 512],
                            psB[:, tt * 512:(tt + 1) * 512]), waits=w_t)
                        E("vector", lambda e, par=par, tt=tt: e.tensor_sub(
                            r1f[:, par * 512:(par + 1) * 512],
                            r1f[:, par * 512:(par + 1) * 512],
                            psA[:, tt * 512:(tt + 1) * 512]))
                        E("vector", lambda e, par=par, col=tt * NCH + oc: e.tensor_reduce(
                            zr[:, col:col + 1], r1f[:, par * 512:(par + 1) * 512],
                            AX.X, ALU.max, apply_absolute_value=True))
                        f_ap[(tt, oc)] = IL("vector", "v")
                        E("sync", lambda e, oc=oc, tt=tt, par=par, s_=s_: e.dma_start(
                            out=d_xh[s_, oc * 128:(oc + 1) * 128, tt * 512:(tt + 1) * 512],
                            in_=r1f[:, par * 512:(par + 1) * 512]),
                          waits=[f_ap[(tt, oc)]])
                        f_xw[(tt, oc)] = IL("sync", "dxw", 16)
                EV(lambda e: e.tensor_reduce(tiny[:, 4:5], zr[:, 0:32],
                                             AX.X, ALU.max))
                if s_ == 0:
                    EV(lambda e: e.tensor_copy(tiny[:, 5:6], tiny[:, 4:5]))
                else:
                    EV(lambda e: e.tensor_max(tiny[:, 5:6], tiny[:, 5:6],
                                              tiny[:, 4:5]))
                f_ln1_v = cc("v")

            f_xh_all = [cc("dxh"), cc("dxw")]
            if STOP_AFTER == "ln1":
                E("gpsimd", lambda e: e.dma_start(out=d_out[0, :, :], in_=d_xh[0, :, :]),
                  waits=[cc("v"), cc("t"), cc("s")] + f_xh_all)
                IL("gpsimd", "dou0", 16)
                for di_, ap_ in enumerate((ones_s[0:1, 0:128], epsb[:, 0:1],
                                           tiny[:, 4:5], tiny[:, 5:6],
                                           rowA[0:1, 0:256], cbias[:, 0:1],
                                           gmx1[0:1, 0:128], biasg[:, 0:1])):
                    E("gpsimd", lambda e, di_=di_, ap_=ap_: e.dma_start(
                        out=d_out[1, di_ * 128:di_ * 128 + ap_.shape[0], 0:ap_.shape[1]],
                        in_=ap_, allow_slow_non_contiguous=True))
                    IL("gpsimd", "dou0", 16)
                E("sync", lambda e: e.nop(), waits=[cc("dou0")])

            if STOP_AFTER in (None, "h", "c1"):
                MK("ln1_end")
                f_xmax = pmax128(5, 5, 2, waits_v=[f_ln1_v])
                EV(lambda e: e.memset(gmx1[:, :], 0.0), waits=[f_xmax])
                f_gm = EV(lambda e: e.tensor_copy(gmx1[0:1, 0:1], rowA[0:1, 5:6]))
                E("gpsimd", lambda e: e.dma_start(out=d_ar1i[:, :], in_=gmx1[:, :]),
                  waits=[f_gm])
                IL("gpsimd", "dar", 16)
                f_ar_in = cc("dar")
                E("gpsimd", lambda e: e.collective_compute(
                    "AllReduce", ALU.max, replica_groups=[list(range(NCORE))],
                    ins=[d_ar1i[:, :]], outs=[d_ar1o[:, :]]), waits=[f_ar_in])
                IL("gpsimd", "cc")
                f_cc1 = cc("cc")
                E("gpsimd", lambda e: e.dma_start(out=gmx1[:, :], in_=d_ar1o[:, :]),
                  waits=[f_cc1])
                IL("gpsimd", "dar", 16)
                f_gmax1 = cc("dar")

                E("scalar", lambda e: e.activation(rowA[0:1, 6:7], rowA[0:1, 0:1], AFT.Sigmoid),
                  waits=[cc("dcl"), f_gmax1])
                f_sig = IL("scalar", "s")
                EV(lambda e: e.tensor_mul(rowA[0:1, 7:8], rowA[0:1, 6:7],
                                          gmx1[0:1, 0:1]), waits=[f_sig, f_gmax1])
                EV(lambda e: e.tensor_scalar(rowA[0:1, 8:9], rowA[0:1, 7:8],
                                             1.0 / 127.0, None, ALU.mult))
                EV(lambda e: e.reciprocal(rowA[0:1, 9:10], rowA[0:1, 8:9]))
                bcast128(9, rsx_b)
                bcast128(8, sxs_b)
                f_sx = cc("v")

                E("tensor", lambda e: e.nop(), waits=[])
                f_A_t = IL("tensor", "t")
                E("scalar", lambda e: e.copy(tiny[0:1, 13:14], tiny[0:1, 13:14]), waits=[f_sx])
                f_A_s = IL("scalar", "s")
                E("vector", lambda e: e.tensor_copy(tiny[0:1, 12:13], tiny[0:1, 12:13]))
                f_A_v = IL("vector", "v")
                E("gpsimd", lambda e: e.memset(tiny[0:1, 11:12], 0.0))
                f_A_g = IL("gpsimd", "g")
        if STOP_AFTER in (None, "h", "c1"):
            barrier_A = [f_A_t, f_A_s, f_A_v, f_A_g, f_sx]

            with ExitStack() as bctx:
                sbq = lambda name, shape, dt=BF16: bctx.enter_context(nc.sbuf_tensor(name, shape, dt))
                xq = sbq("xq", [128, NCH * 4096])
                hst = sbq("hst", [128, 2 * 4096], F32)
                rndq = bctx.enter_context(nc.sbuf_tensor("rndq", [128, 2 * 2048], I32))
                rndw = bctx.enter_context(nc.sbuf_tensor("rndw", [128, 1024], I32))
                xhq = sbq("xhq", [128, 3 * 2048], F32)
                wgo = sbq("wgo", [128, 1024], F32)
                wuo = sbq("wuo", [128, 1024], F32)
                wgT = sbq("wgT", [128, 2 * 1024])
                wuT = sbq("wuT", [128, 2 * 1024])
                wqi = sbq("wqi", [128, 4 * 1024])
                gbf = sbq("gbf", [128, 3 * 512])

                def xqv(chv, lo, n):
                    return xq[:, chv * 4096 + lo: chv * 4096 + lo + n]

                f_xq = {}
                f_xq_seq = {}
                nq = 0
                for s2 in range(SEQ):
                    for chv in range(NCH):
                        par = nq % 3
                        dsem = ("dxf0", "dxf1", "dxw")[par]
                        w_d = list(barrier_A) + f_xh_all
                        if (nq - 3) in f_xq:
                            w_d.append(f_xq[nq - 3])
                        E("sync", lambda e, chv=chv, s2=s2, par=par: e.dma_start(
                            out=xhq[:, par * 2048:(par + 1) * 2048],
                            in_=d_xh[s2, chv * 128:(chv + 1) * 128, :]), waits=w_d)
                        f_xd = IL("sync", dsem, 16)
                        qpar = nq % 2
                        w_s = [f_xd]
                        if (nq - 2) in f_xq:
                            w_s.append(f_xq[nq - 2])
                        else:
                            w_s += list(barrier_A)
                        E("scalar", lambda e, par=par, qpar=qpar: e.activation(
                            rndq.ap().bitcast(F32)[:, qpar * 2048:(qpar + 1) * 2048],
                            xhq[:, par * 2048:(par + 1) * 2048],
                            AFT.Identity, bias=cbias[:, :], scale=rsx_b[:, :]),
                          waits=w_s)
                        f_rd = IL("scalar", "s")
                        E("vector", lambda e, qpar=qpar: e.tensor_scalar(
                            rndq.ap().bitcast(F32)[:, qpar * 2048:(qpar + 1) * 2048],
                            rndq.ap().bitcast(F32)[:, qpar * 2048:(qpar + 1) * 2048],
                            C_RND + 127.0, C_RND - 128.0, ALU.min, ALU.max), waits=[f_rd])
                        E("vector", lambda e, chv=chv, s2=s2, qpar=qpar: e.tensor_scalar(
                            xqv(chv, s2 * T, T),
                            rndq.ap().bitcast(F32)[:, qpar * 2048:(qpar + 1) * 2048],
                            C_RND, None, ALU.subtract))
                        f_xq[nq] = IL("vector", "v")
                        nq += 1
                    f_xq_seq[s2] = cc("v")
                f_xq_done = cc("v")

                MK("B_xq_end")
                f_wq = {}
                f_int = {}
                f_rnd = {}
                f_trf = {}
                f_sp = {}
                f_hred_last = None

                def emit_chain(ic):
                    for wi, (dsrc, wo, svec) in enumerate(
                            ((d_wg, wgo, sgv), (d_wu, wuo, suv))):
                        w_d = list(barrier_A) if ic == 0 and wi == 0 else []
                        if (ic - 1, wi) in f_rnd:
                            w_d.append(f_rnd[(ic - 1, wi)])
                        E("scalar", lambda e, dsrc=dsrc, wo=wo, ic=ic: e.dma_start(
                            out=wo[:, :], in_=dsrc[ic * 128:(ic + 1) * 128, :]), waits=w_d)
                        f_wd = IL("scalar", "dwg" if wi == 0 else "dwu", 16)
                        w_r = [f_wd]
                        if wi == 1:
                            w_r.append(f_rnd[(ic, 0)])
                        EV(lambda e, wo=wo: e.tensor_reduce(
                            tiny[:, 6:7], wo[:, :], AX.X, ALU.max,
                            apply_absolute_value=True), waits=w_r)
                        EV(lambda e: e.reciprocal(tiny[:, 7:8], tiny[:, 6:7]))
                        EV(lambda e: e.tensor_scalar(tiny[:, 7:8], tiny[:, 7:8],
                                                     127.0, None, ALU.mult))
                        EV(lambda e: e.tensor_scalar(
                            tiny[:, 8:9], tiny[:, 6:7], 1.0 / 127.0, None, ALU.mult))
                        f_rsw = EV(lambda e, svec=svec, ic=ic: e.tensor_mul(
                            svec[:, ic:ic + 1], tiny[:, 8:9], sxs_b[:, :]))
                        E("scalar", lambda e, wo=wo: e.activation(
                            rndw.ap().bitcast(F32), wo[:, :], AFT.Identity,
                            bias=cbias[:, :], scale=tiny[:, 7:8]),
                          waits=[f_rsw, cc("v")])
                        f_rnd[(ic, wi)] = IL("scalar", "s")
                        E("vector", lambda e: e.tensor_scalar(
                            rndw.ap().bitcast(F32), rndw.ap().bitcast(F32),
                            C_RND + 127.0, C_RND - 128.0, ALU.min, ALU.max),
                          waits=[f_rnd[(ic, wi)]])
                        qslot = ((ic % 2) * 2 + wi) * 1024
                        E("vector", lambda e, qslot=qslot: e.tensor_scalar(
                            wqi[:, qslot:qslot + 1024], rndw.ap().bitcast(F32),
                            C_RND, None, ALU.subtract),
                          waits=[f_trf.get((ic - 2, wi))])
                        f_int[(ic, wi)] = IL("vector", "v")

                def emit_tr(ic, wi):
                    if True:
                        wT = wgT if wi == 0 else wuT
                        pb = psAb if wi == 0 else psBb
                        qslot = ((ic % 2) * 2 + wi) * 1024
                        for b in range(8):
                            E("tensor", lambda e, b=b, pb=pb, qslot=qslot: e.transpose(
                                pb[:, 3072 + 128 * b: 3072 + 128 * (b + 1)],
                                wqi[:, qslot + 128 * b: qslot + 128 * (b + 1)],
                                ident_s[:, :]),
                              waits=[f_int[(ic, wi)], f_wq.get((ic - 1, wi))]
                                    if b == 0 else [])
                        f_trf[(ic, wi)] = IL("tensor", "t")
                        slot = (ic % 2) * 1024
                        E("scalar", lambda e, wT=wT, slot=slot, pb=pb: e.copy(
                            wT[:, slot:slot + 1024], pb[:, 3072:4096]),
                          waits=[f_trf[(ic, wi)]])
                        f_wq[(ic, wi)] = IL("scalar", "s")

                def emit_mms(ic):
                    slot = (ic % 2) * 1024
                    hslot = (ic % 2) * 4096
                    for tt8 in range(SEQ * TT):
                        pg = 512 * (tt8 % 3)
                        w_ = []
                        if tt8 == 0:
                            w_ += [f_wq[(ic, 0)], f_wq[(ic, 1)], f_xq_seq[0]]
                            if (ic - 2) in f_sp:
                                w_.append(f_sp[ic - 2])
                        if tt8 == 4:
                            w_.append(f_xq_seq[1])
                        prev = f_wq.get(("evac", ic, tt8 - 3)) or f_wq.get(("evac", ic - 1, tt8 + 5))
                        if prev:
                            w_.append(prev)
                        for b in range(8):
                            E("tensor", lambda e, b=b, tt8=tt8, slot=slot, pg=pg: e.matmul(
                                psA[:, pg:pg + 512],
                                wgT[:, slot + 128 * b: slot + 128 * (b + 1)],
                                xqv(b, tt8 * 512, 512),
                                start=(b == 0), stop=(b == 7)), waits=w_ if b == 0 else [])
                        f_g = IL("tensor", "t")
                        for b in range(8):
                            E("tensor", lambda e, b=b, tt8=tt8, slot=slot, pg=pg: e.matmul(
                                psB[:, pg:pg + 512],
                                wuT[:, slot + 128 * b: slot + 128 * (b + 1)],
                                xqv(b, tt8 * 512, 512),
                                start=(b == 0), stop=(b == 7)))
                        f_u = IL("tensor", "t")
                        gslot = 512 * (tt8 % 3)
                        w_silu = [f_g]
                        if prev:
                            w_silu.append(prev)
                        E("scalar", lambda e, pg=pg, gslot=gslot, ic=ic: e.activation(
                            gbf[:, gslot:gslot + 512], psA[:, pg:pg + 512], AFT.Silu,
                            scale=sgv[:, ic:ic + 1]), waits=w_silu)
                        f_silu = IL("scalar", "s")
                        E("vector", lambda e, pg=pg, gslot=gslot, ic=ic, hslot=hslot, tt8=tt8:
                          e.scalar_tensor_tensor(
                              hst[:, hslot + tt8 * 512: hslot + (tt8 + 1) * 512],
                              psB[:, pg:pg + 512], suv[:, ic:ic + 1],
                              gbf[:, gslot:gslot + 512], ALU.mult, ALU.mult),
                          waits=[f_u, f_silu])
                        E("vector", lambda e, hslot=hslot, tt8=tt8, ic=ic: e.tensor_reduce(
                            hred[:, ic * 8 + tt8: ic * 8 + tt8 + 1],
                            hst[:, hslot + tt8 * 512: hslot + (tt8 + 1) * 512],
                            AX.X, ALU.max, apply_absolute_value=True))
                        f_wq[("evac", ic, tt8)] = IL("vector", "v")
                        if ic + 1 < NIC and tt8 in (3, 6):
                            emit_tr(ic + 1, 0 if tt8 == 3 else 1)
                    f_hr = f_wq[("evac", ic, SEQ * TT - 1)]
                    E("sync", lambda e, ic=ic, hslot=hslot: e.dma_start(
                        out=d_spill[:, :, ic * 512:(ic + 1) * 512].transpose([1, 0, 2]),
                        in_=hst[:, hslot:hslot + 4096].rearrange("p (a n) -> p a n", a=8)),
                      waits=[f_hr])
                    f_sp[ic] = IL("sync", "dsp0" if ic % 2 == 0 else "dsp1", 16)
                    return f_hr

                emit_chain(0)
                emit_tr(0, 0)
                emit_tr(0, 1)
                emit_chain(1)
                for ic in range(NIC):
                    MK(f"B_ic{ic}")
                    if ic + 2 < NIC:
                        emit_chain(ic + 2)
                    f_hred_last = emit_mms(ic)
                f_spill_all = [cc("dsp0"), cc("dsp1")]

                if STOP_AFTER == "h":
                    for a_ in range(6):
                        n_ = 2048 if a_ < 5 else NIC * 512 - 5 * 2048
                        E("gpsimd", lambda e, a_=a_, n_=n_: e.dma_start(
                            out=d_out[0, a_ * 128:(a_ + 1) * 128, 0:n_],
                            in_=d_spill[0, :, a_ * 2048: a_ * 2048 + n_]),
                          waits=([cc("v"), cc("t"), cc("s")] + f_spill_all) if a_ == 0 else [])
                        IL("gpsimd", "dou0", 16)
                    for di_, ap_ in enumerate((gmx1[0:1, 0:128], rowA[0:1, 0:256],
                                               rsx_b[:, 0:1], sxs_b[:, 0:1],
                                               sgv[:, 0:1], suv[:, 0:1],
                                               tiny[:, 5:6], hred[:, 0:1])):
                        E("gpsimd", lambda e, di_=di_, ap_=ap_: e.dma_start(
                            out=d_out[1, di_ * 128:di_ * 128 + ap_.shape[0],
                                      0:ap_.shape[1]], in_=ap_,
                            allow_slow_non_contiguous=True))
                        IL("gpsimd", "dou0", 16)
                    E("sync", lambda e: e.nop(), waits=[cc("dou0")])

                MK("B_mm_end")
                EV(lambda e: e.tensor_reduce(tiny[:, 9:10], hred[:, :], AX.X,
                                             ALU.max), waits=[f_hred_last])
                f_hm = pmax128(9, 12, 3)
                EV(lambda e: e.memset(gmx2[:, :], 0.0), waits=[f_hm])
                f_gm2 = EV(lambda e: e.tensor_copy(gmx2[0:1, 0:1], rowA[0:1, 12:13]))
                E("gpsimd", lambda e: e.dma_start(out=d_ar2i[:, :], in_=gmx2[:, :]),
                  waits=[f_gm2])
                IL("gpsimd", "dar", 16)
                f_ar2_in = cc("dar")
                E("gpsimd", lambda e: e.collective_compute(
                    "AllReduce", ALU.max, replica_groups=[list(range(NCORE))],
                    ins=[d_ar2i[:, :]], outs=[d_ar2o[:, :]]), waits=[f_ar2_in])
                IL("gpsimd", "cc")
                f_cc2 = cc("cc")
                E("gpsimd", lambda e: e.dma_start(out=gmx2[:, :], in_=d_ar2o[:, :]),
                  waits=[f_cc2])
                IL("gpsimd", "dar", 16)
                f_gmax2 = cc("dar")

                E("scalar", lambda e: e.activation(rowA[0:1, 13:14], rowA[0:1, 1:2], AFT.Sigmoid),
                  waits=[f_gmax2])
                f_sig2 = IL("scalar", "s")
                EV(lambda e: e.tensor_mul(rowA[0:1, 14:15], rowA[0:1, 13:14],
                                          gmx2[0:1, 0:1]), waits=[f_sig2, f_gmax2])
                EV(lambda e: e.tensor_scalar(rowA[0:1, 15:16], rowA[0:1, 14:15],
                                             1.0 / 127.0, None, ALU.mult))
                EV(lambda e: e.reciprocal(rowA[0:1, 16:17], rowA[0:1, 15:16]))
                bcast128(16, rsh_b)
                bcast128(15, sh_b)
                f_sh = cc("v")

                E("tensor", lambda e: e.nop(), waits=[])
                f_B_t = IL("tensor", "t")
                E("scalar", lambda e: e.copy(tiny[0:1, 13:14], tiny[0:1, 13:14]), waits=[f_sh])
                f_B_s = IL("scalar", "s")
                E("vector", lambda e: e.tensor_copy(tiny[0:1, 12:13], tiny[0:1, 12:13]))
                f_B_v = IL("vector", "v")
                E("gpsimd", lambda e: e.memset(tiny[0:1, 11:12], 0.0))
                f_B_g = IL("gpsimd", "g")
            barrier_B = [f_B_t, f_B_s, f_B_v, f_B_g, f_sh] + f_spill_all
            SKIP_C = STOP_AFTER == "h"

            if not SKIP_C:
                with ExitStack() as cctx:
                    sbc = lambda name, shape, dt=BF16: cctx.enter_context(nc.sbuf_tensor(name, shape, dt))
                    wdT = sbc("wdT", [128, NIC * 1024])
                    wdo = sbc("wdo", [128, 2 * I_FF], F32)
                    rndc = cctx.enter_context(nc.sbuf_tensor("rndc", [128, I_FF], I32))
                    wdi = sbc("wdi", [128, I_FF])
                    rb = sbc("rb", [128, 2 * 2048], F32)
                    xr = sbc("xr", [128, 2 * 8 * 512], F32)
                    r2f = sbc("r2f", [128, 8 * 512], F32)
                    r2b = sbc("r2b", [128, 2 * 512])
                    hq = sbc("hq", [128, 2 * 512])
                    outs = sbc("outs", [128, 2 * 512], F32)
                    sqc = sbc("sqc", [128, 2 * 512])
                    strowc = sbc("strowc", [1, 2048], F32)
                    strowc_b = sbc("strowc_b", [1, 1024])

                    MK("C_start")
                    rndc2 = rb.ap().bitcast(I32)[:, 0:I_FF]
                    wdi2 = r2f.ap().bitcast(BF16)[:, 0:I_FF]
                    f_dq = {}
                    f_intq = {}

                    def emit_wdchain(hc):
                        slot = (hc % 2) * I_FF
                        rnd_t = rndc.ap().bitcast(F32) if hc % 2 == 0 else rndc2.bitcast(F32)
                        wdi_t = wdi if hc % 2 == 0 else wdi2
                        w_ = list(barrier_B) if hc == 0 else []
                        if ("rnd", hc - 2) in f_dq:
                            w_.append(f_dq[("rnd", hc - 2)])
                        E("sync", lambda e, hc=hc, slot=slot: e.dma_start(
                            out=wdo[:, slot:slot + I_FF], in_=d_wd[hc * 128:(hc + 1) * 128, :]),
                          waits=w_)
                        f_wd = IL("sync", "dwd0" if hc % 2 == 0 else "dwd1", 16)
                        EV(lambda e, slot=slot: e.tensor_reduce(
                            tiny[:, 6:7], wdo[:, slot:slot + I_FF], AX.X, ALU.max,
                            apply_absolute_value=True),
                           waits=[f_wd, f_dq.get(("rnd", hc - 1))] +
                                 (list(barrier_B) if hc == 0 else []))
                        EV(lambda e: e.reciprocal(tiny[:, 7:8], tiny[:, 6:7]))
                        EV(lambda e: e.tensor_scalar(tiny[:, 7:8], tiny[:, 7:8],
                                                     127.0, None, ALU.mult))
                        EV(lambda e: e.tensor_scalar(tiny[:, 8:9], tiny[:, 6:7],
                                                     1.0 / 127.0, None, ALU.mult))
                        f_rsw = EV(lambda e, hc=hc: e.tensor_mul(sdv[:, hc:hc + 1], tiny[:, 8:9],
                                                                 sh_b[:, :]))
                        E("scalar", lambda e, slot=slot, rnd_t=rnd_t: e.activation(
                            rnd_t, wdo[:, slot:slot + I_FF], AFT.Identity,
                            bias=cbias[:, :], scale=tiny[:, 7:8]),
                          waits=[f_rsw] + (list(barrier_B) if hc == 0 else []))
                        f_rd = IL("scalar", "s")
                        f_dq[("rnd", hc)] = f_rd
                        E("vector", lambda e, rnd_t=rnd_t: e.tensor_scalar(
                            rnd_t, rnd_t,
                            C_RND + 127.0, C_RND - 128.0, ALU.min, ALU.max), waits=[f_rd])
                        E("vector", lambda e, rnd_t=rnd_t, wdi_t=wdi_t: e.tensor_scalar(
                            wdi_t[:, 0:I_FF], rnd_t, C_RND, None, ALU.subtract))
                        f_intq[hc] = IL("vector", "v")

                    def emit_wdtr(hc):
                        wdi_t = wdi if hc % 2 == 0 else wdi2
                        for b8 in range(3):
                            nb = 8 if b8 < 2 else 6
                            for u in range(nb):
                                icx = b8 * 8 + u
                                E("tensor", lambda e, icx=icx, u=u, wdi_t=wdi_t: e.transpose(
                                    psBb[:, 3072 + 128 * u: 3072 + 128 * (u + 1)],
                                    wdi_t[:, 128 * icx:128 * (icx + 1)], ident_s[:, :]),
                                  waits=([f_intq[hc]] + (list(barrier_B) if hc == 0 else []) +
                                         ([f_dq[hc - 1]] if (hc - 1) in f_dq else []))
                                        if (b8 == 0 and u == 0) else
                                        ([f_dq[("ev", hc, b8 - 1)]] if u == 0 else []))
                            f_tr = IL("tensor", "t")
                            E("vector", lambda e, hc=hc, b8=b8, nb=nb: e.tensor_copy(
                                wdT[:, :].rearrange("p (i n) -> p i n", i=NIC)
                                   [:, b8 * 8: b8 * 8 + nb, 128 * hc:128 * (hc + 1)],
                                psBb[:, 3072:3072 + nb * 128].rearrange("p (i n) -> p i n", i=nb)),
                              waits=[f_tr])
                            f_dq[("ev", hc, b8)] = IL("vector", "v")
                        f_dq[hc] = f_dq[("ev", hc, 2)]

                    emit_wdchain(0)
                    for hc in range(NCH):
                        if hc + 1 < NCH:
                            emit_wdchain(hc + 1)
                        emit_wdtr(hc)
                    f_wdT_done = f_dq[NCH - 1]

                    if HAVE_DBG:
                        EV(lambda e: e.tensor_reduce(tiny[:, 12:13],
                                                     wdT[:, 0:2048].bitcast(F32),
                                                     AX.X, ALU.max,
                                                     apply_absolute_value=True),
                           waits=[f_wdT_done])
                        for di_, ap_ in enumerate((rowA[0:1, 0:256], gmx2[0:1, 0:128],
                                                   sdv[0:1, 0:8], rsh_b[0:1, 0:1],
                                                   sh_b[0:1, 0:1], tiny[0:1, 12:13])):
                            E("gpsimd", lambda e, di_=di_, ap_=ap_: e.dma_start(
                                out=d_dbg[di_:di_ + 1, 0:ap_.shape[1]],
                                in_=ap_, allow_slow_non_contiguous=True),
                              waits=[cc("v"), cc("s"), cc("t")])
                            IL("gpsimd", "dou0", 16)

                    f_ev = {}
                    f_rbd = {}
                    f_ln2 = {}
                    dbg_fc = {}
                    f_r2s = {}
                    f_vars = {}
                    _ord = list(DOWN_ORDER or range(SEQ * TT))
                    for tt8i, tt8 in enumerate(_ord):
                        MK(f"C_tt8_{tt8i}")
                        p_tt8 = _ord[tt8i - 1] if tt8i > 0 else None
                        s_ = tt8 // TT
                        tt = tt8 % TT
                        t0 = s_ * T

                        def emit_rb(rs_, tt8=tt8):
                            n = 2048 if rs_ < 5 else 1024
                            slot = (rs_ % 2) * 2048
                            w_ = list(f_spill_all)
                            if p_tt8 is None:
                                w_.append(f_wdT_done)
                            if rs_ in (2, 3) and (tt8, "rb") in dbg_fc:
                                w_.append(dbg_fc[(tt8, "rb")])
                            if rs_ >= 2:
                                w_.append(f_ev[(tt8, rs_ * 4 - 5)])
                            if p_tt8 is not None:
                                w_.append(f_ev.get((p_tt8, 19 if rs_ == 0 else NIC - 1)))
                            E("sync", lambda e, tt8=tt8, rs_=rs_, n=n, slot=slot: e.dma_start(
                                out=rb[:, slot:slot + n],
                                in_=d_spill[tt8, :, rs_ * 2048: rs_ * 2048 + n]), waits=w_)
                            f_rbd[(tt8, rs_)] = IL("sync", "drb0" if rs_ % 2 == 0 else "drb1", 16)

                        xpar = tt8i % 2
                        dsemx = "dxf0" if xpar == 0 else "dxf1"
                        w_xr = list(f_xh_all)
                        if tt8i >= 2:
                            w_xr.append(f_r2s[tt8i - 2])
                            if tt8i == 2 and (_ord[0], "xr") in dbg_fc:
                                w_xr.append(dbg_fc[(_ord[0], "xr")])
                        E("sync", lambda e, s_=s_, tt=tt, xpar=xpar: e.dma_start(
                            out=xr[:, xpar * 4096:(xpar + 1) * 4096]
                                .rearrange("p (a n) -> p a n", a=8),
                            in_=d_xh[s_, :, tt * 512:(tt + 1) * 512]
                                .rearrange("(a p) n -> p a n", p=128)), waits=w_xr)
                        f_xrd = IL("sync", dsemx, 16)
                        emit_rb(0)
                        emit_rb(1)
                        if HAVE_DBG and tt8i == 0:
                            E("gpsimd", lambda e: e.dma_start(
                                out=d_dbg2[:, 0:2048], in_=rb[:, 0:2048]),
                              waits=[f_rbd[(tt8, 0)]])
                            dbg_fc[(tt8, "rb")] = IL("gpsimd", "dgw", 16)
                            E("gpsimd", lambda e: e.dma_start(
                                out=d_dbg2[:, 3072:7168], in_=xr[:, 0:4096]),
                              waits=[f_xrd])
                            dbg_fc[(tt8, "xr")] = IL("gpsimd", "dgw", 16)
                        for ic in range(NIC):
                            if ic >= 4 and (ic - 4) % 4 == 0 and (ic - 4) // 4 + 2 < 6 \
                                    and (tt8, (ic - 4) // 4 + 2) not in f_rbd:
                                emit_rb((ic - 4) // 4 + 2)
                            rs_ = (ic * 512) // 2048
                            off = (ic * 512) % 2048 + ((rs_ % 2) * 2048)
                            hslot = (ic % 2) * 512
                            qs = hslot
                            E("scalar", lambda e, off=off, qs=qs: e.activation(
                                rndc.ap().bitcast(F32)[:, qs:qs + 512], rb[:, off:off + 512],
                                AFT.Identity, bias=cbias[:, :], scale=rsh_b[:, :]),
                              waits=[f_rbd[(tt8, rs_)], f_wdT_done] +
                                    ([f_ev[(tt8, ic - 2)]] if (tt8, ic - 2) in f_ev else []) +
                                    ([f_ev.get((p_tt8, NIC - 1))] if (ic < 2 and p_tt8 is not None)
                                     else []) +
                                    ([dbg_fc[(tt8, "hq")]] if (ic == 2 and (tt8, "hq") in dbg_fc)
                                     else []))
                            f_rd = IL("scalar", "s")
                            E("vector", lambda e, qs=qs: e.tensor_scalar(
                                rndc.ap().bitcast(F32)[:, qs:qs + 512],
                                rndc.ap().bitcast(F32)[:, qs:qs + 512],
                                C_RND + 127.0, C_RND - 128.0, ALU.min, ALU.max), waits=[f_rd])
                            E("vector", lambda e, hslot=hslot, qs=qs: e.tensor_scalar(
                                hq[:, hslot:hslot + 512], rndc.ap().bitcast(F32)[:, qs:qs + 512],
                                C_RND, None, ALU.subtract))
                            f_hq = IL("vector", "v")
                            if HAVE_DBG and tt8i == 0 and ic == 0:
                                E("gpsimd", lambda e: e.dma_start(
                                    out=d_dbg2[:, 2048:2560], in_=hq[:, 0:512]),
                                  waits=[f_hq])
                                dbg_fc[(tt8, "hq")] = IL("gpsimd", "dgw", 16)
                                E("gpsimd", lambda e: e.dma_start(
                                    out=d_dbg2[:, 2560:2568], in_=sdv[:, 0:8]),
                                  waits=[f_hq])
                                IL("gpsimd", "dgw", 16)
                            hc_order = (4, 5, 6, 7, 0, 1, 2, 3) if ic == 0 else range(NCH)
                            for hi, hc in enumerate(hc_order):
                                ps = psA if hc < 4 else psB
                                pb = 512 * (hc % 4)
                                w_ = [f_hq] if hi == 0 else []
                                if ic == 0 and p_tt8 is not None:
                                    if hc == 4:
                                        w_.append(f_r2s[tt8i - 1])
                                    elif hc == 0:
                                        w_.append(f_vars[tt8i - 1])
                                    elif hc == 2:
                                        w_.append(f_ln2[("apl", p_tt8)])
                                E("tensor", lambda e, ic=ic, hc=hc, ps=ps, pb=pb, hslot=hslot:
                                  e.matmul(ps[:, pb:pb + 512],
                                           wdT[:, ic * 1024 + hc * 128: ic * 1024 + hc * 128 + 128],
                                           hq[:, hslot:hslot + 512],
                                           start=(ic == 0), stop=(ic == NIC - 1)),
                                  waits=w_)
                            f_ev[(tt8, ic)] = IL("tensor", "t")
                        f_dn = f_ev[(tt8, NIC - 1)]
                        for hc in range(NCH):
                            ps = psA if hc < 4 else psB
                            pb = 512 * (hc % 4)
                            w_e = [f_dn, f_xrd] if hc == 0 else []
                            if hc == 0 and p_tt8 is not None:
                                w_e.append(f_ln2[("apl", p_tt8)])
                                if (p_tt8, "r2f") in dbg_fc:
                                    w_e.append(dbg_fc[(p_tt8, "r2f")])
                            E("vector", lambda e, hc=hc, ps=ps, pb=pb, xpar=xpar:
                              e.scalar_tensor_tensor(
                                  r2f[:, hc * 512:(hc + 1) * 512], ps[:, pb:pb + 512],
                                  sdv[:, hc:hc + 1],
                                  xr[:, xpar * 4096 + hc * 512: xpar * 4096 + (hc + 1) * 512],
                                  ALU.mult, ALU.add), waits=w_e)
                        f_r2 = IL("vector", "v")
                        f_r2s[tt8i] = f_r2
                        if HAVE_DBG and tt8i == 0:
                            E("gpsimd", lambda e: e.dma_start(
                                out=d_dbg2[:, 7168:11264], in_=r2f[:, 0:4096]),
                              waits=[f_r2])
                            dbg_fc[(tt8, "r2f")] = IL("gpsimd", "dgw", 16)

                        f_sq2 = {}
                        for chv in range(NCH):
                            slot = (chv % 2) * 512
                            E("vector", lambda e, chv=chv, slot=slot: e.tensor_copy(
                                r2b[:, slot:slot + 512], r2f[:, chv * 512:(chv + 1) * 512]),
                              waits=([f_sq2[chv - 2]] if chv >= 2 else []))
                            f_cp = IL("vector", "v")
                            E("scalar", lambda e, slot=slot: e.activation(
                                sqc[:, slot:slot + 512], r2b[:, slot:slot + 512], AFT.Square),
                              waits=[f_cp])
                            f_sqs = IL("scalar", "s")
                            E("tensor", lambda e, chv=chv, slot=slot: e.matmul(
                                psA[0:1, 0:512], ones_s[:, 0:1], r2b[:, slot:slot + 512],
                                start=(chv == 0), stop=(chv == NCH - 1)),
                              waits=[f_cp, f_sqs])
                            E("tensor", lambda e, chv=chv, slot=slot: e.matmul(
                                psA[0:1, 512:1024], ones_s[:, 0:1], sqc[:, slot:slot + 512],
                                start=(chv == 0), stop=(chv == NCH - 1)))
                            f_sq2[chv] = IL("tensor", "t")
                        f_st2 = f_sq2[NCH - 1]
                        E("vector", lambda e: e.tensor_scalar(
                            strowc[0:1, 0:512], psA[0:1, 0:512], 1.0 / C, None, ALU.mult),
                          waits=[f_st2] + ([dbg_fc[(p_tt8, "strow")]]
                                           if (p_tt8, "strow") in dbg_fc else []))
                        E("vector", lambda e: e.tensor_scalar(
                            strowc[0:1, 512:1024], psA[0:1, 512:1024], 1.0 / C, None, ALU.mult))
                        E("vector", lambda e: e.tensor_mul(
                            strowc[0:1, 1024:1536], strowc[0:1, 0:512], strowc[0:1, 0:512]))
                        E("vector", lambda e: e.tensor_sub(
                            strowc[0:1, 512:1024], strowc[0:1, 512:1024], strowc[0:1, 1024:1536]))
                        f_var = IL("vector", "v")
                        f_vars[tt8i] = f_var
                        E("scalar", lambda e: e.activation(
                            strowc[0:1, 512:1024], strowc[0:1, 512:1024], AFT.Sqrt,
                            bias=epsb[0:1, 0:1]), waits=[f_var])
                        f_sd2 = IL("scalar", "s")
                        E("vector", lambda e: e.reciprocal(
                            strowc[0:1, 512:1024], strowc[0:1, 512:1024]), waits=[f_sd2])
                        E("vector", lambda e: e.tensor_mul(
                            strowc[0:1, 1024:1536], strowc[0:1, 0:512], strowc[0:1, 512:1024]))
                        E("vector", lambda e: e.tensor_copy(strowc_b[0:1, 0:512],
                                                            strowc[0:1, 512:1024]))
                        E("vector", lambda e: e.tensor_copy(strowc_b[0:1, 512:1024],
                                                            strowc[0:1, 1024:1536]))
                        f_rows2 = IL("vector", "v")
                        if HAVE_DBG and tt8i == 0:
                            E("gpsimd", lambda e: e.dma_start(
                                out=d_dbg2[0:1, 11264:13312], in_=strowc[0:1, 0:2048]),
                              waits=[f_rows2])
                            dbg_fc[(tt8, "strow")] = IL("gpsimd", "dgw", 16)
                        E("tensor", lambda e: e.matmul(
                            psA[:, 1024:1536], ones_s[0:1, :], strowc_b[0:1, 0:512],
                            start=True, stop=True), waits=[f_rows2])
                        E("tensor", lambda e: e.matmul(
                            psA[:, 1536:2048], ones_s[0:1, :], strowc_b[0:1, 512:1024],
                            start=True, stop=True))
                        f_bc2 = IL("tensor", "t")
                        for chv in range(NCH):
                            oslot = (chv % 2) * 512
                            w_o = []
                            if chv == 0:
                                w_o = [f_bc2, f_ln2.get(("od", p_tt8, 6))]
                            elif chv == 1:
                                w_o = [f_ln2.get(("od", p_tt8, 7))]
                            else:
                                w_o = [f_ln2.get(("od", tt8, chv - 2))]
                                if chv == 2 and (tt8, "outs") in dbg_fc:
                                    w_o.append(dbg_fc[(tt8, "outs")])
                            E("vector", lambda e, chv=chv, oslot=oslot: e.tensor_mul(
                                outs[:, oslot:oslot + 512], r2f[:, chv * 512:(chv + 1) * 512],
                                psA[:, 1024:1536]), waits=w_o)
                            E("vector", lambda e, chv=chv, oslot=oslot: e.tensor_sub(
                                outs[:, oslot:oslot + 512], outs[:, oslot:oslot + 512],
                                psA[:, 1536:2048]))
                            f_o = IL("vector", "v")
                            if HAVE_DBG and tt8i == 0 and chv == 0:
                                E("gpsimd", lambda e: e.dma_start(
                                    out=d_dbg2[:, 13312:13824], in_=outs[:, 0:512]),
                                  waits=[f_o])
                                dbg_fc[(tt8, "outs")] = IL("gpsimd", "dgw", 16)
                            E("sync", lambda e, chv=chv, oslot=oslot, s_=s_, tt=tt: e.dma_start(
                                out=d_out[s_, chv * 128:(chv + 1) * 128, tt * 512:(tt + 1) * 512],
                                in_=outs[:, oslot:oslot + 512]), waits=[f_o])
                            f_ln2[("od", tt8, chv)] = IL(
                                "sync", "dou0" if chv % 2 == 0 else "dou1", 16)
                        f_ln2[("apl", tt8)] = cc("v")
                        for hc in range(NCH):
                            f_ln2[(tt8, hc)] = cc("v")

                    MK("C_end")
                    E("sync", lambda e: e.nop(),
                      waits=[cc("dou0"), cc("dou1")] +
                            ([cc("dgw")] if HAVE_DBG else []))

        with nc.Block() as block:
            def mk(eng_name):
                def body(eng):
                    hw = {}
                    for waits, fn, inc_sem, inc_k in S.ops[eng_name]:
                        for s_n, v_ in waits:
                            if hw.get(s_n, -1) >= v_:
                                continue
                            eng.wait_ge(sems[s_n], v_)
                            hw[s_n] = v_
                        inst = fn(eng)
                        if inc_sem is not None:
                            inst.then_inc(sems[inc_sem], inc_k)
                return body
            block.sync(mk("sync"))
            block.tensor(mk("tensor"))
            block.vector(mk("vector"))
            block.scalar(mk("scalar"))
            block.gpsimd(mk("gpsimd"))
    return nc


def _host_prepare(inputs):
    bf16 = ml_dtypes.bfloat16
    x = np.asarray(inputs["L0_attn_x"], dtype=np.float32)
    wqd = np.asarray(inputs["L0_attn_Wq_down"], dtype=np.float32)
    wkd = np.asarray(inputs["L0_attn_Wk_down"], dtype=np.float32)
    wvd = np.asarray(inputs["L0_attn_Wv_down"], dtype=np.float32)
    wquc = np.asarray(inputs["L0_attn_Wq_up_c"], dtype=np.float32)
    wque = np.asarray(inputs["L0_attn_Wq_up_e"], dtype=np.float32)
    wkuc = np.asarray(inputs["L0_attn_Wk_up_c"], dtype=np.float32)
    wkue = np.asarray(inputs["L0_attn_Wk_up_e"], dtype=np.float32)
    wvu = np.asarray(inputs["L0_attn_Wv_up"], dtype=np.float32)
    wc = np.asarray(inputs["L0_attn_Wc"], dtype=np.float32)
    clipx = np.asarray(inputs["L0_ffn_clip_x"], dtype=np.float32).reshape(1, 1)
    clipd = np.asarray(inputs["L0_ffn_clip_down"], dtype=np.float32).reshape(1, 1)
    wg = np.asarray(inputs["L0_ffn_w_gate"], dtype=np.float32)
    wu = np.asarray(inputs["L0_ffn_w_up"], dtype=np.float32)
    wd = np.asarray(inputs["L0_ffn_w_down"], dtype=np.float32)

    ball = np.zeros((C, 4096), dtype=np.float32)
    perm = np.arange(32)
    perm = perm + 1 - 2 * (perm % 2)
    for g in range(4):
        for j in range(4):
            h = 4 * g + j
            r0 = 256 * g + 64 * j
            c0 = 1024 * g
            ball[r0:r0 + 64, c0 + 32 * j:c0 + 32 * j + 32] = wquc
            ball[r0:r0 + 64, c0 + 128 + 32 * j:c0 + 128 + 32 * j + 32] = wque
            ball[r0:r0 + 64, c0 + 256 + 32 * j:c0 + 256 + 32 * j + 32] = wque[:, perm]
            ball[r0:r0 + 64, c0 + 384 + 32 * j:c0 + 384 + 32 * j + 32] = wkuc
            ball[r0:r0 + 64, c0 + 512 + 32 * j:c0 + 512 + 32 * j + 32] = wkue
            ball[r0:r0 + 64, c0 + 640 + 32 * j:c0 + 640 + 32 * j + 32] = wkue[:, perm]
            ball[r0:r0 + 64, c0 + 768 + 64 * j:c0 + 768 + 64 * j + 64] = wvu

    tpos = np.arange(T, dtype=np.float64)
    invf = 1.0 / (10000.0 ** (np.arange(0, 32, 2, dtype=np.float64) / 32.0))
    ang = np.outer(tpos, invf)
    co = np.cos(ang); si = np.sin(ang)
    cos32 = np.concatenate([co, co], axis=1).T
    sin32 = np.concatenate([si, si], axis=1).T
    sgn = np.where(np.arange(32) % 2 == 0, -1.0, 1.0)
    sin32 = sin32 * sgn[:, None]
    cos_t = np.ascontiguousarray(np.tile(cos32, (4, 1))).astype(bf16)
    sin_t = np.ascontiguousarray(np.tile(sin32, (4, 1))).astype(bf16)

    dmask = np.zeros((128, 4 * 512), dtype=np.float32)
    p = np.arange(128)[:, None]
    f = np.arange(512)[None, :]
    for r in range(4):
        dmask[:, 512 * r:512 * (r + 1)] = (128 * r + p <= f).astype(np.float32)
    dmask = dmask.astype(bf16)
    ident = np.eye(128, dtype=np.float32).astype(bf16)

    shared = {
        "wqdT": np.ascontiguousarray(wqd.T), "wkdT": np.ascontiguousarray(wkd.T),
        "wvdT": np.ascontiguousarray(wvd.T), "ball": ball, "wc": wc,
        "wg": wg, "wu": wu, "wd": wd, "cos_t": cos_t, "sin_t": sin_t,
        "dmask": dmask, "ident": ident, "clipx": clipx, "clipd": clipd,
    }
    in_maps = []
    for cid in range(NCORE):
        xc = np.ascontiguousarray(np.transpose(x[2 * cid:2 * cid + 2], (0, 2, 1)))
        m = dict(shared)
        m["x"] = xc
        in_maps.append(m)
    return in_maps


def kernel(**inputs):
    in_maps = _host_prepare(inputs)
    nc = build_core()
    res = run_bass_kernel_spmd(nc, in_maps, core_ids=list(range(NCORE)))
    out = np.empty((B, T, C), dtype=np.float32)
    for cid in range(NCORE):
        o = np.asarray(res.results[cid]["out"])
        out[2 * cid:2 * cid + 2] = np.transpose(o, (0, 2, 1))
    return out


# version 12
# speedup vs baseline: 1.0519x; 1.0519x over previous
import numpy as np
import ml_dtypes
import concourse.bass as bass
import concourse.mybir as mybir
from concourse.bass_utils import run_bass_kernel_spmd
from contextlib import ExitStack

F32 = mybir.dt.float32
BF16 = mybir.dt.bfloat16
I32 = mybir.dt.int32
ALU = mybir.AluOpType
AFT = mybir.ActivationFunctionType
AX = mybir.AxisListType

B, T, C = 16, 2048, 1024
I_FF = 2816
NCORE = 8
SEQ = 2
TT = 4
NCH = 8
NIC = 22
SCALE = 0.125
C_RND = float(np.float32(3 * 2.0 ** 22))
EPS = 1e-5

STOP_AFTER = None
DOWN_ORDER = None
MARKS = []

ENGS = ("sync", "tensor", "vector", "scalar", "gpsimd")
SEMS = ("t", "v", "s", "g", "cc", "dc", "dxx", "dcl", "dgw", "dwc", "dar",
        "dwg", "dwu", "dwd0", "dwd1", "dsp0", "dsp1", "drb0", "drb1",
        "dou0", "dou1", "dxf0", "dxf1", "dxh", "dxw")
SEM_ENG = {"t": "tensor", "v": "vector", "s": "scalar", "g": "gpsimd"}


class Sched:
    def __init__(self):
        self.ops = {e: [] for e in ENGS}
        self.cnt = {s: 0 for s in SEMS}

    def emit(self, eng, fn, waits=()):
        w = {}
        for f in waits:
            if f is None:
                continue
            s, vv = f
            if vv <= 0:
                continue
            w[s] = max(w.get(s, 0), vv)
        self.ops[eng].append([tuple(w.items()), fn, None, 0])

    def inc_last(self, eng, sem, k=1):
        op = self.ops[eng][-1]
        assert op[2] is None
        op[2] = sem
        op[3] = k
        self.cnt[sem] += k
        return (sem, self.cnt[sem])

    def c(self, sem):
        return (sem, self.cnt[sem])


def build_core():
    nc = bass.Bass("TRN2", target_bir_lowering=False, debug=False,
                   detect_race_conditions=False)
    S = Sched()
    E = S.emit
    IL = S.inc_last
    cc = S.c
    MARKS.clear()
    MK = lambda lbl: MARKS.append((lbl, S.cnt["t"], S.cnt["v"], S.cnt["s"]))

    d_x = nc.dram_tensor("x", [SEQ, C, T], F32, kind="ExternalInput").ap()
    d_wqdT = nc.dram_tensor("wqdT", [C, C], F32, kind="ExternalInput").ap()
    d_wkdT = nc.dram_tensor("wkdT", [C, C], F32, kind="ExternalInput").ap()
    d_wvdT = nc.dram_tensor("wvdT", [C, C], F32, kind="ExternalInput").ap()
    d_ball = nc.dram_tensor("ball", [C, 4096], F32, kind="ExternalInput").ap()
    d_wc = nc.dram_tensor("wc", [C, C], F32, kind="ExternalInput").ap()
    d_wg = nc.dram_tensor("wg", [I_FF, C], F32, kind="ExternalInput").ap()
    d_wu = nc.dram_tensor("wu", [I_FF, C], F32, kind="ExternalInput").ap()
    d_wd = nc.dram_tensor("wd", [C, I_FF], F32, kind="ExternalInput").ap()
    d_cos = nc.dram_tensor("cos_t", [128, T], BF16, kind="ExternalInput").ap()
    d_sin = nc.dram_tensor("sin_t", [128, T], BF16, kind="ExternalInput").ap()
    d_dmask = nc.dram_tensor("dmask", [128, 4 * 512], BF16, kind="ExternalInput").ap()
    d_ident = nc.dram_tensor("ident", [128, 128], BF16, kind="ExternalInput").ap()
    d_clipx = nc.dram_tensor("clipx", [1, 1], F32, kind="ExternalInput").ap()
    d_clipd = nc.dram_tensor("clipd", [1, 1], F32, kind="ExternalInput").ap()
    d_out = nc.dram_tensor("out", [SEQ, C, T], F32, kind="ExternalOutput").ap()
    d_dbg = nc.dram_tensor("dbg", [16, 256], F32, kind="ExternalOutput").ap()
    HAVE_DBG = STOP_AFTER == "c1"
    if HAVE_DBG:
        d_dbg2 = nc.dram_tensor("dbg2", [128, 16384], F32, kind="ExternalOutput").ap()

    d_spill = nc.dram_tensor("h_spill", [SEQ * TT, 128, NIC * 512], F32)
    d_xh = nc.dram_tensor("xh", [SEQ, C, T], F32).ap()
    d_ar1i = nc.dram_tensor("ar1i", [1, 128], F32)
    d_ar1o = nc.dram_tensor("ar1o", [1, 128], F32)
    d_ar2i = nc.dram_tensor("ar2i", [1, 128], F32)
    d_ar2o = nc.dram_tensor("ar2o", [1, 128], F32)

    ctx = ExitStack()
    with ctx:
        sbt = lambda name, shape, dt=BF16: ctx.enter_context(nc.sbuf_tensor(name, shape, dt))
        cos_s = sbt("cos_s", [128, T])
        sin_s = sbt("sin_s", [128, T])
        dmask_s = sbt("dmask_s", [128, 4 * 512])
        ident_s = sbt("ident_s", [128, 128])
        ones_s = sbt("ones_s", [128, 128])
        onesf = sbt("onesf", [128, 1], F32)
        cbias = sbt("cbias", [128, 1], F32)
        epsb = sbt("epsb", [128, 1], F32)
        rsx_b = sbt("rsx_b", [128, 1], F32)
        rsh_b = sbt("rsh_b", [128, 1], F32)
        sh_b = sbt("sh_b", [128, 1], F32)
        sxs_b = sbt("sxs_b", [128, 1], F32)
        biasg = sbt("biasg", [128, 1], F32)
        tiny = sbt("tiny", [128, 16], F32)
        mbt = sbt("mbt", [128, 4])
        rowA = sbt("rowA", [1, 256], F32)
        rowC = sbt("rowC", [1, 16])
        gmx1 = sbt("gmx1", [1, 128], F32)
        gmx2 = sbt("gmx2", [1, 128], F32)
        sdv = sbt("sdv", [128, 8], F32)
        sgv = sbt("sgv", [128, NIC], F32)
        suv = sbt("suv", [128, NIC], F32)
        hred = sbt("hred", [128, 8 * NIC], F32)

        psA = ctx.enter_context(nc.psum_tensor("psA", [128, 2048], F32))
        psB = ctx.enter_context(nc.psum_tensor("psB", [128, 2048], F32))
        psAb = psA.ap().bitcast(BF16)
        psBb = psB.ap().bitcast(BF16)
        psAb = psA.ap().bitcast(BF16)

        sems = {}
        for s_ in SEMS:
            sems[s_] = ctx.enter_context(nc.semaphore("sem_" + s_))

        def xsv(chv, lo, n):
            return xs[:, chv * 4096 + lo: chv * 4096 + lo + n]

        E("gpsimd", lambda e: e.memset(ones_s[:, :], 1.0))
        E("gpsimd", lambda e: e.memset(onesf[:, :], 1.0))
        E("gpsimd", lambda e: e.memset(cbias[:, :], C_RND))
        E("gpsimd", lambda e: e.memset(epsb[:, :], EPS))
        E("gpsimd", lambda e: e.memset(rowA[:, :], 1.0))
        E("gpsimd", lambda e: e.memset(tiny[:, :], 0.0))
        IL("gpsimd", "g")
        g_setup = cc("g")
        E("sync", lambda e: e.dma_start(out=cos_s[:, :], in_=d_cos[:, :]))
        IL("sync", "dc", 16)
        E("sync", lambda e: e.dma_start(out=sin_s[:, :], in_=d_sin[:, :]))
        IL("sync", "dc", 16)
        E("sync", lambda e: e.dma_start(out=dmask_s[:, :], in_=d_dmask[:, :]))
        IL("sync", "dc", 16)
        E("sync", lambda e: e.dma_start(out=ident_s[:, :], in_=d_ident[:, :]))
        IL("sync", "dc", 16)
        E("gpsimd", lambda e: e.dma_start(out=rowA[0:1, 0:1], in_=d_clipx[:, :]), waits=[g_setup])
        IL("gpsimd", "dcl", 16)
        E("gpsimd", lambda e: e.dma_start(out=rowA[0:1, 1:2], in_=d_clipd[:, :]))
        IL("gpsimd", "dcl", 16)
        for s_ in range(SEQ):
            for chv in range(NCH):
                E("gpsimd", lambda e, s_=s_, chv=chv: e.dma_start(
                    out=xsv(chv, s_ * T, T),
                    in_=d_x[s_, chv * 128:(chv + 1) * 128, :]))
                IL("gpsimd", "dxx", 16)
        f_dx_x = cc("dxx")
        f_dx_const = cc("dc")

        def _recip_bf(e, out, in_):
            with nc.allow_low_precision(reason="1/rowsum to bf16; same rounding as fp32 recip + bf16 copy"):
                return e.reciprocal(out, in_)

        def EV(fn, waits=()):
            E("vector", fn, waits=list(waits) + [cc("v")])
            return IL("vector", "v")

        def bcast128(col, dst, waits_v=()):
            EV(lambda e: e.tensor_copy(rowC[0:1, 8:9], rowA[0:1, col:col + 1]),
               waits=waits_v)
            EV(lambda e: e.tensor_copy(tiny[0:1, 15:16], rowC[0:1, 8:9]))
            EV(lambda e: e.tensor_sub(tiny[0:1, 14:15], rowA[0:1, col:col + 1],
                                      tiny[0:1, 15:16]))
            f_v = EV(lambda e: e.tensor_copy(rowC[0:1, 9:10], tiny[0:1, 14:15]))
            E("tensor", lambda e: e.matmul(psB[:, 1536:1537], ones_s[0:1, :], rowC[0:1, 8:9],
                                           start=True, stop=False), waits=[f_v])
            E("tensor", lambda e: e.matmul(psB[:, 1536:1537], ones_s[0:1, :], rowC[0:1, 9:10],
                                           start=False, stop=True))
            f_t = IL("tensor", "t")
            return EV(lambda e: e.tensor_copy(dst[:, :], psB[:, 1536:1537]), waits=[f_t])

        def pmax128(tc, dstc, mc, waits_v=()):
            f_v = EV(lambda e: e.tensor_copy(mbt[:, mc:mc + 1], tiny[:, tc:tc + 1]),
                     waits=waits_v)
            E("tensor", lambda e: e.matmul(psB[0:1, 1536:1664], mbt[:, mc:mc + 1],
                                           ident_s[:, :], start=True, stop=True),
              waits=[f_v])
            f_t = IL("tensor", "t")
            EV(lambda e: e.tensor_copy(rowA[0:1, 128:256], psB[0:1, 1536:1664]),
               waits=[f_t])
            return EV(lambda e: e.tensor_reduce(rowA[0:1, dstc:dstc + 1],
                                                rowA[0:1, 128:256], AX.X, ALU.max))

        f_A_end = {}
        with ExitStack() as actx:
            sa = lambda name, shape, dt=BF16: actx.enter_context(nc.sbuf_tensor(name, shape, dt))
            xs = sa("xs", [128, NCH * 4096])
            eff = sa("eff", [128, NCH * 1024])
            qcb = sa("qcb", [128, T]); qeb = sa("qeb", [128, T])
            kcb = sa("kcb", [128, T]); keb = sa("keb", [128, T])
            erb = sa("erb", [128, T])
            vsb = sa("vsb", [128, 16 * 260])
            attu = sa("attu", [128, 2 * 2048])
            ysb = sa("ysb", [128, NCH * 2048])
            zr = sa("zr", [128, 2048], F32)
            zrb = sa("zrb", [128, 2048])
            wdt3 = sa("wdt3", [128, 3 * 2048])
            xf = wdt3[:, 0:2048].bitcast(F32)
            r1f = wdt3[:, 2048:4096].bitcast(F32)
            ballg = sa("ballg", [128, 2048])
            wcs = sa("wcs", [128, 2 * 1024])
            strow = sa("strow", [1, 1536], F32)
            strow_b = sa("strow_b", [1, 1536])

            E("gpsimd", lambda e: e.memset(zr[:, :], 1.0), waits=[g_setup])
            IL("gpsimd", "g")

            f_exp0_last = None
            f_exp1_last = None
            f_avnorm_last = None
            f_proj_prev = None
            f_vsb_free = None
            f_ln1_v = None

            for s_ in range(SEQ):
                t0 = s_ * T
                for g in range(4):
                    MK(f"s{s_}g{g}_start")
                    hl0 = 256 * g
                    for wi, dsrc in enumerate((d_wqdT, d_wkdT, d_wvdT)):
                        E("gpsimd", lambda e, wi=wi, dsrc=dsrc, hl0=hl0: e.dma_start(
                            out=wdt3[:, wi * 2048: (wi + 1) * 2048]
                                .rearrange("p (a n) -> p a n", a=2),
                            in_=dsrc[hl0:hl0 + 256, :].rearrange("(a p) n -> p a n", p=128)),
                          waits=[f_proj_prev, f_dx_x if (s_ == 0 and g == 0) else None,
                                 f_ln1_v, cc("dxw") if g == 0 else None])
                        IL("gpsimd", "dgw", 16)
                    E("gpsimd", lambda e, hl0=hl0, g=g: e.dma_start(
                        out=ballg[:].rearrange("p (a n) -> p a n", a=2),
                        in_=d_ball[hl0:hl0 + 256, 1024 * g:1024 * (g + 1)]
                            .rearrange("(a p) n -> p a n", p=128)))
                    IL("gpsimd", "dgw", 16)
                    f_gdma = cc("dgw")

                    f_effmm = [None] * NCH
                    f_effev = [None] * NCH
                    for chv in range(NCH):
                        cols = ((0, 384, psA), (512, 384, psA), (1024, 256, psA)) if chv % 2 == 0 \
                            else ((1536, 384, psA), (0, 384, psB), (512, 256, psB))
                        for ri, (pb, nn, ps) in enumerate(cols):
                            r0 = (0, 384, 768)[ri]
                            for a in range(2):
                                w_ = []
                                if chv == 0 and ri == 0 and a == 0:
                                    w_ = [f_gdma, f_dx_const, f_exp0_last,
                                          f_exp1_last, f_avnorm_last, f_ln1_v]
                                if chv >= 2 and ri == 0 and a == 0:
                                    w_ = [f_effev[chv - 2]]
                                E("tensor", lambda e, chv=chv, ri=ri, r0=r0, pb=pb, nn=nn,
                                  ps=ps, a=a: e.matmul(
                                    ps[:, pb:pb + nn],
                                    wdt3[:, ri * 2048 + a * 1024 + chv * 128:
                                         ri * 2048 + a * 1024 + chv * 128 + 128],
                                    ballg[:, a * 1024 + r0: a * 1024 + r0 + nn],
                                    start=(a == 0), stop=(a == 1)), waits=w_)
                        f_effmm[chv] = IL("tensor", "t")
                        for ri, (pb, nn, ps) in enumerate(cols):
                            co = (0, 384, 768)[ri]
                            E("vector", lambda e, chv=chv, pb=pb, nn=nn, ps=ps, co=co:
                              e.tensor_copy(eff[:, chv * 1024 + co: chv * 1024 + co + nn],
                                            ps[:, pb:pb + nn]),
                              waits=[f_effmm[chv]] + ([f_proj_prev] if chv == 0 and ri == 0 else []))
                        f_effev[chv] = IL("vector", "v")
                    f_eff_done = f_effev[NCH - 1]

                    dests = ((qcb, 0), (qeb, 1), (erb, 2), (kcb, 3), (keb, 4), (erb, 5))
                    f_pj = {}
                    f_rope_done = None
                    for di, (dst, ci) in enumerate(dests):
                        for tt in range(TT):
                            pb = 512 * (tt % 2)
                            w_ = [f_eff_done] if (di == 0 and tt <= 1) else []
                            prev = f_pj.get((di, tt - 2)) or f_pj.get((di - 1, tt + 2))
                            if prev is not None:
                                w_.append(prev)
                            if di == 5 and tt == 0 and f_rope_done is not None:
                                w_.append(f_rope_done)
                            for chv in range(NCH):
                                E("tensor", lambda e, ci=ci, tt=tt, chv=chv, pb=pb, t0=t0: e.matmul(
                                    psA[:, pb:pb + 512],
                                    eff[:, chv * 1024 + ci * 128: chv * 1024 + ci * 128 + 128],
                                    xsv(chv, t0 + tt * 512, 512),
                                    start=(chv == 0), stop=(chv == NCH - 1)),
                                  waits=w_ if chv == 0 else [])
                            f_mm = IL("tensor", "t")
                            E("vector", lambda e, dst=dst, tt=tt, pb=pb: e.tensor_copy(
                                dst[:, tt * 512:(tt + 1) * 512], psA[:, pb:pb + 512]),
                              waits=[f_mm])
                            f_pj[(di, tt)] = IL("vector", "v")
                        if di == 2:
                            E("vector", lambda e: e.tensor_mul(erb[:, :], erb[:, :], sin_s[:, :]),
                              waits=[f_pj[(di, TT - 1)]])
                            E("vector", lambda e: e.tensor_mul(qeb[:, :], qeb[:, :], cos_s[:, :]))
                            E("vector", lambda e: e.tensor_add(qeb[:, :], qeb[:, :], erb[:, :]))
                            f_rope_done = IL("vector", "v")
                        if di == 5:
                            E("vector", lambda e: e.tensor_mul(erb[:, :], erb[:, :], sin_s[:, :]),
                              waits=[f_pj[(di, TT - 1)]])
                            E("vector", lambda e: e.tensor_mul(keb[:, :], keb[:, :], cos_s[:, :]))
                            E("vector", lambda e: e.tensor_add(keb[:, :], keb[:, :], erb[:, :]))
                            f_ropek = IL("vector", "v")
                    f_qk_done = cc("v")

                    E("vector", lambda e: e.memset(vsb[:, :], 1.0),
                      waits=[f_vsb_free, f_eff_done])
                    f_vsb_ones = IL("vector", "v")
                    f_vp = {}
                    for tc in range(16):
                        pb = 1024 + 512 * (tc % 2)
                        w_ = [f_vsb_ones] if tc == 0 else []
                        if tc >= 2:
                            w_.append(f_vp[tc - 2])
                        for chv in range(NCH):
                            E("tensor", lambda e, tc=tc, chv=chv, pb=pb, t0=t0: e.matmul(
                                psA[:, pb:pb + 256],
                                xsv(chv, t0 + tc * 128, 128),
                                eff[:, chv * 1024 + 768: chv * 1024 + 1024],
                                start=(chv == 0), stop=(chv == NCH - 1)),
                              waits=w_ if chv == 0 else [])
                        f_mm = IL("tensor", "t")
                        E("vector", lambda e, tc=tc, pb=pb: e.tensor_copy(
                            vsb[:, 260 * tc: 260 * tc + 260]
                                .rearrange("p (j d) -> p j d", j=4)[:, :, 0:64],
                            psA[:, pb:pb + 256].rearrange("p (j d) -> p j d", j=4)),
                          waits=[f_mm])
                        f_vp[tc] = IL("vector", "v")
                    f_v_done = cc("v")

                    EV(lambda e: e.tensor_reduce(tiny[:, 0:1], qcb[:, :], AX.X,
                                                 ALU.max, apply_absolute_value=True),
                       waits=[f_qk_done])
                    EV(lambda e: e.tensor_reduce(tiny[:, 1:2], qeb[:, :], AX.X,
                                                 ALU.max, apply_absolute_value=True))
                    EV(lambda e: e.tensor_max(tiny[:, 0:1], tiny[:, 0:1], tiny[:, 1:2]))
                    EV(lambda e: e.tensor_reduce(tiny[:, 2:3], kcb[:, :], AX.X,
                                                 ALU.max, apply_absolute_value=True))
                    EV(lambda e: e.tensor_reduce(tiny[:, 3:4], keb[:, :], AX.X,
                                                 ALU.max, apply_absolute_value=True))
                    EV(lambda e: e.tensor_max(tiny[:, 2:3], tiny[:, 2:3], tiny[:, 3:4]))
                    pmax128(0, 2, 0)
                    pmax128(2, 3, 1)
                    EV(lambda e: e.tensor_mul(rowA[0:1, 4:5], rowA[0:1, 2:3],
                                              rowA[0:1, 3:4]))
                    EV(lambda e: e.tensor_scalar(rowA[0:1, 4:5], rowA[0:1, 4:5],
                                                 -8.0, None, ALU.mult))
                    f_bias_v = EV(lambda e: e.tensor_copy(rowC[0:1, 4:5], rowA[0:1, 4:5]))
                    E("tensor", lambda e: e.matmul(psB[:, 1536:1537], ones_s[0:1, :],
                                                   rowC[0:1, 4:5], start=True, stop=True),
                      waits=[f_bias_v])
                    f_bias_t = IL("tensor", "t")
                    f_bias = EV(lambda e: e.tensor_copy(biasg[:, :], psB[:, 1536:1537]),
                                waits=[f_bias_t])

                    MK(f"s{s_}g{g}_proj")
                    pend = {"qt": None, "f_evac": None}

                    def emit_norm(g=g):
                        qn = pend["qt"]
                        f_ev_ = pend["f_evac"]
                        pend["qt"] = None
                        for j in range(4):
                            if j % 2 == 0:
                                E("tensor", lambda e, j=j: e.matmul(
                                    psB[0:64, 512 * j:512 * j + 512],
                                    ones_s[64:65, 0:64],
                                    zrb[64:65, 512 * j:512 * j + 512],
                                    start=True, stop=True, tile_position=(64, 0)),
                                  waits=[f_ev_] if j == 0 else [])
                            else:
                                E("tensor", lambda e, j=j: e.matmul(
                                    psB[64:128, 512 * j:512 * j + 512],
                                    ones_s[0:1, 0:64],
                                    zrb[0:1, 512 * j:512 * j + 512],
                                    start=True, stop=True, tile_position=(0, 64)))
                        f_outer = IL("tensor", "t")
                        for j in range(4):
                            yc = 2 * g + j // 2
                            pr = 64 * (j % 2)
                            E("vector", lambda e, j=j, yc=yc, pr=pr, qn=qn: e.tensor_mul(
                                ysb[pr:pr + 64, yc * 2048 + qn * 512: yc * 2048 + qn * 512 + 512],
                                ysb[pr:pr + 64, yc * 2048 + qn * 512: yc * 2048 + qn * 512 + 512],
                                psB[pr:pr + 64, 512 * j:512 * j + 512]),
                              waits=[f_outer] if j == 0 else [])
                        return IL("vector", "v")

                    for qt in range(TT):
                        nkt = 4 * qt + 4
                        f_av = {}
                        av_args = {}

                        def emit_av(kt, nkt):
                            a0, f_dep0, f_dep1 = av_args[kt]
                            w_av = [f_dep0]
                            if kt == 0:
                                w_av.append(f_avnorm_last)
                                w_av.append(f_v_done)
                            for j in range(4):
                                w_j = w_av if j == 0 else ([f_dep1] if j == 2 else [])
                                if j % 2 == 0:
                                    E("tensor", lambda e, j=j, kt=kt, a0=a0, nkt=nkt: e.matmul(
                                        psB[0:65, 512 * j:512 * (j + 1)],
                                        vsb[:, 260 * kt + 65 * j: 260 * kt + 65 * j + 65],
                                        attu[:, a0 + 512 * j: a0 + 512 * (j + 1)],
                                        start=(kt == 0), stop=(kt == nkt - 1)),
                                      waits=w_j)
                                else:
                                    E("tensor", lambda e, j=j, kt=kt, a0=a0, nkt=nkt: e.matmul(
                                        psB[64:128, 512 * j:512 * (j + 1)],
                                        vsb[:, 260 * kt + 65 * j: 260 * kt + 65 * j + 64],
                                        attu[:, a0 + 512 * j: a0 + 512 * (j + 1)],
                                        start=(kt == 0), stop=(kt == nkt - 1)))
                                    E("tensor", lambda e, j=j, kt=kt, a0=a0, nkt=nkt: e.matmul(
                                        psB[0:1, 512 * j:512 * (j + 1)],
                                        ones_s[:, 0:1],
                                        attu[:, a0 + 512 * j: a0 + 512 * (j + 1)],
                                        start=(kt == 0), stop=(kt == nkt - 1)))
                            f_av[kt] = IL("tensor", "t")

                        for kt in range(nkt):
                            w_sc0 = [f_exp0_last, f_bias] if kt == 0 else [f_exp0_last]
                            w_sc1 = [f_exp1_last] if kt > 0 else [f_exp1_last, f_bias]
                            if qt == 0 and kt == 0:
                                w_sc0 += [f_qk_done, f_avnorm_last]
                                w_sc1 += [f_qk_done]
                            for half in range(2):
                                js = (0, 1) if half == 0 else (2, 3)
                                w_h = w_sc0 if half == 0 else w_sc1
                                for j in js:
                                    E("tensor", lambda e, j=j, kt=kt, qt=qt: e.matmul(
                                        psA[:, 512 * j:512 * (j + 1)],
                                        kcb[32 * j:32 * (j + 1), kt * 128:(kt + 1) * 128],
                                        qcb[32 * j:32 * (j + 1), qt * 512:(qt + 1) * 512],
                                        start=True, stop=False, tile_position=(32 * j, 0)),
                                      waits=w_h if j == js[0] else [])
                                for j in js:
                                    E("tensor", lambda e, j=j, kt=kt, qt=qt: e.matmul(
                                        psA[:, 512 * j:512 * (j + 1)],
                                        keb[32 * j:32 * (j + 1), kt * 128:(kt + 1) * 128],
                                        qeb[32 * j:32 * (j + 1), qt * 512:(qt + 1) * 512],
                                        start=False, stop=True, tile_position=(32 * j, 0)))
                                if half == 0:
                                    f_sc0 = IL("tensor", "t")
                                else:
                                    f_sc1 = IL("tensor", "t")
                            a0 = (kt % 2) * 2048
                            w_e0 = [f_sc0]
                            w_e1 = [f_sc1]
                            if f_av.get(kt - 2) is not None:
                                w_e0.append(f_av[kt - 2])
                                w_e1.append(f_av[kt - 2])
                            E("scalar", lambda e, a0=a0: e.activation(
                                attu[:, a0:a0 + 1024], psA[:, 0:1024], AFT.Exp,
                                bias=biasg[:, :], scale=SCALE), waits=w_e0)
                            f_exp0_last = IL("scalar", "s")
                            E("scalar", lambda e, a0=a0: e.activation(
                                attu[:, a0 + 1024:a0 + 2048], psA[:, 1024:2048], AFT.Exp,
                                bias=biasg[:, :], scale=SCALE), waits=w_e1)
                            f_exp1_last = IL("scalar", "s")
                            f_dep0, f_dep1 = f_exp0_last, f_exp1_last
                            if kt >= 4 * qt:
                                r = kt - 4 * qt
                                for j in range(4):
                                    E("vector", lambda e, a0=a0, j=j, r=r: e.tensor_mul(
                                        attu[:, a0 + 512 * j: a0 + 512 * (j + 1)],
                                        attu[:, a0 + 512 * j: a0 + 512 * (j + 1)],
                                        dmask_s[:, r * 512:(r + 1) * 512]),
                                      waits=[f_exp0_last] if j == 0 else
                                            ([f_exp1_last] if j == 2 else []))
                                f_dep1 = IL("vector", "v")
                                f_dep0 = f_dep1
                            av_args[kt] = (a0, f_dep0, f_dep1)
                            if kt == 1 and pend["qt"] is not None:
                                f_avnorm_last = emit_norm()
                            if kt >= 1:
                                emit_av(kt - 1, nkt)
                        emit_av(nkt - 1, nkt)
                        f_av_fin = f_av[nkt - 1]

                        for j in range(4):
                            yc = 2 * g + j // 2
                            prow = 64 * (j % 2)
                            if j % 2 == 0:
                                E("vector", lambda e, j=j, yc=yc, qt=qt: e.tensor_copy(
                                    ysb[0:64, yc * 2048 + qt * 512: yc * 2048 + qt * 512 + 512],
                                    psB[0:64, 512 * j:512 * j + 512]),
                                  waits=[f_av_fin] if j == 0 else [])
                                E("vector", lambda e, j=j: _recip_bf(
                                    e, zrb[64:65, 512 * j:512 * j + 512],
                                    psB[64:65, 512 * j:512 * j + 512]))
                            else:
                                E("vector", lambda e, j=j, yc=yc, qt=qt: e.tensor_copy(
                                    ysb[64:128, yc * 2048 + qt * 512: yc * 2048 + qt * 512 + 512],
                                    psB[64:128, 512 * j:512 * j + 512]))
                                E("vector", lambda e, j=j: _recip_bf(
                                    e, zrb[0:1, 512 * j:512 * j + 512],
                                    psB[0:1, 512 * j:512 * j + 512]))
                        f_evac = IL("vector", "v")
                        pend["qt"] = qt
                        pend["f_evac"] = f_evac
                        if qt == TT - 1:
                            f_avnorm_last = emit_norm()
                    f_proj_prev = cc("t")
                    f_vsb_free = cc("t")

                MK(f"s{s_}_attn_end")
                f_groups_done = cc("t")
                f_y_all = f_avnorm_last
                f_wc = {}
                f_wcd = {}
                f_xhw = {}
                for oc in range(NCH):
                    slot = (oc % 2) * 1024
                    E("gpsimd", lambda e, oc=oc, slot=slot: e.dma_start(
                        out=wcs[:, slot:slot + 1024].rearrange("p (a n) -> p a n", a=8),
                        in_=d_wc[:, oc * 128:(oc + 1) * 128]
                            .rearrange("(a p) n -> p a n", p=128)),
                      waits=[f_wc[oc - 2]] if oc >= 2 else
                            ([f_A_end.get("wcs_prev")] if f_A_end.get("wcs_prev") else []))
                    f_wcd[oc] = IL("gpsimd", "dwc", 16)
                    for tt in range(TT):
                        pb = 512 * (tt % 4)
                        par = tt % 2
                        dsem = "dxf0" if par == 0 else "dxf1"
                        prev = f_wc.get((oc, tt - 4)) or f_wc.get((oc - 1, tt))
                        prev2 = f_wc.get((oc, tt - 2)) or f_wc.get((oc - 1, tt + 2))
                        E("sync", lambda e, oc=oc, tt=tt, par=par, t0=t0, s_=s_: e.dma_start(
                            out=xf[:, par * 512:(par + 1) * 512],
                            in_=d_x[s_, oc * 128:(oc + 1) * 128, tt * 512:(tt + 1) * 512]),
                          waits=[prev2, f_groups_done] if prev2 else [f_groups_done])
                        f_xfd = IL("sync", dsem, 16)
                        w_ = [f_wcd[oc]]
                        if tt == 0 and oc == 0:
                            w_.append(f_y_all)
                        if prev:
                            w_.append(prev)
                        for chv in range(NCH):
                            E("tensor", lambda e, oc=oc, tt=tt, chv=chv, pb=pb, slot=slot:
                              e.matmul(psA[:, pb:pb + 512],
                                       wcs[:, slot + chv * 128: slot + chv * 128 + 128],
                                       ysb[:, chv * 2048 + tt * 512: chv * 2048 + tt * 512 + 512],
                                       start=(chv == 0), stop=(chv == NCH - 1)),
                              waits=w_ if chv == 0 else [])
                        f_mm = IL("tensor", "t")
                        w_stt = [f_mm, f_xfd]
                        pw = f_xhw.get((oc, tt - 2)) or f_xhw.get((oc - 1, tt + 2))
                        if pw:
                            w_stt.append(pw)
                        E("vector", lambda e, par=par, pb=pb: e.scalar_tensor_tensor(
                            r1f[:, par * 512:(par + 1) * 512], psA[:, pb:pb + 512],
                            onesf[:, :], xf[:, par * 512:(par + 1) * 512],
                            ALU.mult, ALU.add),
                          waits=w_stt)
                        E("vector", lambda e, oc=oc, tt=tt, par=par, t0=t0: e.tensor_copy(
                            xsv(oc, t0 + tt * 512, 512), r1f[:, par * 512:(par + 1) * 512]))
                        f_r1v = IL("vector", "v")
                        E("sync", lambda e, oc=oc, tt=tt, par=par, s_=s_: e.dma_start(
                            out=d_xh[s_, oc * 128:(oc + 1) * 128, tt * 512:(tt + 1) * 512],
                            in_=r1f[:, par * 512:(par + 1) * 512]), waits=[f_r1v])
                        f_xhw[(oc, tt)] = IL("sync", "dxh", 16)
                        f_wc[(oc, tt)] = f_r1v
                    f_wc[oc] = f_wc[(oc, TT - 1)]
                f_r1w_all = cc("dxh")
                f_A_end["wcs_prev"] = cc("t")
                f_r1 = cc("v")

                MK(f"s{s_}_wc_end")
                f_sq = {}
                f_st = {}
                for chv in range(NCH):
                    slot = 0
                    w_ = [f_r1] if chv == 0 else [f_st[chv - 1]]
                    E("scalar", lambda e, chv=chv, slot=slot, t0=t0: e.activation(
                        attu[:, slot:slot + 2048], xsv(chv, t0, 2048), AFT.Square), waits=w_)
                    f_sq[chv] = IL("scalar", "s")
                    for tt in range(TT):
                        E("tensor", lambda e, chv=chv, tt=tt, t0=t0: e.matmul(
                            psB[0:1, tt * 512:(tt + 1) * 512], ones_s[:, 0:1],
                            xsv(chv, t0 + tt * 512, 512),
                            start=(chv == 0), stop=(chv == NCH - 1)),
                          waits=[f_sq[chv], f_r1, f_avnorm_last] if (chv == 0 and tt == 0)
                                else [f_sq[chv]] if tt == 0 else [])
                        E("tensor", lambda e, chv=chv, tt=tt, slot=slot: e.matmul(
                            psA[0:1, tt * 512:(tt + 1) * 512], ones_s[:, 0:1],
                            attu[:, slot + tt * 512: slot + (tt + 1) * 512],
                            start=(chv == 0), stop=(chv == NCH - 1)))
                    f_st[chv] = IL("tensor", "t")
                f_stats = f_st[NCH - 1]

                f_ap = {}
                f_xw = {}
                f_apn = {}
                f_xwn = {}
                xfs3 = [xf[:, 0:512], xf[:, 512:1024],
                        wdt3[:, 4096:6144].bitcast(F32)[:, 0:512]]
                for tt in range(TT):
                    E("vector", lambda e, tt=tt: e.tensor_scalar(
                        strow[0:1, 0:512], psB[0:1, tt * 512:(tt + 1) * 512],
                        1.0 / C, None, ALU.mult), waits=[f_stats])
                    E("vector", lambda e, tt=tt: e.tensor_scalar(
                        strow[0:1, 512:1024], psA[0:1, tt * 512:(tt + 1) * 512],
                        1.0 / C, None, ALU.mult))
                    E("vector", lambda e: e.tensor_mul(
                        strow[0:1, 1024:1536], strow[0:1, 0:512], strow[0:1, 0:512]))
                    E("vector", lambda e: e.tensor_sub(
                        strow[0:1, 512:1024], strow[0:1, 512:1024], strow[0:1, 1024:1536]))
                    f_var = IL("vector", "v")
                    E("scalar", lambda e: e.activation(
                        strow[0:1, 512:1024], strow[0:1, 512:1024], AFT.Sqrt,
                        bias=epsb[0:1, 0:1]), waits=[f_var])
                    f_sd = IL("scalar", "s")
                    E("vector", lambda e: e.reciprocal(
                        strow[0:1, 512:1024], strow[0:1, 512:1024]), waits=[f_sd])
                    E("vector", lambda e: e.tensor_mul(
                        strow[0:1, 1024:1536], strow[0:1, 0:512], strow[0:1, 512:1024]))
                    E("vector", lambda e: e.tensor_copy(strow_b[0:1, 0:512],
                                                        strow[0:1, 512:1024]))
                    E("vector", lambda e: e.tensor_copy(strow_b[0:1, 512:1024],
                                                        strow[0:1, 1024:1536]))
                    E("vector", lambda e: e.tensor_copy(strow[0:1, 0:512],
                                                        strow_b[0:1, 0:512]))
                    E("vector", lambda e: e.tensor_sub(strow[0:1, 0:512],
                                                       strow[0:1, 512:1024],
                                                       strow[0:1, 0:512]))
                    E("vector", lambda e: e.tensor_copy(strow_b[0:1, 1024:1536],
                                                        strow[0:1, 0:512]))
                    f_rows = IL("vector", "v")
                    E("tensor", lambda e, tt=tt: e.matmul(
                        psB[:, tt * 512:(tt + 1) * 512], ones_s[0:1, :],
                        strow_b[0:1, 0:512], start=True, stop=False), waits=[f_rows])
                    E("tensor", lambda e, tt=tt: e.matmul(
                        psB[:, tt * 512:(tt + 1) * 512], ones_s[0:1, :],
                        strow_b[0:1, 1024:1536], start=False, stop=True))
                    E("tensor", lambda e, tt=tt: e.matmul(
                        psA[:, tt * 512:(tt + 1) * 512], ones_s[0:1, :],
                        strow_b[0:1, 512:1024], start=True, stop=True))
                    f_bc = IL("tensor", "t")
                    for oc in range(NCH):
                        n_ = tt * NCH + oc
                        parx = n_ % 3
                        parr = oc % 2
                        dsem = ("dxf0", "dxf1", "dxh")[parx]
                        w_d = [f_r1w_all]
                        if (n_ - 3) in f_apn:
                            w_d.append(f_apn[n_ - 3])
                        E("sync", lambda e, oc=oc, tt=tt, parx=parx, s_=s_: e.dma_start(
                            out=xfs3[parx],
                            in_=d_xh[s_, oc * 128:(oc + 1) * 128, tt * 512:(tt + 1) * 512]),
                          waits=w_d)
                        f_xfd = IL("sync", dsem, 16)
                        w_t = [f_xfd, f_bc]
                        pw = f_xwn.get(n_ - 2)
                        if pw:
                            w_t.append(pw)
                        E("vector", lambda e, parx=parx, parr=parr, tt=tt: e.tensor_mul(
                            r1f[:, parr * 512:(parr + 1) * 512],
                            xfs3[parx],
                            psB[:, tt * 512:(tt + 1) * 512]), waits=w_t)
                        E("vector", lambda e, parr=parr, tt=tt: e.tensor_sub(
                            r1f[:, parr * 512:(parr + 1) * 512],
                            r1f[:, parr * 512:(parr + 1) * 512],
                            psA[:, tt * 512:(tt + 1) * 512]))
                        E("vector", lambda e, parr=parr, col=tt * NCH + oc: e.tensor_reduce(
                            zr[:, col:col + 1], r1f[:, parr * 512:(parr + 1) * 512],
                            AX.X, ALU.max, apply_absolute_value=True))
                        f_ap[(tt, oc)] = IL("vector", "v")
                        f_apn[n_] = f_ap[(tt, oc)]
                        E("scalar", lambda e, oc=oc, tt=tt, parr=parr, s_=s_: e.dma_start(
                            out=d_xh[s_, oc * 128:(oc + 1) * 128, tt * 512:(tt + 1) * 512],
                            in_=r1f[:, parr * 512:(parr + 1) * 512]),
                          waits=[f_ap[(tt, oc)]])
                        f_xw[(tt, oc)] = IL("scalar", "dxw", 16)
                        f_xwn[n_] = f_xw[(tt, oc)]
                EV(lambda e: e.tensor_reduce(tiny[:, 4:5], zr[:, 0:32],
                                             AX.X, ALU.max))
                if s_ == 0:
                    EV(lambda e: e.tensor_copy(tiny[:, 5:6], tiny[:, 4:5]))
                else:
                    EV(lambda e: e.tensor_max(tiny[:, 5:6], tiny[:, 5:6],
                                              tiny[:, 4:5]))
                f_ln1_v = cc("v")

            f_xh_all = [cc("dxh"), cc("dxw")]
            if STOP_AFTER == "ln1":
                E("gpsimd", lambda e: e.dma_start(out=d_out[0, :, :], in_=d_xh[0, :, :]),
                  waits=[cc("v"), cc("t"), cc("s")] + f_xh_all)
                IL("gpsimd", "dou0", 16)
                for di_, ap_ in enumerate((ones_s[0:1, 0:128], epsb[:, 0:1],
                                           tiny[:, 4:5], tiny[:, 5:6],
                                           rowA[0:1, 0:256], cbias[:, 0:1],
                                           gmx1[0:1, 0:128], biasg[:, 0:1])):
                    E("gpsimd", lambda e, di_=di_, ap_=ap_: e.dma_start(
                        out=d_out[1, di_ * 128:di_ * 128 + ap_.shape[0], 0:ap_.shape[1]],
                        in_=ap_, allow_slow_non_contiguous=True))
                    IL("gpsimd", "dou0", 16)
                E("sync", lambda e: e.nop(), waits=[cc("dou0")])

            if STOP_AFTER in (None, "h", "c1"):
                MK("ln1_end")
                f_xmax = pmax128(5, 5, 2, waits_v=[f_ln1_v])
                EV(lambda e: e.memset(gmx1[:, :], 0.0), waits=[f_xmax])
                f_gm = EV(lambda e: e.tensor_copy(gmx1[0:1, 0:1], rowA[0:1, 5:6]))
                E("gpsimd", lambda e: e.dma_start(out=d_ar1i[:, :], in_=gmx1[:, :]),
                  waits=[f_gm])
                IL("gpsimd", "dar", 16)
                f_ar_in = cc("dar")
                E("gpsimd", lambda e: e.collective_compute(
                    "AllReduce", ALU.max, replica_groups=[list(range(NCORE))],
                    ins=[d_ar1i[:, :]], outs=[d_ar1o[:, :]]), waits=[f_ar_in])
                IL("gpsimd", "cc")
                f_cc1 = cc("cc")
                E("gpsimd", lambda e: e.dma_start(out=gmx1[:, :], in_=d_ar1o[:, :]),
                  waits=[f_cc1])
                IL("gpsimd", "dar", 16)
                f_gmax1 = cc("dar")

                E("scalar", lambda e: e.activation(rowA[0:1, 6:7], rowA[0:1, 0:1], AFT.Sigmoid),
                  waits=[cc("dcl"), f_gmax1])
                f_sig = IL("scalar", "s")
                EV(lambda e: e.tensor_mul(rowA[0:1, 7:8], rowA[0:1, 6:7],
                                          gmx1[0:1, 0:1]), waits=[f_sig, f_gmax1])
                EV(lambda e: e.tensor_scalar(rowA[0:1, 8:9], rowA[0:1, 7:8],
                                             1.0 / 127.0, None, ALU.mult))
                EV(lambda e: e.reciprocal(rowA[0:1, 9:10], rowA[0:1, 8:9]))
                bcast128(9, rsx_b)
                bcast128(8, sxs_b)
                f_sx = cc("v")

                E("tensor", lambda e: e.nop(), waits=[])
                f_A_t = IL("tensor", "t")
                E("scalar", lambda e: e.copy(tiny[0:1, 13:14], tiny[0:1, 13:14]), waits=[f_sx])
                f_A_s = IL("scalar", "s")
                E("vector", lambda e: e.tensor_copy(tiny[0:1, 12:13], tiny[0:1, 12:13]))
                f_A_v = IL("vector", "v")
                E("gpsimd", lambda e: e.memset(tiny[0:1, 11:12], 0.0))
                f_A_g = IL("gpsimd", "g")
        if STOP_AFTER in (None, "h", "c1"):
            barrier_A = [f_A_t, f_A_s, f_A_v, f_A_g, f_sx]

            with ExitStack() as bctx:
                sbq = lambda name, shape, dt=BF16: bctx.enter_context(nc.sbuf_tensor(name, shape, dt))
                xq = sbq("xq", [128, NCH * 4096])
                hst = sbq("hst", [128, 2 * 4096], F32)
                rndq = bctx.enter_context(nc.sbuf_tensor("rndq", [128, 2 * 2048], I32))
                rndw = bctx.enter_context(nc.sbuf_tensor("rndw", [128, 1024], I32))
                xhq = sbq("xhq", [128, 3 * 2048], F32)
                wgo = sbq("wgo", [128, 1024], F32)
                wuo = sbq("wuo", [128, 1024], F32)
                wgT = sbq("wgT", [128, 2 * 1024])
                wuT = sbq("wuT", [128, 2 * 1024])
                wqi = sbq("wqi", [128, 4 * 1024])
                gbf = sbq("gbf", [128, 3 * 512])

                def xqv(chv, lo, n):
                    return xq[:, chv * 4096 + lo: chv * 4096 + lo + n]

                f_xq = {}
                f_xq_seq = {}
                nq = 0
                for s2 in range(SEQ):
                    for chv in range(NCH):
                        par = nq % 3
                        dsem = ("dxf0", "dxf1", "dxw")[par]
                        w_d = list(barrier_A) + f_xh_all
                        if (nq - 3) in f_xq:
                            w_d.append(f_xq[nq - 3])
                        E("sync", lambda e, chv=chv, s2=s2, par=par: e.dma_start(
                            out=xhq[:, par * 2048:(par + 1) * 2048],
                            in_=d_xh[s2, chv * 128:(chv + 1) * 128, :]), waits=w_d)
                        f_xd = IL("sync", dsem, 16)
                        qpar = nq % 2
                        w_s = [f_xd]
                        if (nq - 2) in f_xq:
                            w_s.append(f_xq[nq - 2])
                        else:
                            w_s += list(barrier_A)
                        E("scalar", lambda e, par=par, qpar=qpar: e.activation(
                            rndq.ap().bitcast(F32)[:, qpar * 2048:(qpar + 1) * 2048],
                            xhq[:, par * 2048:(par + 1) * 2048],
                            AFT.Identity, bias=cbias[:, :], scale=rsx_b[:, :]),
                          waits=w_s)
                        f_rd = IL("scalar", "s")
                        E("vector", lambda e, qpar=qpar: e.tensor_scalar(
                            rndq.ap().bitcast(F32)[:, qpar * 2048:(qpar + 1) * 2048],
                            rndq.ap().bitcast(F32)[:, qpar * 2048:(qpar + 1) * 2048],
                            C_RND + 127.0, C_RND - 128.0, ALU.min, ALU.max), waits=[f_rd])
                        E("vector", lambda e, chv=chv, s2=s2, qpar=qpar: e.tensor_scalar(
                            xqv(chv, s2 * T, T),
                            rndq.ap().bitcast(F32)[:, qpar * 2048:(qpar + 1) * 2048],
                            C_RND, None, ALU.subtract))
                        f_xq[nq] = IL("vector", "v")
                        nq += 1
                    f_xq_seq[s2] = cc("v")
                f_xq_done = cc("v")

                MK("B_xq_end")
                f_wq = {}
                f_int = {}
                f_rnd = {}
                f_trf = {}
                f_sp = {}
                f_hred_last = None

                def emit_chain(ic):
                    for wi, (dsrc, wo, svec) in enumerate(
                            ((d_wg, wgo, sgv), (d_wu, wuo, suv))):
                        w_d = list(barrier_A) if ic == 0 and wi == 0 else []
                        if (ic - 1, wi) in f_rnd:
                            w_d.append(f_rnd[(ic - 1, wi)])
                        E("scalar", lambda e, dsrc=dsrc, wo=wo, ic=ic: e.dma_start(
                            out=wo[:, :], in_=dsrc[ic * 128:(ic + 1) * 128, :]), waits=w_d)
                        f_wd = IL("scalar", "dwg" if wi == 0 else "dwu", 16)
                        w_r = [f_wd]
                        if wi == 1:
                            w_r.append(f_rnd[(ic, 0)])
                        EV(lambda e, wo=wo: e.tensor_reduce(
                            tiny[:, 6:7], wo[:, :], AX.X, ALU.max,
                            apply_absolute_value=True), waits=w_r)
                        EV(lambda e: e.reciprocal(tiny[:, 7:8], tiny[:, 6:7]))
                        EV(lambda e: e.tensor_scalar(tiny[:, 7:8], tiny[:, 7:8],
                                                     127.0, None, ALU.mult))
                        EV(lambda e: e.tensor_scalar(
                            tiny[:, 8:9], tiny[:, 6:7], 1.0 / 127.0, None, ALU.mult))
                        f_rsw = EV(lambda e, svec=svec, ic=ic: e.tensor_mul(
                            svec[:, ic:ic + 1], tiny[:, 8:9], sxs_b[:, :]))
                        E("scalar", lambda e, wo=wo: e.activation(
                            rndw.ap().bitcast(F32), wo[:, :], AFT.Identity,
                            bias=cbias[:, :], scale=tiny[:, 7:8]),
                          waits=[f_rsw, cc("v")])
                        f_rnd[(ic, wi)] = IL("scalar", "s")
                        E("vector", lambda e: e.tensor_scalar(
                            rndw.ap().bitcast(F32), rndw.ap().bitcast(F32),
                            C_RND + 127.0, C_RND - 128.0, ALU.min, ALU.max),
                          waits=[f_rnd[(ic, wi)]])
                        qslot = ((ic % 2) * 2 + wi) * 1024
                        E("vector", lambda e, qslot=qslot: e.tensor_scalar(
                            wqi[:, qslot:qslot + 1024], rndw.ap().bitcast(F32),
                            C_RND, None, ALU.subtract),
                          waits=[f_trf.get((ic - 2, wi))])
                        f_int[(ic, wi)] = IL("vector", "v")

                def emit_tr(ic, wi):
                    if True:
                        wT = wgT if wi == 0 else wuT
                        pb = psAb if wi == 0 else psBb
                        qslot = ((ic % 2) * 2 + wi) * 1024
                        for b in range(8):
                            E("tensor", lambda e, b=b, pb=pb, qslot=qslot: e.transpose(
                                pb[:, 3072 + 128 * b: 3072 + 128 * (b + 1)],
                                wqi[:, qslot + 128 * b: qslot + 128 * (b + 1)],
                                ident_s[:, :]),
                              waits=[f_int[(ic, wi)], f_wq.get((ic - 1, wi))]
                                    if b == 0 else [])
                        f_trf[(ic, wi)] = IL("tensor", "t")
                        slot = (ic % 2) * 1024
                        E("scalar", lambda e, wT=wT, slot=slot, pb=pb: e.copy(
                            wT[:, slot:slot + 1024], pb[:, 3072:4096]),
                          waits=[f_trf[(ic, wi)]])
                        f_wq[(ic, wi)] = IL("scalar", "s")

                def emit_mms(ic):
                    slot = (ic % 2) * 1024
                    hslot = (ic % 2) * 4096
                    for tt8 in range(SEQ * TT):
                        pg = 512 * (tt8 % 3)
                        w_ = []
                        if tt8 == 0:
                            w_ += [f_wq[(ic, 0)], f_wq[(ic, 1)], f_xq_seq[0]]
                            if (ic - 2) in f_sp:
                                w_.append(f_sp[ic - 2])
                        if tt8 == 4:
                            w_.append(f_xq_seq[1])
                        prev = f_wq.get(("evac", ic, tt8 - 3)) or f_wq.get(("evac", ic - 1, tt8 + 5))
                        if prev:
                            w_.append(prev)
                        for b in range(8):
                            E("tensor", lambda e, b=b, tt8=tt8, slot=slot, pg=pg: e.matmul(
                                psA[:, pg:pg + 512],
                                wgT[:, slot + 128 * b: slot + 128 * (b + 1)],
                                xqv(b, tt8 * 512, 512),
                                start=(b == 0), stop=(b == 7)), waits=w_ if b == 0 else [])
                        f_g = IL("tensor", "t")
                        for b in range(8):
                            E("tensor", lambda e, b=b, tt8=tt8, slot=slot, pg=pg: e.matmul(
                                psB[:, pg:pg + 512],
                                wuT[:, slot + 128 * b: slot + 128 * (b + 1)],
                                xqv(b, tt8 * 512, 512),
                                start=(b == 0), stop=(b == 7)))
                        f_u = IL("tensor", "t")
                        gslot = 512 * (tt8 % 3)
                        w_silu = [f_g]
                        if prev:
                            w_silu.append(prev)
                        E("scalar", lambda e, pg=pg, gslot=gslot, ic=ic: e.activation(
                            gbf[:, gslot:gslot + 512], psA[:, pg:pg + 512], AFT.Silu,
                            scale=sgv[:, ic:ic + 1]), waits=w_silu)
                        f_silu = IL("scalar", "s")
                        E("vector", lambda e, pg=pg, gslot=gslot, ic=ic, hslot=hslot, tt8=tt8:
                          e.scalar_tensor_tensor(
                              hst[:, hslot + tt8 * 512: hslot + (tt8 + 1) * 512],
                              psB[:, pg:pg + 512], suv[:, ic:ic + 1],
                              gbf[:, gslot:gslot + 512], ALU.mult, ALU.mult),
                          waits=[f_u, f_silu])
                        E("vector", lambda e, hslot=hslot, tt8=tt8, ic=ic: e.tensor_reduce(
                            hred[:, ic * 8 + tt8: ic * 8 + tt8 + 1],
                            hst[:, hslot + tt8 * 512: hslot + (tt8 + 1) * 512],
                            AX.X, ALU.max, apply_absolute_value=True))
                        f_wq[("evac", ic, tt8)] = IL("vector", "v")
                        if ic + 1 < NIC and tt8 in (3, 6):
                            emit_tr(ic + 1, 0 if tt8 == 3 else 1)
                    f_hr = f_wq[("evac", ic, SEQ * TT - 1)]
                    E("sync", lambda e, ic=ic, hslot=hslot: e.dma_start(
                        out=d_spill[:, :, ic * 512:(ic + 1) * 512].transpose([1, 0, 2]),
                        in_=hst[:, hslot:hslot + 4096].rearrange("p (a n) -> p a n", a=8)),
                      waits=[f_hr])
                    f_sp[ic] = IL("sync", "dsp0" if ic % 2 == 0 else "dsp1", 16)
                    return f_hr

                emit_chain(0)
                emit_tr(0, 0)
                emit_tr(0, 1)
                emit_chain(1)
                for ic in range(NIC):
                    MK(f"B_ic{ic}")
                    if ic + 2 < NIC:
                        emit_chain(ic + 2)
                    f_hred_last = emit_mms(ic)
                f_spill_all = [cc("dsp0"), cc("dsp1")]

                if STOP_AFTER == "h":
                    for a_ in range(6):
                        n_ = 2048 if a_ < 5 else NIC * 512 - 5 * 2048
                        E("gpsimd", lambda e, a_=a_, n_=n_: e.dma_start(
                            out=d_out[0, a_ * 128:(a_ + 1) * 128, 0:n_],
                            in_=d_spill[0, :, a_ * 2048: a_ * 2048 + n_]),
                          waits=([cc("v"), cc("t"), cc("s")] + f_spill_all) if a_ == 0 else [])
                        IL("gpsimd", "dou0", 16)
                    for di_, ap_ in enumerate((gmx1[0:1, 0:128], rowA[0:1, 0:256],
                                               rsx_b[:, 0:1], sxs_b[:, 0:1],
                                               sgv[:, 0:1], suv[:, 0:1],
                                               tiny[:, 5:6], hred[:, 0:1])):
                        E("gpsimd", lambda e, di_=di_, ap_=ap_: e.dma_start(
                            out=d_out[1, di_ * 128:di_ * 128 + ap_.shape[0],
                                      0:ap_.shape[1]], in_=ap_,
                            allow_slow_non_contiguous=True))
                        IL("gpsimd", "dou0", 16)
                    E("sync", lambda e: e.nop(), waits=[cc("dou0")])

                MK("B_mm_end")
                EV(lambda e: e.tensor_reduce(tiny[:, 9:10], hred[:, :], AX.X,
                                             ALU.max), waits=[f_hred_last])
                f_hm = pmax128(9, 12, 3)
                EV(lambda e: e.memset(gmx2[:, :], 0.0), waits=[f_hm])
                f_gm2 = EV(lambda e: e.tensor_copy(gmx2[0:1, 0:1], rowA[0:1, 12:13]))
                E("gpsimd", lambda e: e.dma_start(out=d_ar2i[:, :], in_=gmx2[:, :]),
                  waits=[f_gm2])
                IL("gpsimd", "dar", 16)
                f_ar2_in = cc("dar")
                E("gpsimd", lambda e: e.collective_compute(
                    "AllReduce", ALU.max, replica_groups=[list(range(NCORE))],
                    ins=[d_ar2i[:, :]], outs=[d_ar2o[:, :]]), waits=[f_ar2_in])
                IL("gpsimd", "cc")
                f_cc2 = cc("cc")
                E("gpsimd", lambda e: e.dma_start(out=gmx2[:, :], in_=d_ar2o[:, :]),
                  waits=[f_cc2])
                IL("gpsimd", "dar", 16)
                f_gmax2 = cc("dar")

                E("scalar", lambda e: e.activation(rowA[0:1, 13:14], rowA[0:1, 1:2], AFT.Sigmoid),
                  waits=[f_gmax2])
                f_sig2 = IL("scalar", "s")
                EV(lambda e: e.tensor_mul(rowA[0:1, 14:15], rowA[0:1, 13:14],
                                          gmx2[0:1, 0:1]), waits=[f_sig2, f_gmax2])
                EV(lambda e: e.tensor_scalar(rowA[0:1, 15:16], rowA[0:1, 14:15],
                                             1.0 / 127.0, None, ALU.mult))
                EV(lambda e: e.reciprocal(rowA[0:1, 16:17], rowA[0:1, 15:16]))
                bcast128(16, rsh_b)
                bcast128(15, sh_b)
                f_sh = cc("v")

                E("tensor", lambda e: e.nop(), waits=[])
                f_B_t = IL("tensor", "t")
                E("scalar", lambda e: e.copy(tiny[0:1, 13:14], tiny[0:1, 13:14]), waits=[f_sh])
                f_B_s = IL("scalar", "s")
                E("vector", lambda e: e.tensor_copy(tiny[0:1, 12:13], tiny[0:1, 12:13]))
                f_B_v = IL("vector", "v")
                E("gpsimd", lambda e: e.memset(tiny[0:1, 11:12], 0.0))
                f_B_g = IL("gpsimd", "g")
            barrier_B = [f_B_t, f_B_s, f_B_v, f_B_g, f_sh] + f_spill_all
            SKIP_C = STOP_AFTER == "h"

            if not SKIP_C:
                with ExitStack() as cctx:
                    sbc = lambda name, shape, dt=BF16: cctx.enter_context(nc.sbuf_tensor(name, shape, dt))
                    wdT = sbc("wdT", [128, NIC * 1024])
                    wdo = sbc("wdo", [128, 2 * I_FF], F32)
                    rndc = cctx.enter_context(nc.sbuf_tensor("rndc", [128, I_FF], I32))
                    wdi = sbc("wdi", [128, I_FF])
                    rb = sbc("rb", [128, 2 * 2048], F32)
                    xr = sbc("xr", [128, 2 * 8 * 512], F32)
                    r2f = sbc("r2f", [128, 8 * 512], F32)
                    r2b = sbc("r2b", [128, 2 * 512])
                    hq = sbc("hq", [128, 2 * 512])
                    outs = sbc("outs", [128, 2 * 512], F32)
                    sqc = sbc("sqc", [128, 2 * 512])
                    strowc = sbc("strowc", [1, 2048], F32)
                    strowc_b = sbc("strowc_b", [1, 1024])

                    MK("C_start")
                    rndc2 = rb.ap().bitcast(I32)[:, 0:I_FF]
                    wdi2 = r2f.ap().bitcast(BF16)[:, 0:I_FF]
                    f_dq = {}
                    f_intq = {}

                    def emit_wdchain(hc):
                        slot = (hc % 2) * I_FF
                        rnd_t = rndc.ap().bitcast(F32) if hc % 2 == 0 else rndc2.bitcast(F32)
                        wdi_t = wdi if hc % 2 == 0 else wdi2
                        w_ = list(barrier_B) if hc == 0 else []
                        if ("rnd", hc - 2) in f_dq:
                            w_.append(f_dq[("rnd", hc - 2)])
                        E("sync", lambda e, hc=hc, slot=slot: e.dma_start(
                            out=wdo[:, slot:slot + I_FF], in_=d_wd[hc * 128:(hc + 1) * 128, :]),
                          waits=w_)
                        f_wd = IL("sync", "dwd0" if hc % 2 == 0 else "dwd1", 16)
                        EV(lambda e, slot=slot: e.tensor_reduce(
                            tiny[:, 6:7], wdo[:, slot:slot + I_FF], AX.X, ALU.max,
                            apply_absolute_value=True),
                           waits=[f_wd, f_dq.get(("rnd", hc - 1))] +
                                 (list(barrier_B) if hc == 0 else []))
                        EV(lambda e: e.reciprocal(tiny[:, 7:8], tiny[:, 6:7]))
                        EV(lambda e: e.tensor_scalar(tiny[:, 7:8], tiny[:, 7:8],
                                                     127.0, None, ALU.mult))
                        EV(lambda e: e.tensor_scalar(tiny[:, 8:9], tiny[:, 6:7],
                                                     1.0 / 127.0, None, ALU.mult))
                        f_rsw = EV(lambda e, hc=hc: e.tensor_mul(sdv[:, hc:hc + 1], tiny[:, 8:9],
                                                                 sh_b[:, :]))
                        E("scalar", lambda e, slot=slot, rnd_t=rnd_t: e.activation(
                            rnd_t, wdo[:, slot:slot + I_FF], AFT.Identity,
                            bias=cbias[:, :], scale=tiny[:, 7:8]),
                          waits=[f_rsw] + (list(barrier_B) if hc == 0 else []))
                        f_rd = IL("scalar", "s")
                        f_dq[("rnd", hc)] = f_rd
                        E("vector", lambda e, rnd_t=rnd_t: e.tensor_scalar(
                            rnd_t, rnd_t,
                            C_RND + 127.0, C_RND - 128.0, ALU.min, ALU.max), waits=[f_rd])
                        E("vector", lambda e, rnd_t=rnd_t, wdi_t=wdi_t: e.tensor_scalar(
                            wdi_t[:, 0:I_FF], rnd_t, C_RND, None, ALU.subtract))
                        f_intq[hc] = IL("vector", "v")

                    def emit_wdtr(hc):
                        wdi_t = wdi if hc % 2 == 0 else wdi2
                        for b8 in range(3):
                            nb = 8 if b8 < 2 else 6
                            for u in range(nb):
                                icx = b8 * 8 + u
                                E("tensor", lambda e, icx=icx, u=u, wdi_t=wdi_t: e.transpose(
                                    psBb[:, 3072 + 128 * u: 3072 + 128 * (u + 1)],
                                    wdi_t[:, 128 * icx:128 * (icx + 1)], ident_s[:, :]),
                                  waits=([f_intq[hc]] + (list(barrier_B) if hc == 0 else []) +
                                         ([f_dq[hc - 1]] if (hc - 1) in f_dq else []))
                                        if (b8 == 0 and u == 0) else
                                        ([f_dq[("ev", hc, b8 - 1)]] if u == 0 else []))
                            f_tr = IL("tensor", "t")
                            E("vector", lambda e, hc=hc, b8=b8, nb=nb: e.tensor_copy(
                                wdT[:, :].rearrange("p (i n) -> p i n", i=NIC)
                                   [:, b8 * 8: b8 * 8 + nb, 128 * hc:128 * (hc + 1)],
                                psBb[:, 3072:3072 + nb * 128].rearrange("p (i n) -> p i n", i=nb)),
                              waits=[f_tr])
                            f_dq[("ev", hc, b8)] = IL("vector", "v")
                        f_dq[hc] = f_dq[("ev", hc, 2)]

                    emit_wdchain(0)
                    for hc in range(NCH):
                        if hc + 1 < NCH:
                            emit_wdchain(hc + 1)
                        emit_wdtr(hc)
                    f_wdT_done = f_dq[NCH - 1]

                    if HAVE_DBG:
                        EV(lambda e: e.tensor_reduce(tiny[:, 12:13],
                                                     wdT[:, 0:2048].bitcast(F32),
                                                     AX.X, ALU.max,
                                                     apply_absolute_value=True),
                           waits=[f_wdT_done])
                        for di_, ap_ in enumerate((rowA[0:1, 0:256], gmx2[0:1, 0:128],
                                                   sdv[0:1, 0:8], rsh_b[0:1, 0:1],
                                                   sh_b[0:1, 0:1], tiny[0:1, 12:13])):
                            E("gpsimd", lambda e, di_=di_, ap_=ap_: e.dma_start(
                                out=d_dbg[di_:di_ + 1, 0:ap_.shape[1]],
                                in_=ap_, allow_slow_non_contiguous=True),
                              waits=[cc("v"), cc("s"), cc("t")])
                            IL("gpsimd", "dou0", 16)

                    f_ev = {}
                    f_rbd = {}
                    f_ln2 = {}
                    dbg_fc = {}
                    f_r2s = {}
                    f_vars = {}
                    _ord = list(DOWN_ORDER or range(SEQ * TT))
                    for tt8i, tt8 in enumerate(_ord):
                        MK(f"C_tt8_{tt8i}")
                        p_tt8 = _ord[tt8i - 1] if tt8i > 0 else None
                        s_ = tt8 // TT
                        tt = tt8 % TT
                        t0 = s_ * T

                        def emit_rb(rs_, tt8=tt8):
                            n = 2048 if rs_ < 5 else 1024
                            slot = (rs_ % 2) * 2048
                            w_ = list(f_spill_all)
                            if p_tt8 is None:
                                w_.append(f_wdT_done)
                            if rs_ in (2, 3) and (tt8, "rb") in dbg_fc:
                                w_.append(dbg_fc[(tt8, "rb")])
                            if rs_ >= 2:
                                w_.append(f_ev[(tt8, rs_ * 4 - 5)])
                            if p_tt8 is not None:
                                w_.append(f_ev.get((p_tt8, 19 if rs_ == 0 else NIC - 1)))
                            E("sync", lambda e, tt8=tt8, rs_=rs_, n=n, slot=slot: e.dma_start(
                                out=rb[:, slot:slot + n],
                                in_=d_spill[tt8, :, rs_ * 2048: rs_ * 2048 + n]), waits=w_)
                            f_rbd[(tt8, rs_)] = IL("sync", "drb0" if rs_ % 2 == 0 else "drb1", 16)

                        xpar = tt8i % 2
                        dsemx = "dxf0" if xpar == 0 else "dxf1"
                        w_xr = list(f_xh_all)
                        if tt8i >= 2:
                            w_xr.append(f_r2s[tt8i - 2])
                            if tt8i == 2 and (_ord[0], "xr") in dbg_fc:
                                w_xr.append(dbg_fc[(_ord[0], "xr")])
                        E("sync", lambda e, s_=s_, tt=tt, xpar=xpar: e.dma_start(
                            out=xr[:, xpar * 4096:(xpar + 1) * 4096]
                                .rearrange("p (a n) -> p a n", a=8),
                            in_=d_xh[s_, :, tt * 512:(tt + 1) * 512]
                                .rearrange("(a p) n -> p a n", p=128)), waits=w_xr)
                        f_xrd = IL("sync", dsemx, 16)
                        emit_rb(0)
                        emit_rb(1)
                        if HAVE_DBG and tt8i == 0:
                            E("gpsimd", lambda e: e.dma_start(
                                out=d_dbg2[:, 0:2048], in_=rb[:, 0:2048]),
                              waits=[f_rbd[(tt8, 0)]])
                            dbg_fc[(tt8, "rb")] = IL("gpsimd", "dgw", 16)
                            E("gpsimd", lambda e: e.dma_start(
                                out=d_dbg2[:, 3072:7168], in_=xr[:, 0:4096]),
                              waits=[f_xrd])
                            dbg_fc[(tt8, "xr")] = IL("gpsimd", "dgw", 16)
                        for ic in range(NIC):
                            if ic >= 4 and (ic - 4) % 4 == 0 and (ic - 4) // 4 + 2 < 6 \
                                    and (tt8, (ic - 4) // 4 + 2) not in f_rbd:
                                emit_rb((ic - 4) // 4 + 2)
                            rs_ = (ic * 512) // 2048
                            off = (ic * 512) % 2048 + ((rs_ % 2) * 2048)
                            hslot = (ic % 2) * 512
                            qs = hslot
                            E("scalar", lambda e, off=off, qs=qs: e.activation(
                                rndc.ap().bitcast(F32)[:, qs:qs + 512], rb[:, off:off + 512],
                                AFT.Identity, bias=cbias[:, :], scale=rsh_b[:, :]),
                              waits=[f_rbd[(tt8, rs_)], f_wdT_done] +
                                    ([f_ev[(tt8, ic - 2)]] if (tt8, ic - 2) in f_ev else []) +
                                    ([f_ev.get((p_tt8, NIC - 1))] if (ic < 2 and p_tt8 is not None)
                                     else []) +
                                    ([dbg_fc[(tt8, "hq")]] if (ic == 2 and (tt8, "hq") in dbg_fc)
                                     else []))
                            f_rd = IL("scalar", "s")
                            E("vector", lambda e, qs=qs: e.tensor_scalar(
                                rndc.ap().bitcast(F32)[:, qs:qs + 512],
                                rndc.ap().bitcast(F32)[:, qs:qs + 512],
                                C_RND + 127.0, C_RND - 128.0, ALU.min, ALU.max), waits=[f_rd])
                            E("vector", lambda e, hslot=hslot, qs=qs: e.tensor_scalar(
                                hq[:, hslot:hslot + 512], rndc.ap().bitcast(F32)[:, qs:qs + 512],
                                C_RND, None, ALU.subtract))
                            f_hq = IL("vector", "v")
                            if HAVE_DBG and tt8i == 0 and ic == 0:
                                E("gpsimd", lambda e: e.dma_start(
                                    out=d_dbg2[:, 2048:2560], in_=hq[:, 0:512]),
                                  waits=[f_hq])
                                dbg_fc[(tt8, "hq")] = IL("gpsimd", "dgw", 16)
                                E("gpsimd", lambda e: e.dma_start(
                                    out=d_dbg2[:, 2560:2568], in_=sdv[:, 0:8]),
                                  waits=[f_hq])
                                IL("gpsimd", "dgw", 16)
                            hc_order = (4, 5, 6, 7, 0, 1, 2, 3) if ic == 0 else range(NCH)
                            for hi, hc in enumerate(hc_order):
                                ps = psA if hc < 4 else psB
                                pb = 512 * (hc % 4)
                                w_ = [f_hq] if hi == 0 else []
                                if ic == 0 and p_tt8 is not None:
                                    if hc == 4:
                                        w_.append(f_r2s[tt8i - 1])
                                    elif hc == 0:
                                        w_.append(f_vars[tt8i - 1])
                                    elif hc == 2:
                                        w_.append(f_ln2[("apl", p_tt8)])
                                E("tensor", lambda e, ic=ic, hc=hc, ps=ps, pb=pb, hslot=hslot:
                                  e.matmul(ps[:, pb:pb + 512],
                                           wdT[:, ic * 1024 + hc * 128: ic * 1024 + hc * 128 + 128],
                                           hq[:, hslot:hslot + 512],
                                           start=(ic == 0), stop=(ic == NIC - 1)),
                                  waits=w_)
                            f_ev[(tt8, ic)] = IL("tensor", "t")
                        f_dn = f_ev[(tt8, NIC - 1)]
                        for hc in range(NCH):
                            ps = psA if hc < 4 else psB
                            pb = 512 * (hc % 4)
                            w_e = [f_dn, f_xrd] if hc == 0 else []
                            if hc == 0 and p_tt8 is not None:
                                w_e.append(f_ln2[("apl", p_tt8)])
                                if (p_tt8, "r2f") in dbg_fc:
                                    w_e.append(dbg_fc[(p_tt8, "r2f")])
                            E("vector", lambda e, hc=hc, ps=ps, pb=pb, xpar=xpar:
                              e.scalar_tensor_tensor(
                                  r2f[:, hc * 512:(hc + 1) * 512], ps[:, pb:pb + 512],
                                  sdv[:, hc:hc + 1],
                                  xr[:, xpar * 4096 + hc * 512: xpar * 4096 + (hc + 1) * 512],
                                  ALU.mult, ALU.add), waits=w_e)
                        f_r2 = IL("vector", "v")
                        f_r2s[tt8i] = f_r2
                        if HAVE_DBG and tt8i == 0:
                            E("gpsimd", lambda e: e.dma_start(
                                out=d_dbg2[:, 7168:11264], in_=r2f[:, 0:4096]),
                              waits=[f_r2])
                            dbg_fc[(tt8, "r2f")] = IL("gpsimd", "dgw", 16)

                        f_sq2 = {}
                        for chv in range(NCH):
                            slot = (chv % 2) * 512
                            E("vector", lambda e, chv=chv, slot=slot: e.tensor_copy(
                                r2b[:, slot:slot + 512], r2f[:, chv * 512:(chv + 1) * 512]),
                              waits=([f_sq2[chv - 2]] if chv >= 2 else []))
                            f_cp = IL("vector", "v")
                            E("scalar", lambda e, slot=slot: e.activation(
                                sqc[:, slot:slot + 512], r2b[:, slot:slot + 512], AFT.Square),
                              waits=[f_cp])
                            f_sqs = IL("scalar", "s")
                            E("tensor", lambda e, chv=chv, slot=slot: e.matmul(
                                psA[0:1, 0:512], ones_s[:, 0:1], r2b[:, slot:slot + 512],
                                start=(chv == 0), stop=(chv == NCH - 1)),
                              waits=[f_cp, f_sqs])
                            E("tensor", lambda e, chv=chv, slot=slot: e.matmul(
                                psA[0:1, 512:1024], ones_s[:, 0:1], sqc[:, slot:slot + 512],
                                start=(chv == 0), stop=(chv == NCH - 1)))
                            f_sq2[chv] = IL("tensor", "t")
                        f_st2 = f_sq2[NCH - 1]
                        E("vector", lambda e: e.tensor_scalar(
                            strowc[0:1, 0:512], psA[0:1, 0:512], 1.0 / C, None, ALU.mult),
                          waits=[f_st2] + ([dbg_fc[(p_tt8, "strow")]]
                                           if (p_tt8, "strow") in dbg_fc else []))
                        E("vector", lambda e: e.tensor_scalar(
                            strowc[0:1, 512:1024], psA[0:1, 512:1024], 1.0 / C, None, ALU.mult))
                        E("vector", lambda e: e.tensor_mul(
                            strowc[0:1, 1024:1536], strowc[0:1, 0:512], strowc[0:1, 0:512]))
                        E("vector", lambda e: e.tensor_sub(
                            strowc[0:1, 512:1024], strowc[0:1, 512:1024], strowc[0:1, 1024:1536]))
                        f_var = IL("vector", "v")
                        f_vars[tt8i] = f_var
                        E("scalar", lambda e: e.activation(
                            strowc[0:1, 512:1024], strowc[0:1, 512:1024], AFT.Sqrt,
                            bias=epsb[0:1, 0:1]), waits=[f_var])
                        f_sd2 = IL("scalar", "s")
                        E("vector", lambda e: e.reciprocal(
                            strowc[0:1, 512:1024], strowc[0:1, 512:1024]), waits=[f_sd2])
                        E("vector", lambda e: e.tensor_mul(
                            strowc[0:1, 1024:1536], strowc[0:1, 0:512], strowc[0:1, 512:1024]))
                        E("vector", lambda e: e.tensor_copy(strowc_b[0:1, 0:512],
                                                            strowc[0:1, 512:1024]))
                        E("vector", lambda e: e.tensor_copy(strowc_b[0:1, 512:1024],
                                                            strowc[0:1, 1024:1536]))
                        f_rows2 = IL("vector", "v")
                        if HAVE_DBG and tt8i == 0:
                            E("gpsimd", lambda e: e.dma_start(
                                out=d_dbg2[0:1, 11264:13312], in_=strowc[0:1, 0:2048]),
                              waits=[f_rows2])
                            dbg_fc[(tt8, "strow")] = IL("gpsimd", "dgw", 16)
                        E("tensor", lambda e: e.matmul(
                            psA[:, 1024:1536], ones_s[0:1, :], strowc_b[0:1, 0:512],
                            start=True, stop=True), waits=[f_rows2])
                        E("tensor", lambda e: e.matmul(
                            psA[:, 1536:2048], ones_s[0:1, :], strowc_b[0:1, 512:1024],
                            start=True, stop=True))
                        f_bc2 = IL("tensor", "t")
                        for chv in range(NCH):
                            oslot = (chv % 2) * 512
                            w_o = []
                            if chv == 0:
                                w_o = [f_bc2, f_ln2.get(("od", p_tt8, 6))]
                            elif chv == 1:
                                w_o = [f_ln2.get(("od", p_tt8, 7))]
                            else:
                                w_o = [f_ln2.get(("od", tt8, chv - 2))]
                                if chv == 2 and (tt8, "outs") in dbg_fc:
                                    w_o.append(dbg_fc[(tt8, "outs")])
                            E("vector", lambda e, chv=chv, oslot=oslot: e.tensor_mul(
                                outs[:, oslot:oslot + 512], r2f[:, chv * 512:(chv + 1) * 512],
                                psA[:, 1024:1536]), waits=w_o)
                            E("vector", lambda e, chv=chv, oslot=oslot: e.tensor_sub(
                                outs[:, oslot:oslot + 512], outs[:, oslot:oslot + 512],
                                psA[:, 1536:2048]))
                            f_o = IL("vector", "v")
                            if HAVE_DBG and tt8i == 0 and chv == 0:
                                E("gpsimd", lambda e: e.dma_start(
                                    out=d_dbg2[:, 13312:13824], in_=outs[:, 0:512]),
                                  waits=[f_o])
                                dbg_fc[(tt8, "outs")] = IL("gpsimd", "dgw", 16)
                            E("sync", lambda e, chv=chv, oslot=oslot, s_=s_, tt=tt: e.dma_start(
                                out=d_out[s_, chv * 128:(chv + 1) * 128, tt * 512:(tt + 1) * 512],
                                in_=outs[:, oslot:oslot + 512]), waits=[f_o])
                            f_ln2[("od", tt8, chv)] = IL(
                                "sync", "dou0" if chv % 2 == 0 else "dou1", 16)
                        f_ln2[("apl", tt8)] = cc("v")
                        for hc in range(NCH):
                            f_ln2[(tt8, hc)] = cc("v")

                    MK("C_end")
                    E("sync", lambda e: e.nop(),
                      waits=[cc("dou0"), cc("dou1")] +
                            ([cc("dgw")] if HAVE_DBG else []))

        with nc.Block() as block:
            def mk(eng_name):
                def body(eng):
                    hw = {}
                    for waits, fn, inc_sem, inc_k in S.ops[eng_name]:
                        for s_n, v_ in waits:
                            if hw.get(s_n, -1) >= v_:
                                continue
                            eng.wait_ge(sems[s_n], v_)
                            hw[s_n] = v_
                        inst = fn(eng)
                        if inc_sem is not None:
                            inst.then_inc(sems[inc_sem], inc_k)
                return body
            block.sync(mk("sync"))
            block.tensor(mk("tensor"))
            block.vector(mk("vector"))
            block.scalar(mk("scalar"))
            block.gpsimd(mk("gpsimd"))
    return nc


def _host_prepare(inputs):
    bf16 = ml_dtypes.bfloat16
    x = np.asarray(inputs["L0_attn_x"], dtype=np.float32)
    wqd = np.asarray(inputs["L0_attn_Wq_down"], dtype=np.float32)
    wkd = np.asarray(inputs["L0_attn_Wk_down"], dtype=np.float32)
    wvd = np.asarray(inputs["L0_attn_Wv_down"], dtype=np.float32)
    wquc = np.asarray(inputs["L0_attn_Wq_up_c"], dtype=np.float32)
    wque = np.asarray(inputs["L0_attn_Wq_up_e"], dtype=np.float32)
    wkuc = np.asarray(inputs["L0_attn_Wk_up_c"], dtype=np.float32)
    wkue = np.asarray(inputs["L0_attn_Wk_up_e"], dtype=np.float32)
    wvu = np.asarray(inputs["L0_attn_Wv_up"], dtype=np.float32)
    wc = np.asarray(inputs["L0_attn_Wc"], dtype=np.float32)
    clipx = np.asarray(inputs["L0_ffn_clip_x"], dtype=np.float32).reshape(1, 1)
    clipd = np.asarray(inputs["L0_ffn_clip_down"], dtype=np.float32).reshape(1, 1)
    wg = np.asarray(inputs["L0_ffn_w_gate"], dtype=np.float32)
    wu = np.asarray(inputs["L0_ffn_w_up"], dtype=np.float32)
    wd = np.asarray(inputs["L0_ffn_w_down"], dtype=np.float32)

    ball = np.zeros((C, 4096), dtype=np.float32)
    perm = np.arange(32)
    perm = perm + 1 - 2 * (perm % 2)
    for g in range(4):
        for j in range(4):
            h = 4 * g + j
            r0 = 256 * g + 64 * j
            c0 = 1024 * g
            ball[r0:r0 + 64, c0 + 32 * j:c0 + 32 * j + 32] = wquc
            ball[r0:r0 + 64, c0 + 128 + 32 * j:c0 + 128 + 32 * j + 32] = wque
            ball[r0:r0 + 64, c0 + 256 + 32 * j:c0 + 256 + 32 * j + 32] = wque[:, perm]
            ball[r0:r0 + 64, c0 + 384 + 32 * j:c0 + 384 + 32 * j + 32] = wkuc
            ball[r0:r0 + 64, c0 + 512 + 32 * j:c0 + 512 + 32 * j + 32] = wkue
            ball[r0:r0 + 64, c0 + 640 + 32 * j:c0 + 640 + 32 * j + 32] = wkue[:, perm]
            ball[r0:r0 + 64, c0 + 768 + 64 * j:c0 + 768 + 64 * j + 64] = wvu

    tpos = np.arange(T, dtype=np.float64)
    invf = 1.0 / (10000.0 ** (np.arange(0, 32, 2, dtype=np.float64) / 32.0))
    ang = np.outer(tpos, invf)
    co = np.cos(ang); si = np.sin(ang)
    cos32 = np.concatenate([co, co], axis=1).T
    sin32 = np.concatenate([si, si], axis=1).T
    sgn = np.where(np.arange(32) % 2 == 0, -1.0, 1.0)
    sin32 = sin32 * sgn[:, None]
    cos_t = np.ascontiguousarray(np.tile(cos32, (4, 1))).astype(bf16)
    sin_t = np.ascontiguousarray(np.tile(sin32, (4, 1))).astype(bf16)

    dmask = np.zeros((128, 4 * 512), dtype=np.float32)
    p = np.arange(128)[:, None]
    f = np.arange(512)[None, :]
    for r in range(4):
        dmask[:, 512 * r:512 * (r + 1)] = (128 * r + p <= f).astype(np.float32)
    dmask = dmask.astype(bf16)
    ident = np.eye(128, dtype=np.float32).astype(bf16)

    shared = {
        "wqdT": np.ascontiguousarray(wqd.T), "wkdT": np.ascontiguousarray(wkd.T),
        "wvdT": np.ascontiguousarray(wvd.T), "ball": ball, "wc": wc,
        "wg": wg, "wu": wu, "wd": wd, "cos_t": cos_t, "sin_t": sin_t,
        "dmask": dmask, "ident": ident, "clipx": clipx, "clipd": clipd,
    }
    in_maps = []
    for cid in range(NCORE):
        xc = np.ascontiguousarray(np.transpose(x[2 * cid:2 * cid + 2], (0, 2, 1)))
        m = dict(shared)
        m["x"] = xc
        in_maps.append(m)
    return in_maps


def kernel(**inputs):
    in_maps = _host_prepare(inputs)
    nc = build_core()
    res = run_bass_kernel_spmd(nc, in_maps, core_ids=list(range(NCORE)))
    out = np.empty((B, T, C), dtype=np.float32)
    for cid in range(NCORE):
        o = np.asarray(res.results[cid]["out"])
        out[2 * cid:2 * cid + 2] = np.transpose(o, (0, 2, 1))
    return out


# version 13
# speedup vs baseline: 1.0720x; 1.0191x over previous
import numpy as np
import ml_dtypes
import concourse.bass as bass
import concourse.mybir as mybir
from concourse.bass_utils import run_bass_kernel_spmd
from contextlib import ExitStack

F32 = mybir.dt.float32
BF16 = mybir.dt.bfloat16
I32 = mybir.dt.int32
ALU = mybir.AluOpType
AFT = mybir.ActivationFunctionType
AX = mybir.AxisListType

B, T, C = 16, 2048, 1024
I_FF = 2816
NCORE = 8
SEQ = 2
TT = 4
NCH = 8
NIC = 22
SCALE = 0.125
C_RND = float(np.float32(3 * 2.0 ** 22))
EPS = 1e-5

STOP_AFTER = None
DOWN_ORDER = None
MARKS = []

ENGS = ("sync", "tensor", "vector", "scalar", "gpsimd")
SEMS = ("t", "v", "s", "g", "cc", "dc", "dxx", "dcl", "dgw", "dwc", "dar",
        "dwg", "dwu", "dwd0", "dwd1", "dsp0", "dsp1", "drb0", "drb1",
        "dou0", "dou1", "dxf0", "dxf1", "dxh", "dxw")
SEM_ENG = {"t": "tensor", "v": "vector", "s": "scalar", "g": "gpsimd"}


class Sched:
    def __init__(self):
        self.ops = {e: [] for e in ENGS}
        self.cnt = {s: 0 for s in SEMS}

    def emit(self, eng, fn, waits=()):
        w = {}
        for f in waits:
            if f is None:
                continue
            s, vv = f
            if vv <= 0:
                continue
            w[s] = max(w.get(s, 0), vv)
        self.ops[eng].append([tuple(w.items()), fn, None, 0])

    def inc_last(self, eng, sem, k=1):
        op = self.ops[eng][-1]
        assert op[2] is None
        op[2] = sem
        op[3] = k
        self.cnt[sem] += k
        return (sem, self.cnt[sem])

    def c(self, sem):
        return (sem, self.cnt[sem])


def build_core():
    nc = bass.Bass("TRN2", target_bir_lowering=False, debug=False,
                   detect_race_conditions=False)
    S = Sched()
    E = S.emit
    IL = S.inc_last
    cc = S.c
    MARKS.clear()
    MK = lambda lbl: MARKS.append((lbl, S.cnt["t"], S.cnt["v"], S.cnt["s"]))

    d_x = nc.dram_tensor("x", [SEQ, C, T], F32, kind="ExternalInput").ap()
    d_wqdT = nc.dram_tensor("wqdT", [C, C], F32, kind="ExternalInput").ap()
    d_wkdT = nc.dram_tensor("wkdT", [C, C], F32, kind="ExternalInput").ap()
    d_wvdT = nc.dram_tensor("wvdT", [C, C], F32, kind="ExternalInput").ap()
    d_ball = nc.dram_tensor("ball", [C, 4096], F32, kind="ExternalInput").ap()
    d_wc = nc.dram_tensor("wc", [C, C], F32, kind="ExternalInput").ap()
    d_wg = nc.dram_tensor("wg", [I_FF, C], F32, kind="ExternalInput").ap()
    d_wu = nc.dram_tensor("wu", [I_FF, C], F32, kind="ExternalInput").ap()
    d_wd = nc.dram_tensor("wd", [C, I_FF], F32, kind="ExternalInput").ap()
    d_cos = nc.dram_tensor("cos_t", [128, T], BF16, kind="ExternalInput").ap()
    d_sin = nc.dram_tensor("sin_t", [128, T], BF16, kind="ExternalInput").ap()
    d_dmask = nc.dram_tensor("dmask", [128, 4 * 512], BF16, kind="ExternalInput").ap()
    d_ident = nc.dram_tensor("ident", [128, 128], BF16, kind="ExternalInput").ap()
    d_clipx = nc.dram_tensor("clipx", [1, 1], F32, kind="ExternalInput").ap()
    d_clipd = nc.dram_tensor("clipd", [1, 1], F32, kind="ExternalInput").ap()
    d_out = nc.dram_tensor("out", [SEQ, C, T], F32, kind="ExternalOutput").ap()
    d_dbg = nc.dram_tensor("dbg", [16, 256], F32, kind="ExternalOutput").ap()
    HAVE_DBG = STOP_AFTER == "c1"
    if HAVE_DBG:
        d_dbg2 = nc.dram_tensor("dbg2", [128, 16384], F32, kind="ExternalOutput").ap()

    d_spill = nc.dram_tensor("h_spill", [SEQ * TT, 128, NIC * 512], F32)
    d_xh = nc.dram_tensor("xh", [SEQ, C, T], F32).ap()
    d_ar1i = nc.dram_tensor("ar1i", [1, 128], F32)
    d_ar1o = nc.dram_tensor("ar1o", [1, 128], F32)
    d_ar2i = nc.dram_tensor("ar2i", [1, 128], F32)
    d_ar2o = nc.dram_tensor("ar2o", [1, 128], F32)

    ctx = ExitStack()
    with ctx:
        sbt = lambda name, shape, dt=BF16: ctx.enter_context(nc.sbuf_tensor(name, shape, dt))
        cos_s = sbt("cos_s", [128, T])
        sin_s = sbt("sin_s", [128, T])
        dmask_s = sbt("dmask_s", [128, 4 * 512])
        ident_s = sbt("ident_s", [128, 128])
        ones_s = sbt("ones_s", [128, 128])
        onesf = sbt("onesf", [128, 1], F32)
        cbias = sbt("cbias", [128, 1], F32)
        epsb = sbt("epsb", [128, 1], F32)
        rsx_b = sbt("rsx_b", [128, 1], F32)
        rsh_b = sbt("rsh_b", [128, 1], F32)
        sh_b = sbt("sh_b", [128, 1], F32)
        sxs_b = sbt("sxs_b", [128, 1], F32)
        biasg = sbt("biasg", [128, 1], F32)
        tiny = sbt("tiny", [128, 16], F32)
        mbt = sbt("mbt", [128, 4])
        rowA = sbt("rowA", [1, 256], F32)
        rowC = sbt("rowC", [1, 16])
        gmx1 = sbt("gmx1", [1, 128], F32)
        gmx2 = sbt("gmx2", [1, 128], F32)
        sdv = sbt("sdv", [128, 8], F32)
        sgv = sbt("sgv", [128, NIC], F32)
        suv = sbt("suv", [128, NIC], F32)
        hred = sbt("hred", [128, 8 * NIC], F32)

        psA = ctx.enter_context(nc.psum_tensor("psA", [128, 2048], F32))
        psB = ctx.enter_context(nc.psum_tensor("psB", [128, 2048], F32))
        psAb = psA.ap().bitcast(BF16)
        psBb = psB.ap().bitcast(BF16)
        psAb = psA.ap().bitcast(BF16)

        sems = {}
        for s_ in SEMS:
            sems[s_] = ctx.enter_context(nc.semaphore("sem_" + s_))

        def xsv(chv, lo, n):
            return xs[:, chv * 4096 + lo: chv * 4096 + lo + n]

        E("gpsimd", lambda e: e.memset(ones_s[:, :], 1.0))
        E("gpsimd", lambda e: e.memset(onesf[:, :], 1.0))
        E("gpsimd", lambda e: e.memset(cbias[:, :], C_RND))
        E("gpsimd", lambda e: e.memset(epsb[:, :], EPS))
        E("gpsimd", lambda e: e.memset(rowA[:, :], 1.0))
        E("gpsimd", lambda e: e.memset(tiny[:, :], 0.0))
        IL("gpsimd", "g")
        g_setup = cc("g")
        E("sync", lambda e: e.dma_start(out=cos_s[:, :], in_=d_cos[:, :]))
        IL("sync", "dc", 16)
        E("sync", lambda e: e.dma_start(out=sin_s[:, :], in_=d_sin[:, :]))
        IL("sync", "dc", 16)
        E("sync", lambda e: e.dma_start(out=dmask_s[:, :], in_=d_dmask[:, :]))
        IL("sync", "dc", 16)
        E("sync", lambda e: e.dma_start(out=ident_s[:, :], in_=d_ident[:, :]))
        IL("sync", "dc", 16)
        E("gpsimd", lambda e: e.dma_start(out=rowA[0:1, 0:1], in_=d_clipx[:, :]), waits=[g_setup])
        IL("gpsimd", "dcl", 16)
        E("gpsimd", lambda e: e.dma_start(out=rowA[0:1, 1:2], in_=d_clipd[:, :]))
        IL("gpsimd", "dcl", 16)
        for s_ in range(SEQ):
            for chv in range(NCH):
                E("gpsimd", lambda e, s_=s_, chv=chv: e.dma_start(
                    out=xsv(chv, s_ * T, T),
                    in_=d_x[s_, chv * 128:(chv + 1) * 128, :]))
                IL("gpsimd", "dxx", 16)
        f_dx_x = cc("dxx")
        f_dx_const = cc("dc")

        def _recip_bf(e, out, in_):
            with nc.allow_low_precision(reason="1/rowsum to bf16; same rounding as fp32 recip + bf16 copy"):
                return e.reciprocal(out, in_)

        def EV(fn, waits=()):
            E("vector", fn, waits=list(waits) + [cc("v")])
            return IL("vector", "v")

        def bcast128(col, dst, waits_v=()):
            EV(lambda e: e.tensor_copy(rowC[0:1, 8:9], rowA[0:1, col:col + 1]),
               waits=waits_v)
            EV(lambda e: e.tensor_copy(tiny[0:1, 15:16], rowC[0:1, 8:9]))
            EV(lambda e: e.tensor_sub(tiny[0:1, 14:15], rowA[0:1, col:col + 1],
                                      tiny[0:1, 15:16]))
            f_v = EV(lambda e: e.tensor_copy(rowC[0:1, 9:10], tiny[0:1, 14:15]))
            E("tensor", lambda e: e.matmul(psB[:, 1536:1537], ones_s[0:1, :], rowC[0:1, 8:9],
                                           start=True, stop=False), waits=[f_v])
            E("tensor", lambda e: e.matmul(psB[:, 1536:1537], ones_s[0:1, :], rowC[0:1, 9:10],
                                           start=False, stop=True))
            f_t = IL("tensor", "t")
            return EV(lambda e: e.tensor_copy(dst[:, :], psB[:, 1536:1537]), waits=[f_t])

        def pmax128(tc, dstc, mc, waits_v=()):
            f_v = EV(lambda e: e.tensor_copy(mbt[:, mc:mc + 1], tiny[:, tc:tc + 1]),
                     waits=waits_v)
            E("tensor", lambda e: e.matmul(psB[0:1, 1536:1664], mbt[:, mc:mc + 1],
                                           ident_s[:, :], start=True, stop=True),
              waits=[f_v])
            f_t = IL("tensor", "t")
            EV(lambda e: e.tensor_copy(rowA[0:1, 128:256], psB[0:1, 1536:1664]),
               waits=[f_t])
            return EV(lambda e: e.tensor_reduce(rowA[0:1, dstc:dstc + 1],
                                                rowA[0:1, 128:256], AX.X, ALU.max))

        f_A_end = {}
        with ExitStack() as actx:
            sa = lambda name, shape, dt=BF16: actx.enter_context(nc.sbuf_tensor(name, shape, dt))
            xs = sa("xs", [128, NCH * 4096])
            eff = sa("eff", [128, NCH * 1024])
            qcb = sa("qcb", [128, T]); qeb = sa("qeb", [128, T])
            kcb = sa("kcb", [128, T]); keb = sa("keb", [128, T])
            erb = sa("erb", [128, T])
            vsb = sa("vsb", [128, 16 * 260])
            attu = sa("attu", [128, 2 * 2048])
            ysb = sa("ysb", [128, NCH * 2048])
            zr = sa("zr", [128, 2048], F32)
            zrb = sa("zrb", [128, 2048])
            wdt3 = sa("wdt3", [128, 3 * 2048])
            xf = wdt3[:, 0:2048].bitcast(F32)
            r1f = wdt3[:, 2048:4096].bitcast(F32)
            ballg = sa("ballg", [128, 2048])
            wcs = sa("wcs", [128, 2 * 1024])
            strow = sa("strow", [1, 1536], F32)
            strow_b = sa("strow_b", [1, 1536])

            E("gpsimd", lambda e: e.memset(zr[:, :], 1.0), waits=[g_setup])
            IL("gpsimd", "g")

            f_exp0_last = None
            f_exp1_last = None
            f_avnorm_last = None
            f_proj_prev = None
            f_vsb_free = None
            f_ln1_v = None

            for s_ in range(SEQ):
                t0 = s_ * T
                for g in range(4):
                    MK(f"s{s_}g{g}_start")
                    hl0 = 256 * g
                    for wi, dsrc in enumerate((d_wqdT, d_wkdT, d_wvdT)):
                        E("gpsimd", lambda e, wi=wi, dsrc=dsrc, hl0=hl0: e.dma_start(
                            out=wdt3[:, wi * 2048: (wi + 1) * 2048]
                                .rearrange("p (a n) -> p a n", a=2),
                            in_=dsrc[hl0:hl0 + 256, :].rearrange("(a p) n -> p a n", p=128)),
                          waits=[f_proj_prev, f_dx_x if (s_ == 0 and g == 0) else None,
                                 f_ln1_v, cc("dxw") if g == 0 else None])
                        IL("gpsimd", "dgw", 16)
                    E("gpsimd", lambda e, hl0=hl0, g=g: e.dma_start(
                        out=ballg[:].rearrange("p (a n) -> p a n", a=2),
                        in_=d_ball[hl0:hl0 + 256, 1024 * g:1024 * (g + 1)]
                            .rearrange("(a p) n -> p a n", p=128)))
                    IL("gpsimd", "dgw", 16)
                    f_gdma = cc("dgw")

                    f_effmm = [None] * NCH
                    f_effev = [None] * NCH
                    for chv in range(NCH):
                        cols = ((0, 384, psA), (512, 384, psA), (1024, 256, psA)) if chv % 2 == 0 \
                            else ((1536, 384, psA), (0, 384, psB), (512, 256, psB))
                        for ri, (pb, nn, ps) in enumerate(cols):
                            r0 = (0, 384, 768)[ri]
                            for a in range(2):
                                w_ = []
                                if chv == 0 and ri == 0 and a == 0:
                                    w_ = [f_gdma, f_dx_const, f_exp0_last,
                                          f_exp1_last, f_avnorm_last, f_ln1_v]
                                if chv >= 2 and ri == 0 and a == 0:
                                    w_ = [f_effev[chv - 2]]
                                E("tensor", lambda e, chv=chv, ri=ri, r0=r0, pb=pb, nn=nn,
                                  ps=ps, a=a: e.matmul(
                                    ps[:, pb:pb + nn],
                                    wdt3[:, ri * 2048 + a * 1024 + chv * 128:
                                         ri * 2048 + a * 1024 + chv * 128 + 128],
                                    ballg[:, a * 1024 + r0: a * 1024 + r0 + nn],
                                    start=(a == 0), stop=(a == 1)), waits=w_)
                        f_effmm[chv] = IL("tensor", "t")
                        for ri, (pb, nn, ps) in enumerate(cols):
                            co = (0, 384, 768)[ri]
                            E("vector", lambda e, chv=chv, pb=pb, nn=nn, ps=ps, co=co:
                              e.tensor_copy(eff[:, chv * 1024 + co: chv * 1024 + co + nn],
                                            ps[:, pb:pb + nn]),
                              waits=[f_effmm[chv]] + ([f_proj_prev] if chv == 0 and ri == 0 else []))
                        f_effev[chv] = IL("vector", "v")
                    f_eff_done = f_effev[NCH - 1]

                    dests = ((qcb, 0), (qeb, 1), (erb, 2), (kcb, 3), (keb, 4), (erb, 5))
                    f_pj = {}
                    f_rope_done = None
                    for di, (dst, ci) in enumerate(dests):
                        for tt in range(TT):
                            pb = 512 * (tt % 2)
                            w_ = [f_eff_done] if (di == 0 and tt <= 1) else []
                            prev = f_pj.get((di, tt - 2)) or f_pj.get((di - 1, tt + 2))
                            if prev is not None:
                                w_.append(prev)
                            if di == 5 and tt == 0 and f_rope_done is not None:
                                w_.append(f_rope_done)
                            for chv in range(NCH):
                                E("tensor", lambda e, ci=ci, tt=tt, chv=chv, pb=pb, t0=t0: e.matmul(
                                    psA[:, pb:pb + 512],
                                    eff[:, chv * 1024 + ci * 128: chv * 1024 + ci * 128 + 128],
                                    xsv(chv, t0 + tt * 512, 512),
                                    start=(chv == 0), stop=(chv == NCH - 1)),
                                  waits=w_ if chv == 0 else [])
                            f_mm = IL("tensor", "t")
                            E("vector", lambda e, dst=dst, tt=tt, pb=pb: e.tensor_copy(
                                dst[:, tt * 512:(tt + 1) * 512], psA[:, pb:pb + 512]),
                              waits=[f_mm])
                            f_pj[(di, tt)] = IL("vector", "v")
                        if di == 2:
                            E("vector", lambda e: e.tensor_mul(erb[:, :], erb[:, :], sin_s[:, :]),
                              waits=[f_pj[(di, TT - 1)]])
                            E("vector", lambda e: e.tensor_mul(qeb[:, :], qeb[:, :], cos_s[:, :]))
                            E("vector", lambda e: e.tensor_add(qeb[:, :], qeb[:, :], erb[:, :]))
                            f_rope_done = IL("vector", "v")
                        if di == 5:
                            E("vector", lambda e: e.tensor_mul(erb[:, :], erb[:, :], sin_s[:, :]),
                              waits=[f_pj[(di, TT - 1)]])
                            E("vector", lambda e: e.tensor_mul(keb[:, :], keb[:, :], cos_s[:, :]))
                            E("vector", lambda e: e.tensor_add(keb[:, :], keb[:, :], erb[:, :]))
                            f_ropek = IL("vector", "v")
                    f_qk_done = cc("v")

                    E("vector", lambda e: e.memset(vsb[:, :], 1.0),
                      waits=[f_vsb_free, f_eff_done])
                    f_vsb_ones = IL("vector", "v")
                    f_vp = {}
                    for tc in range(16):
                        pb = 1024 + 512 * (tc % 2)
                        w_ = [f_vsb_ones] if tc == 0 else []
                        if tc >= 2:
                            w_.append(f_vp[tc - 2])
                        for chv in range(NCH):
                            E("tensor", lambda e, tc=tc, chv=chv, pb=pb, t0=t0: e.matmul(
                                psA[:, pb:pb + 256],
                                xsv(chv, t0 + tc * 128, 128),
                                eff[:, chv * 1024 + 768: chv * 1024 + 1024],
                                start=(chv == 0), stop=(chv == NCH - 1)),
                              waits=w_ if chv == 0 else [])
                        f_mm = IL("tensor", "t")
                        E("vector", lambda e, tc=tc, pb=pb: e.tensor_copy(
                            vsb[:, 260 * tc: 260 * tc + 260]
                                .rearrange("p (j d) -> p j d", j=4)[:, :, 0:64],
                            psA[:, pb:pb + 256].rearrange("p (j d) -> p j d", j=4)),
                          waits=[f_mm])
                        f_vp[tc] = IL("vector", "v")
                    f_v_done = cc("v")

                    EV(lambda e: e.tensor_reduce(tiny[:, 0:1], qcb[:, :], AX.X,
                                                 ALU.max, apply_absolute_value=True),
                       waits=[f_qk_done])
                    EV(lambda e: e.tensor_reduce(tiny[:, 1:2], qeb[:, :], AX.X,
                                                 ALU.max, apply_absolute_value=True))
                    EV(lambda e: e.tensor_max(tiny[:, 0:1], tiny[:, 0:1], tiny[:, 1:2]))
                    EV(lambda e: e.tensor_reduce(tiny[:, 2:3], kcb[:, :], AX.X,
                                                 ALU.max, apply_absolute_value=True))
                    EV(lambda e: e.tensor_reduce(tiny[:, 3:4], keb[:, :], AX.X,
                                                 ALU.max, apply_absolute_value=True))
                    EV(lambda e: e.tensor_max(tiny[:, 2:3], tiny[:, 2:3], tiny[:, 3:4]))
                    pmax128(0, 2, 0)
                    pmax128(2, 3, 1)
                    EV(lambda e: e.tensor_mul(rowA[0:1, 4:5], rowA[0:1, 2:3],
                                              rowA[0:1, 3:4]))
                    EV(lambda e: e.tensor_scalar(rowA[0:1, 4:5], rowA[0:1, 4:5],
                                                 -8.0, None, ALU.mult))
                    f_bias_v = EV(lambda e: e.tensor_copy(rowC[0:1, 4:5], rowA[0:1, 4:5]))
                    E("tensor", lambda e: e.matmul(psB[:, 1536:1537], ones_s[0:1, :],
                                                   rowC[0:1, 4:5], start=True, stop=True),
                      waits=[f_bias_v])
                    f_bias_t = IL("tensor", "t")
                    f_bias = EV(lambda e: e.tensor_copy(biasg[:, :], psB[:, 1536:1537]),
                                waits=[f_bias_t])

                    MK(f"s{s_}g{g}_proj")
                    pend = {"qt": None, "f_evac": None}

                    def emit_norm(g=g):
                        qn = pend["qt"]
                        f_ev_ = pend["f_evac"]
                        pend["qt"] = None
                        for j in range(4):
                            if j % 2 == 0:
                                E("tensor", lambda e, j=j: e.matmul(
                                    psB[0:64, 512 * j:512 * j + 512],
                                    ones_s[64:65, 0:64],
                                    zrb[64:65, 512 * j:512 * j + 512],
                                    start=True, stop=True, tile_position=(64, 0)),
                                  waits=[f_ev_] if j == 0 else [])
                            else:
                                E("tensor", lambda e, j=j: e.matmul(
                                    psB[64:128, 512 * j:512 * j + 512],
                                    ones_s[0:1, 0:64],
                                    zrb[0:1, 512 * j:512 * j + 512],
                                    start=True, stop=True, tile_position=(0, 64)))
                        f_outer = IL("tensor", "t")
                        for j in range(4):
                            yc = 2 * g + j // 2
                            pr = 64 * (j % 2)
                            E("vector", lambda e, j=j, yc=yc, pr=pr, qn=qn: e.tensor_mul(
                                ysb[pr:pr + 64, yc * 2048 + qn * 512: yc * 2048 + qn * 512 + 512],
                                ysb[pr:pr + 64, yc * 2048 + qn * 512: yc * 2048 + qn * 512 + 512],
                                psB[pr:pr + 64, 512 * j:512 * j + 512]),
                              waits=[f_outer] if j == 0 else [])
                        return IL("vector", "v")

                    for qt in range(TT):
                        nkt = 4 * qt + 4
                        f_av = {}
                        av_args = {}

                        def emit_av(kt, nkt):
                            a0, f_dep0, f_dep1 = av_args[kt]
                            w_av = [f_dep0]
                            if kt == 0:
                                w_av.append(f_avnorm_last)
                                w_av.append(f_v_done)
                            for j in range(4):
                                w_j = w_av if j == 0 else ([f_dep1] if j == 2 else [])
                                if j % 2 == 0:
                                    E("tensor", lambda e, j=j, kt=kt, a0=a0, nkt=nkt: e.matmul(
                                        psB[0:65, 512 * j:512 * (j + 1)],
                                        vsb[:, 260 * kt + 65 * j: 260 * kt + 65 * j + 65],
                                        attu[:, a0 + 512 * j: a0 + 512 * (j + 1)],
                                        start=(kt == 0), stop=(kt == nkt - 1)),
                                      waits=w_j)
                                else:
                                    E("tensor", lambda e, j=j, kt=kt, a0=a0, nkt=nkt: e.matmul(
                                        psB[64:128, 512 * j:512 * (j + 1)],
                                        vsb[:, 260 * kt + 65 * j: 260 * kt + 65 * j + 64],
                                        attu[:, a0 + 512 * j: a0 + 512 * (j + 1)],
                                        start=(kt == 0), stop=(kt == nkt - 1)))
                                    E("tensor", lambda e, j=j, kt=kt, a0=a0, nkt=nkt: e.matmul(
                                        psB[0:1, 512 * j:512 * (j + 1)],
                                        ones_s[:, 0:1],
                                        attu[:, a0 + 512 * j: a0 + 512 * (j + 1)],
                                        start=(kt == 0), stop=(kt == nkt - 1)))
                            f_av[kt] = IL("tensor", "t")

                        for kt in range(nkt):
                            w_sc0 = [f_exp0_last, f_bias] if kt == 0 else [f_exp0_last]
                            w_sc1 = [f_exp1_last] if kt > 0 else [f_exp1_last, f_bias]
                            if qt == 0 and kt == 0:
                                w_sc0 += [f_qk_done, f_avnorm_last]
                                w_sc1 += [f_qk_done]
                            for half in range(2):
                                js = (0, 1) if half == 0 else (2, 3)
                                w_h = w_sc0 if half == 0 else w_sc1
                                for j in js:
                                    E("tensor", lambda e, j=j, kt=kt, qt=qt: e.matmul(
                                        psA[:, 512 * j:512 * (j + 1)],
                                        kcb[32 * j:32 * (j + 1), kt * 128:(kt + 1) * 128],
                                        qcb[32 * j:32 * (j + 1), qt * 512:(qt + 1) * 512],
                                        start=True, stop=False, tile_position=(32 * j, 0)),
                                      waits=w_h if j == js[0] else [])
                                for j in js:
                                    E("tensor", lambda e, j=j, kt=kt, qt=qt: e.matmul(
                                        psA[:, 512 * j:512 * (j + 1)],
                                        keb[32 * j:32 * (j + 1), kt * 128:(kt + 1) * 128],
                                        qeb[32 * j:32 * (j + 1), qt * 512:(qt + 1) * 512],
                                        start=False, stop=True, tile_position=(32 * j, 0)))
                                if half == 0:
                                    f_sc0 = IL("tensor", "t")
                                else:
                                    f_sc1 = IL("tensor", "t")
                            a0 = (kt % 2) * 2048
                            w_e0 = [f_sc0]
                            w_e1 = [f_sc1]
                            if f_av.get(kt - 2) is not None:
                                w_e0.append(f_av[kt - 2])
                                w_e1.append(f_av[kt - 2])
                            E("scalar", lambda e, a0=a0: e.activation(
                                attu[:, a0:a0 + 1024], psA[:, 0:1024], AFT.Exp,
                                bias=biasg[:, :], scale=SCALE), waits=w_e0)
                            f_exp0_last = IL("scalar", "s")
                            E("scalar", lambda e, a0=a0: e.activation(
                                attu[:, a0 + 1024:a0 + 2048], psA[:, 1024:2048], AFT.Exp,
                                bias=biasg[:, :], scale=SCALE), waits=w_e1)
                            f_exp1_last = IL("scalar", "s")
                            f_dep0, f_dep1 = f_exp0_last, f_exp1_last
                            if kt >= 4 * qt:
                                r = kt - 4 * qt
                                for j in range(4):
                                    E("vector", lambda e, a0=a0, j=j, r=r: e.tensor_mul(
                                        attu[:, a0 + 512 * j: a0 + 512 * (j + 1)],
                                        attu[:, a0 + 512 * j: a0 + 512 * (j + 1)],
                                        dmask_s[:, r * 512:(r + 1) * 512]),
                                      waits=[f_exp0_last] if j == 0 else
                                            ([f_exp1_last] if j == 2 else []))
                                f_dep1 = IL("vector", "v")
                                f_dep0 = f_dep1
                            av_args[kt] = (a0, f_dep0, f_dep1)
                            if kt == 1 and pend["qt"] is not None:
                                f_avnorm_last = emit_norm()
                            if kt >= 1:
                                emit_av(kt - 1, nkt)
                        emit_av(nkt - 1, nkt)
                        f_av_fin = f_av[nkt - 1]

                        for j in range(4):
                            yc = 2 * g + j // 2
                            prow = 64 * (j % 2)
                            if j % 2 == 0:
                                E("vector", lambda e, j=j, yc=yc, qt=qt: e.tensor_copy(
                                    ysb[0:64, yc * 2048 + qt * 512: yc * 2048 + qt * 512 + 512],
                                    psB[0:64, 512 * j:512 * j + 512]),
                                  waits=[f_av_fin] if j == 0 else [])
                                E("vector", lambda e, j=j: _recip_bf(
                                    e, zrb[64:65, 512 * j:512 * j + 512],
                                    psB[64:65, 512 * j:512 * j + 512]))
                            else:
                                E("vector", lambda e, j=j, yc=yc, qt=qt: e.tensor_copy(
                                    ysb[64:128, yc * 2048 + qt * 512: yc * 2048 + qt * 512 + 512],
                                    psB[64:128, 512 * j:512 * j + 512]))
                                E("vector", lambda e, j=j: _recip_bf(
                                    e, zrb[0:1, 512 * j:512 * j + 512],
                                    psB[0:1, 512 * j:512 * j + 512]))
                        f_evac = IL("vector", "v")
                        pend["qt"] = qt
                        pend["f_evac"] = f_evac
                        if qt == TT - 1:
                            f_avnorm_last = emit_norm()
                    f_proj_prev = cc("t")
                    f_vsb_free = cc("t")

                MK(f"s{s_}_attn_end")
                f_groups_done = cc("t")
                f_y_all = f_avnorm_last
                f_wc = {}
                f_wcd = {}
                f_xhw = {}
                xfs3w = [xf[:, 0:512], xf[:, 512:1024],
                         wdt3[:, 4096:6144].bitcast(F32)[:, 0:512]]
                f_wcn = {}
                f_xhn = {}
                for oc in range(NCH):
                    slot = (oc % 2) * 1024
                    E("gpsimd", lambda e, oc=oc, slot=slot: e.dma_start(
                        out=wcs[:, slot:slot + 1024].rearrange("p (a n) -> p a n", a=8),
                        in_=d_wc[:, oc * 128:(oc + 1) * 128]
                            .rearrange("(a p) n -> p a n", p=128)),
                      waits=[f_wc[oc - 2]] if oc >= 2 else
                            ([f_A_end.get("wcs_prev")] if f_A_end.get("wcs_prev") else []))
                    f_wcd[oc] = IL("gpsimd", "dwc", 16)
                    for tt in range(TT):
                        n_ = oc * TT + tt
                        pb = 512 * (tt % 4)
                        parx = n_ % 3
                        parr = n_ % 2
                        dsem = ("dxf0", "dxf1", "dxw")[parx]
                        prev = f_wc.get((oc, tt - 4)) or f_wc.get((oc - 1, tt))
                        prev3 = f_wcn.get(n_ - 3)
                        E("sync", lambda e, oc=oc, tt=tt, parx=parx, t0=t0, s_=s_: e.dma_start(
                            out=xfs3w[parx],
                            in_=d_x[s_, oc * 128:(oc + 1) * 128, tt * 512:(tt + 1) * 512]),
                          waits=[prev3, f_groups_done] if prev3 else [f_groups_done])
                        f_xfd = IL("sync", dsem, 16)
                        w_ = [f_wcd[oc]]
                        if tt == 0 and oc == 0:
                            w_.append(f_y_all)
                        if prev:
                            w_.append(prev)
                        for chv in range(NCH):
                            E("tensor", lambda e, oc=oc, tt=tt, chv=chv, pb=pb, slot=slot:
                              e.matmul(psA[:, pb:pb + 512],
                                       wcs[:, slot + chv * 128: slot + chv * 128 + 128],
                                       ysb[:, chv * 2048 + tt * 512: chv * 2048 + tt * 512 + 512],
                                       start=(chv == 0), stop=(chv == NCH - 1)),
                              waits=w_ if chv == 0 else [])
                        f_mm = IL("tensor", "t")
                        w_stt = [f_mm, f_xfd]
                        pw = f_xhn.get(n_ - 2)
                        if pw:
                            w_stt.append(pw)
                        E("vector", lambda e, parx=parx, parr=parr, pb=pb:
                          e.scalar_tensor_tensor(
                            r1f[:, parr * 512:(parr + 1) * 512], psA[:, pb:pb + 512],
                            onesf[:, :], xfs3w[parx],
                            ALU.mult, ALU.add),
                          waits=w_stt)
                        E("vector", lambda e, oc=oc, tt=tt, parr=parr, t0=t0: e.tensor_copy(
                            xsv(oc, t0 + tt * 512, 512), r1f[:, parr * 512:(parr + 1) * 512]))
                        f_r1v = IL("vector", "v")
                        E("scalar", lambda e, oc=oc, tt=tt, parr=parr, s_=s_: e.dma_start(
                            out=d_xh[s_, oc * 128:(oc + 1) * 128, tt * 512:(tt + 1) * 512],
                            in_=r1f[:, parr * 512:(parr + 1) * 512]), waits=[f_r1v])
                        f_xhw[(oc, tt)] = IL("scalar", "dxh", 16)
                        f_xhn[n_] = f_xhw[(oc, tt)]
                        f_wc[(oc, tt)] = f_r1v
                        f_wcn[n_] = f_r1v
                    f_wc[oc] = f_wc[(oc, TT - 1)]
                f_r1w_all = cc("dxh")
                f_A_end["wcs_prev"] = cc("t")
                f_r1 = cc("v")

                MK(f"s{s_}_wc_end")
                f_sq = {}
                f_st = {}
                for chv in range(NCH):
                    slot = 0
                    w_ = [f_r1] if chv == 0 else [f_st[chv - 1]]
                    E("scalar", lambda e, chv=chv, slot=slot, t0=t0: e.activation(
                        attu[:, slot:slot + 2048], xsv(chv, t0, 2048), AFT.Square), waits=w_)
                    f_sq[chv] = IL("scalar", "s")
                    for tt in range(TT):
                        E("tensor", lambda e, chv=chv, tt=tt, t0=t0: e.matmul(
                            psB[0:1, tt * 512:(tt + 1) * 512], ones_s[:, 0:1],
                            xsv(chv, t0 + tt * 512, 512),
                            start=(chv == 0), stop=(chv == NCH - 1)),
                          waits=[f_sq[chv], f_r1, f_avnorm_last] if (chv == 0 and tt == 0)
                                else [f_sq[chv]] if tt == 0 else [])
                        E("tensor", lambda e, chv=chv, tt=tt, slot=slot: e.matmul(
                            psA[0:1, tt * 512:(tt + 1) * 512], ones_s[:, 0:1],
                            attu[:, slot + tt * 512: slot + (tt + 1) * 512],
                            start=(chv == 0), stop=(chv == NCH - 1)))
                    f_st[chv] = IL("tensor", "t")
                f_stats = f_st[NCH - 1]

                f_ap = {}
                f_xw = {}
                f_apn = {}
                f_xwn = {}
                xfs3 = [xf[:, 0:512], xf[:, 512:1024],
                        wdt3[:, 4096:6144].bitcast(F32)[:, 0:512]]
                for tt in range(TT):
                    E("vector", lambda e, tt=tt: e.tensor_scalar(
                        strow[0:1, 0:512], psB[0:1, tt * 512:(tt + 1) * 512],
                        1.0 / C, None, ALU.mult), waits=[f_stats])
                    E("vector", lambda e, tt=tt: e.tensor_scalar(
                        strow[0:1, 512:1024], psA[0:1, tt * 512:(tt + 1) * 512],
                        1.0 / C, None, ALU.mult))
                    E("vector", lambda e: e.tensor_mul(
                        strow[0:1, 1024:1536], strow[0:1, 0:512], strow[0:1, 0:512]))
                    E("vector", lambda e: e.tensor_sub(
                        strow[0:1, 512:1024], strow[0:1, 512:1024], strow[0:1, 1024:1536]))
                    f_var = IL("vector", "v")
                    E("scalar", lambda e: e.activation(
                        strow[0:1, 512:1024], strow[0:1, 512:1024], AFT.Sqrt,
                        bias=epsb[0:1, 0:1]), waits=[f_var])
                    f_sd = IL("scalar", "s")
                    E("vector", lambda e: e.reciprocal(
                        strow[0:1, 512:1024], strow[0:1, 512:1024]), waits=[f_sd])
                    E("vector", lambda e: e.tensor_mul(
                        strow[0:1, 1024:1536], strow[0:1, 0:512], strow[0:1, 512:1024]))
                    E("vector", lambda e: e.tensor_copy(strow_b[0:1, 0:512],
                                                        strow[0:1, 512:1024]))
                    E("vector", lambda e: e.tensor_copy(strow_b[0:1, 512:1024],
                                                        strow[0:1, 1024:1536]))
                    E("vector", lambda e: e.tensor_copy(strow[0:1, 0:512],
                                                        strow_b[0:1, 0:512]))
                    E("vector", lambda e: e.tensor_sub(strow[0:1, 0:512],
                                                       strow[0:1, 512:1024],
                                                       strow[0:1, 0:512]))
                    E("vector", lambda e: e.tensor_copy(strow_b[0:1, 1024:1536],
                                                        strow[0:1, 0:512]))
                    f_rows = IL("vector", "v")
                    E("tensor", lambda e, tt=tt: e.matmul(
                        psB[:, tt * 512:(tt + 1) * 512], ones_s[0:1, :],
                        strow_b[0:1, 0:512], start=True, stop=False), waits=[f_rows])
                    E("tensor", lambda e, tt=tt: e.matmul(
                        psB[:, tt * 512:(tt + 1) * 512], ones_s[0:1, :],
                        strow_b[0:1, 1024:1536], start=False, stop=True))
                    E("tensor", lambda e, tt=tt: e.matmul(
                        psA[:, tt * 512:(tt + 1) * 512], ones_s[0:1, :],
                        strow_b[0:1, 512:1024], start=True, stop=True))
                    f_bc = IL("tensor", "t")
                    for oc in range(NCH):
                        n_ = tt * NCH + oc
                        parx = n_ % 3
                        parr = oc % 2
                        dsem = ("dxf0", "dxf1", "dxh")[parx]
                        w_d = [f_r1w_all]
                        if (n_ - 3) in f_apn:
                            w_d.append(f_apn[n_ - 3])
                        E("sync", lambda e, oc=oc, tt=tt, parx=parx, s_=s_: e.dma_start(
                            out=xfs3[parx],
                            in_=d_xh[s_, oc * 128:(oc + 1) * 128, tt * 512:(tt + 1) * 512]),
                          waits=w_d)
                        f_xfd = IL("sync", dsem, 16)
                        w_t = [f_xfd, f_bc]
                        pw = f_xwn.get(n_ - 2)
                        if pw:
                            w_t.append(pw)
                        E("vector", lambda e, parx=parx, parr=parr, tt=tt: e.tensor_mul(
                            r1f[:, parr * 512:(parr + 1) * 512],
                            xfs3[parx],
                            psB[:, tt * 512:(tt + 1) * 512]), waits=w_t)
                        E("vector", lambda e, parr=parr, tt=tt: e.tensor_sub(
                            r1f[:, parr * 512:(parr + 1) * 512],
                            r1f[:, parr * 512:(parr + 1) * 512],
                            psA[:, tt * 512:(tt + 1) * 512]))
                        E("vector", lambda e, parr=parr, col=tt * NCH + oc: e.tensor_reduce(
                            zr[:, col:col + 1], r1f[:, parr * 512:(parr + 1) * 512],
                            AX.X, ALU.max, apply_absolute_value=True))
                        f_ap[(tt, oc)] = IL("vector", "v")
                        f_apn[n_] = f_ap[(tt, oc)]
                        E("scalar", lambda e, oc=oc, tt=tt, parr=parr, s_=s_: e.dma_start(
                            out=d_xh[s_, oc * 128:(oc + 1) * 128, tt * 512:(tt + 1) * 512],
                            in_=r1f[:, parr * 512:(parr + 1) * 512]),
                          waits=[f_ap[(tt, oc)]])
                        f_xw[(tt, oc)] = IL("scalar", "dxw", 16)
                        f_xwn[n_] = f_xw[(tt, oc)]
                EV(lambda e: e.tensor_reduce(tiny[:, 4:5], zr[:, 0:32],
                                             AX.X, ALU.max))
                if s_ == 0:
                    EV(lambda e: e.tensor_copy(tiny[:, 5:6], tiny[:, 4:5]))
                else:
                    EV(lambda e: e.tensor_max(tiny[:, 5:6], tiny[:, 5:6],
                                              tiny[:, 4:5]))
                f_ln1_v = cc("v")

            f_xh_all = [cc("dxh"), cc("dxw")]
            if STOP_AFTER == "ln1":
                E("gpsimd", lambda e: e.dma_start(out=d_out[0, :, :], in_=d_xh[0, :, :]),
                  waits=[cc("v"), cc("t"), cc("s")] + f_xh_all)
                IL("gpsimd", "dou0", 16)
                for di_, ap_ in enumerate((ones_s[0:1, 0:128], epsb[:, 0:1],
                                           tiny[:, 4:5], tiny[:, 5:6],
                                           rowA[0:1, 0:256], cbias[:, 0:1],
                                           gmx1[0:1, 0:128], biasg[:, 0:1])):
                    E("gpsimd", lambda e, di_=di_, ap_=ap_: e.dma_start(
                        out=d_out[1, di_ * 128:di_ * 128 + ap_.shape[0], 0:ap_.shape[1]],
                        in_=ap_, allow_slow_non_contiguous=True))
                    IL("gpsimd", "dou0", 16)
                E("sync", lambda e: e.nop(), waits=[cc("dou0")])

            if STOP_AFTER in (None, "h", "c1"):
                MK("ln1_end")
                f_xmax = pmax128(5, 5, 2, waits_v=[f_ln1_v])
                EV(lambda e: e.memset(gmx1[:, :], 0.0), waits=[f_xmax])
                f_gm = EV(lambda e: e.tensor_copy(gmx1[0:1, 0:1], rowA[0:1, 5:6]))
                E("gpsimd", lambda e: e.dma_start(out=d_ar1i[:, :], in_=gmx1[:, :]),
                  waits=[f_gm])
                IL("gpsimd", "dar", 16)
                f_ar_in = cc("dar")
                E("gpsimd", lambda e: e.collective_compute(
                    "AllReduce", ALU.max, replica_groups=[list(range(NCORE))],
                    ins=[d_ar1i[:, :]], outs=[d_ar1o[:, :]]), waits=[f_ar_in])
                IL("gpsimd", "cc")
                f_cc1 = cc("cc")
                E("gpsimd", lambda e: e.dma_start(out=gmx1[:, :], in_=d_ar1o[:, :]),
                  waits=[f_cc1])
                IL("gpsimd", "dar", 16)
                f_gmax1 = cc("dar")

                E("scalar", lambda e: e.activation(rowA[0:1, 6:7], rowA[0:1, 0:1], AFT.Sigmoid),
                  waits=[cc("dcl"), f_gmax1])
                f_sig = IL("scalar", "s")
                EV(lambda e: e.tensor_mul(rowA[0:1, 7:8], rowA[0:1, 6:7],
                                          gmx1[0:1, 0:1]), waits=[f_sig, f_gmax1])
                EV(lambda e: e.tensor_scalar(rowA[0:1, 8:9], rowA[0:1, 7:8],
                                             1.0 / 127.0, None, ALU.mult))
                EV(lambda e: e.reciprocal(rowA[0:1, 9:10], rowA[0:1, 8:9]))
                bcast128(9, rsx_b)
                bcast128(8, sxs_b)
                f_sx = cc("v")

                E("tensor", lambda e: e.nop(), waits=[])
                f_A_t = IL("tensor", "t")
                E("scalar", lambda e: e.copy(tiny[0:1, 13:14], tiny[0:1, 13:14]), waits=[f_sx])
                f_A_s = IL("scalar", "s")
                E("vector", lambda e: e.tensor_copy(tiny[0:1, 12:13], tiny[0:1, 12:13]))
                f_A_v = IL("vector", "v")
                E("gpsimd", lambda e: e.memset(tiny[0:1, 11:12], 0.0))
                f_A_g = IL("gpsimd", "g")
        if STOP_AFTER in (None, "h", "c1"):
            barrier_A = [f_A_t, f_A_s, f_A_v, f_A_g, f_sx]

            with ExitStack() as bctx:
                sbq = lambda name, shape, dt=BF16: bctx.enter_context(nc.sbuf_tensor(name, shape, dt))
                xq = sbq("xq", [128, NCH * 4096])
                hst = sbq("hst", [128, 2 * 4096], F32)
                rndq = bctx.enter_context(nc.sbuf_tensor("rndq", [128, 2 * 2048], I32))
                rndw = bctx.enter_context(nc.sbuf_tensor("rndw", [128, 1024], I32))
                xhq = sbq("xhq", [128, 3 * 2048], F32)
                wgo = sbq("wgo", [128, 1024], F32)
                wuo = sbq("wuo", [128, 1024], F32)
                wgT = sbq("wgT", [128, 2 * 1024])
                wuT = sbq("wuT", [128, 2 * 1024])
                wqi = sbq("wqi", [128, 4 * 1024])
                gbf = sbq("gbf", [128, 3 * 512])

                def xqv(chv, lo, n):
                    return xq[:, chv * 4096 + lo: chv * 4096 + lo + n]

                f_xq = {}
                f_xq_seq = {}
                nq = 0
                for s2 in range(SEQ):
                    for chv in range(NCH):
                        par = nq % 3
                        dsem = ("dxf0", "dxf1", "dxw")[par]
                        w_d = list(barrier_A) + f_xh_all
                        if (nq - 3) in f_xq:
                            w_d.append(f_xq[nq - 3])
                        E("sync", lambda e, chv=chv, s2=s2, par=par: e.dma_start(
                            out=xhq[:, par * 2048:(par + 1) * 2048],
                            in_=d_xh[s2, chv * 128:(chv + 1) * 128, :]), waits=w_d)
                        f_xd = IL("sync", dsem, 16)
                        qpar = nq % 2
                        w_s = [f_xd]
                        if (nq - 2) in f_xq:
                            w_s.append(f_xq[nq - 2])
                        else:
                            w_s += list(barrier_A)
                        E("scalar", lambda e, par=par, qpar=qpar: e.activation(
                            rndq.ap().bitcast(F32)[:, qpar * 2048:(qpar + 1) * 2048],
                            xhq[:, par * 2048:(par + 1) * 2048],
                            AFT.Identity, bias=cbias[:, :], scale=rsx_b[:, :]),
                          waits=w_s)
                        f_rd = IL("scalar", "s")
                        E("vector", lambda e, qpar=qpar: e.tensor_scalar(
                            rndq.ap().bitcast(F32)[:, qpar * 2048:(qpar + 1) * 2048],
                            rndq.ap().bitcast(F32)[:, qpar * 2048:(qpar + 1) * 2048],
                            C_RND + 127.0, C_RND - 128.0, ALU.min, ALU.max), waits=[f_rd])
                        E("vector", lambda e, chv=chv, s2=s2, qpar=qpar: e.tensor_scalar(
                            xqv(chv, s2 * T, T),
                            rndq.ap().bitcast(F32)[:, qpar * 2048:(qpar + 1) * 2048],
                            C_RND, None, ALU.subtract))
                        f_xq[nq] = IL("vector", "v")
                        nq += 1
                    f_xq_seq[s2] = cc("v")
                f_xq_done = cc("v")

                MK("B_xq_end")
                f_wq = {}
                f_int = {}
                f_rnd = {}
                f_trf = {}
                f_sp = {}
                f_hred_last = None

                def emit_chain(ic):
                    for wi, (dsrc, wo, svec) in enumerate(
                            ((d_wg, wgo, sgv), (d_wu, wuo, suv))):
                        w_d = list(barrier_A) if ic == 0 and wi == 0 else []
                        if (ic - 1, wi) in f_rnd:
                            w_d.append(f_rnd[(ic - 1, wi)])
                        E("scalar", lambda e, dsrc=dsrc, wo=wo, ic=ic: e.dma_start(
                            out=wo[:, :], in_=dsrc[ic * 128:(ic + 1) * 128, :]), waits=w_d)
                        f_wd = IL("scalar", "dwg" if wi == 0 else "dwu", 16)
                        w_r = [f_wd]
                        if wi == 1:
                            w_r.append(f_rnd[(ic, 0)])
                        EV(lambda e, wo=wo: e.tensor_reduce(
                            tiny[:, 6:7], wo[:, :], AX.X, ALU.max,
                            apply_absolute_value=True), waits=w_r)
                        EV(lambda e: e.reciprocal(tiny[:, 7:8], tiny[:, 6:7]))
                        EV(lambda e: e.tensor_scalar(tiny[:, 7:8], tiny[:, 7:8],
                                                     127.0, None, ALU.mult))
                        EV(lambda e: e.tensor_scalar(
                            tiny[:, 8:9], tiny[:, 6:7], 1.0 / 127.0, None, ALU.mult))
                        f_rsw = EV(lambda e, svec=svec, ic=ic: e.tensor_mul(
                            svec[:, ic:ic + 1], tiny[:, 8:9], sxs_b[:, :]))
                        E("scalar", lambda e, wo=wo: e.activation(
                            rndw.ap().bitcast(F32), wo[:, :], AFT.Identity,
                            bias=cbias[:, :], scale=tiny[:, 7:8]),
                          waits=[f_rsw, cc("v")])
                        f_rnd[(ic, wi)] = IL("scalar", "s")
                        E("vector", lambda e: e.tensor_scalar(
                            rndw.ap().bitcast(F32), rndw.ap().bitcast(F32),
                            C_RND + 127.0, C_RND - 128.0, ALU.min, ALU.max),
                          waits=[f_rnd[(ic, wi)]])
                        qslot = ((ic % 2) * 2 + wi) * 1024
                        E("vector", lambda e, qslot=qslot: e.tensor_scalar(
                            wqi[:, qslot:qslot + 1024], rndw.ap().bitcast(F32),
                            C_RND, None, ALU.subtract),
                          waits=[f_trf.get((ic - 2, wi))])
                        f_int[(ic, wi)] = IL("vector", "v")

                def emit_tr(ic, wi):
                    if True:
                        wT = wgT if wi == 0 else wuT
                        pb = psAb if wi == 0 else psBb
                        qslot = ((ic % 2) * 2 + wi) * 1024
                        for b in range(8):
                            E("tensor", lambda e, b=b, pb=pb, qslot=qslot: e.transpose(
                                pb[:, 3072 + 128 * b: 3072 + 128 * (b + 1)],
                                wqi[:, qslot + 128 * b: qslot + 128 * (b + 1)],
                                ident_s[:, :]),
                              waits=[f_int[(ic, wi)], f_wq.get((ic - 1, wi))]
                                    if b == 0 else [])
                        f_trf[(ic, wi)] = IL("tensor", "t")
                        slot = (ic % 2) * 1024
                        E("scalar", lambda e, wT=wT, slot=slot, pb=pb: e.copy(
                            wT[:, slot:slot + 1024], pb[:, 3072:4096]),
                          waits=[f_trf[(ic, wi)]])
                        f_wq[(ic, wi)] = IL("scalar", "s")

                def emit_mms(ic):
                    slot = (ic % 2) * 1024
                    hslot = (ic % 2) * 4096
                    for tt8 in range(SEQ * TT):
                        pg = 512 * (tt8 % 3)
                        w_ = []
                        if tt8 == 0:
                            w_ += [f_wq[(ic, 0)], f_wq[(ic, 1)], f_xq_seq[0]]
                            if (ic - 2) in f_sp:
                                w_.append(f_sp[ic - 2])
                        if tt8 == 4:
                            w_.append(f_xq_seq[1])
                        prev = f_wq.get(("evac", ic, tt8 - 3)) or f_wq.get(("evac", ic - 1, tt8 + 5))
                        if prev:
                            w_.append(prev)
                        for b in range(8):
                            E("tensor", lambda e, b=b, tt8=tt8, slot=slot, pg=pg: e.matmul(
                                psA[:, pg:pg + 512],
                                wgT[:, slot + 128 * b: slot + 128 * (b + 1)],
                                xqv(b, tt8 * 512, 512),
                                start=(b == 0), stop=(b == 7)), waits=w_ if b == 0 else [])
                        f_g = IL("tensor", "t")
                        for b in range(8):
                            E("tensor", lambda e, b=b, tt8=tt8, slot=slot, pg=pg: e.matmul(
                                psB[:, pg:pg + 512],
                                wuT[:, slot + 128 * b: slot + 128 * (b + 1)],
                                xqv(b, tt8 * 512, 512),
                                start=(b == 0), stop=(b == 7)))
                        f_u = IL("tensor", "t")
                        gslot = 512 * (tt8 % 3)
                        w_silu = [f_g]
                        if prev:
                            w_silu.append(prev)
                        E("scalar", lambda e, pg=pg, gslot=gslot, ic=ic: e.activation(
                            gbf[:, gslot:gslot + 512], psA[:, pg:pg + 512], AFT.Silu,
                            scale=sgv[:, ic:ic + 1]), waits=w_silu)
                        f_silu = IL("scalar", "s")
                        E("vector", lambda e, pg=pg, gslot=gslot, ic=ic, hslot=hslot, tt8=tt8:
                          e.scalar_tensor_tensor(
                              hst[:, hslot + tt8 * 512: hslot + (tt8 + 1) * 512],
                              psB[:, pg:pg + 512], suv[:, ic:ic + 1],
                              gbf[:, gslot:gslot + 512], ALU.mult, ALU.mult),
                          waits=[f_u, f_silu])
                        E("vector", lambda e, hslot=hslot, tt8=tt8, ic=ic: e.tensor_reduce(
                            hred[:, ic * 8 + tt8: ic * 8 + tt8 + 1],
                            hst[:, hslot + tt8 * 512: hslot + (tt8 + 1) * 512],
                            AX.X, ALU.max, apply_absolute_value=True))
                        f_wq[("evac", ic, tt8)] = IL("vector", "v")
                        if ic + 1 < NIC and tt8 in (3, 6):
                            emit_tr(ic + 1, 0 if tt8 == 3 else 1)
                    f_hr = f_wq[("evac", ic, SEQ * TT - 1)]
                    E("sync", lambda e, ic=ic, hslot=hslot: e.dma_start(
                        out=d_spill[:, :, ic * 512:(ic + 1) * 512].transpose([1, 0, 2]),
                        in_=hst[:, hslot:hslot + 4096].rearrange("p (a n) -> p a n", a=8)),
                      waits=[f_hr])
                    f_sp[ic] = IL("sync", "dsp0" if ic % 2 == 0 else "dsp1", 16)
                    return f_hr

                emit_chain(0)
                emit_tr(0, 0)
                emit_tr(0, 1)
                emit_chain(1)
                for ic in range(NIC):
                    MK(f"B_ic{ic}")
                    if ic + 2 < NIC:
                        emit_chain(ic + 2)
                    f_hred_last = emit_mms(ic)
                f_spill_all = [cc("dsp0"), cc("dsp1")]

                if STOP_AFTER == "h":
                    for a_ in range(6):
                        n_ = 2048 if a_ < 5 else NIC * 512 - 5 * 2048
                        E("gpsimd", lambda e, a_=a_, n_=n_: e.dma_start(
                            out=d_out[0, a_ * 128:(a_ + 1) * 128, 0:n_],
                            in_=d_spill[0, :, a_ * 2048: a_ * 2048 + n_]),
                          waits=([cc("v"), cc("t"), cc("s")] + f_spill_all) if a_ == 0 else [])
                        IL("gpsimd", "dou0", 16)
                    for di_, ap_ in enumerate((gmx1[0:1, 0:128], rowA[0:1, 0:256],
                                               rsx_b[:, 0:1], sxs_b[:, 0:1],
                                               sgv[:, 0:1], suv[:, 0:1],
                                               tiny[:, 5:6], hred[:, 0:1])):
                        E("gpsimd", lambda e, di_=di_, ap_=ap_: e.dma_start(
                            out=d_out[1, di_ * 128:di_ * 128 + ap_.shape[0],
                                      0:ap_.shape[1]], in_=ap_,
                            allow_slow_non_contiguous=True))
                        IL("gpsimd", "dou0", 16)
                    E("sync", lambda e: e.nop(), waits=[cc("dou0")])

                MK("B_mm_end")
                EV(lambda e: e.tensor_reduce(tiny[:, 9:10], hred[:, :], AX.X,
                                             ALU.max), waits=[f_hred_last])
                f_hm = pmax128(9, 12, 3)
                EV(lambda e: e.memset(gmx2[:, :], 0.0), waits=[f_hm])
                f_gm2 = EV(lambda e: e.tensor_copy(gmx2[0:1, 0:1], rowA[0:1, 12:13]))
                E("gpsimd", lambda e: e.dma_start(out=d_ar2i[:, :], in_=gmx2[:, :]),
                  waits=[f_gm2])
                IL("gpsimd", "dar", 16)
                f_ar2_in = cc("dar")
                E("gpsimd", lambda e: e.collective_compute(
                    "AllReduce", ALU.max, replica_groups=[list(range(NCORE))],
                    ins=[d_ar2i[:, :]], outs=[d_ar2o[:, :]]), waits=[f_ar2_in])
                IL("gpsimd", "cc")
                f_cc2 = cc("cc")
                E("gpsimd", lambda e: e.dma_start(out=gmx2[:, :], in_=d_ar2o[:, :]),
                  waits=[f_cc2])
                IL("gpsimd", "dar", 16)
                f_gmax2 = cc("dar")

                E("scalar", lambda e: e.activation(rowA[0:1, 13:14], rowA[0:1, 1:2], AFT.Sigmoid),
                  waits=[f_gmax2])
                f_sig2 = IL("scalar", "s")
                EV(lambda e: e.tensor_mul(rowA[0:1, 14:15], rowA[0:1, 13:14],
                                          gmx2[0:1, 0:1]), waits=[f_sig2, f_gmax2])
                EV(lambda e: e.tensor_scalar(rowA[0:1, 15:16], rowA[0:1, 14:15],
                                             1.0 / 127.0, None, ALU.mult))
                EV(lambda e: e.reciprocal(rowA[0:1, 16:17], rowA[0:1, 15:16]))
                bcast128(16, rsh_b)
                bcast128(15, sh_b)
                f_sh = cc("v")

                E("tensor", lambda e: e.nop(), waits=[])
                f_B_t = IL("tensor", "t")
                E("scalar", lambda e: e.copy(tiny[0:1, 13:14], tiny[0:1, 13:14]), waits=[f_sh])
                f_B_s = IL("scalar", "s")
                E("vector", lambda e: e.tensor_copy(tiny[0:1, 12:13], tiny[0:1, 12:13]))
                f_B_v = IL("vector", "v")
                E("gpsimd", lambda e: e.memset(tiny[0:1, 11:12], 0.0))
                f_B_g = IL("gpsimd", "g")
            barrier_B = [f_B_t, f_B_s, f_B_v, f_B_g, f_sh] + f_spill_all
            SKIP_C = STOP_AFTER == "h"

            if not SKIP_C:
                with ExitStack() as cctx:
                    sbc = lambda name, shape, dt=BF16: cctx.enter_context(nc.sbuf_tensor(name, shape, dt))
                    wdT = sbc("wdT", [128, NIC * 1024])
                    wdo = sbc("wdo", [128, 2 * I_FF], F32)
                    rndc = cctx.enter_context(nc.sbuf_tensor("rndc", [128, I_FF], I32))
                    wdi = sbc("wdi", [128, I_FF])
                    rb = sbc("rb", [128, 2 * 2048], F32)
                    xr = sbc("xr", [128, 2 * 8 * 512], F32)
                    r2f = sbc("r2f", [128, 8 * 512], F32)
                    r2b = sbc("r2b", [128, 2 * 512])
                    hq = sbc("hq", [128, 2 * 512])
                    outs = sbc("outs", [128, 2 * 512], F32)
                    sqc = sbc("sqc", [128, 2 * 512])
                    strowc = sbc("strowc", [1, 2048], F32)
                    strowc_b = sbc("strowc_b", [1, 1024])

                    MK("C_start")
                    rndc2 = rb.ap().bitcast(I32)[:, 0:I_FF]
                    wdi2 = r2f.ap().bitcast(BF16)[:, 0:I_FF]
                    f_dq = {}
                    f_intq = {}

                    def emit_wdchain(hc):
                        slot = (hc % 2) * I_FF
                        rnd_t = rndc.ap().bitcast(F32) if hc % 2 == 0 else rndc2.bitcast(F32)
                        wdi_t = wdi if hc % 2 == 0 else wdi2
                        w_ = list(barrier_B) if hc == 0 else []
                        if ("rnd", hc - 2) in f_dq:
                            w_.append(f_dq[("rnd", hc - 2)])
                        E("sync", lambda e, hc=hc, slot=slot: e.dma_start(
                            out=wdo[:, slot:slot + I_FF], in_=d_wd[hc * 128:(hc + 1) * 128, :]),
                          waits=w_)
                        f_wd = IL("sync", "dwd0" if hc % 2 == 0 else "dwd1", 16)
                        EV(lambda e, slot=slot: e.tensor_reduce(
                            tiny[:, 6:7], wdo[:, slot:slot + I_FF], AX.X, ALU.max,
                            apply_absolute_value=True),
                           waits=[f_wd, f_dq.get(("rnd", hc - 1))] +
                                 (list(barrier_B) if hc == 0 else []))
                        EV(lambda e: e.reciprocal(tiny[:, 7:8], tiny[:, 6:7]))
                        EV(lambda e: e.tensor_scalar(tiny[:, 7:8], tiny[:, 7:8],
                                                     127.0, None, ALU.mult))
                        EV(lambda e: e.tensor_scalar(tiny[:, 8:9], tiny[:, 6:7],
                                                     1.0 / 127.0, None, ALU.mult))
                        f_rsw = EV(lambda e, hc=hc: e.tensor_mul(sdv[:, hc:hc + 1], tiny[:, 8:9],
                                                                 sh_b[:, :]))
                        E("scalar", lambda e, slot=slot, rnd_t=rnd_t: e.activation(
                            rnd_t, wdo[:, slot:slot + I_FF], AFT.Identity,
                            bias=cbias[:, :], scale=tiny[:, 7:8]),
                          waits=[f_rsw] + (list(barrier_B) if hc == 0 else []))
                        f_rd = IL("scalar", "s")
                        f_dq[("rnd", hc)] = f_rd
                        E("vector", lambda e, rnd_t=rnd_t: e.tensor_scalar(
                            rnd_t, rnd_t,
                            C_RND + 127.0, C_RND - 128.0, ALU.min, ALU.max), waits=[f_rd])
                        E("vector", lambda e, rnd_t=rnd_t, wdi_t=wdi_t: e.tensor_scalar(
                            wdi_t[:, 0:I_FF], rnd_t, C_RND, None, ALU.subtract))
                        f_intq[hc] = IL("vector", "v")

                    def emit_wdtr(hc):
                        wdi_t = wdi if hc % 2 == 0 else wdi2
                        for b8 in range(3):
                            nb = 8 if b8 < 2 else 6
                            for u in range(nb):
                                icx = b8 * 8 + u
                                E("tensor", lambda e, icx=icx, u=u, wdi_t=wdi_t: e.transpose(
                                    psBb[:, 3072 + 128 * u: 3072 + 128 * (u + 1)],
                                    wdi_t[:, 128 * icx:128 * (icx + 1)], ident_s[:, :]),
                                  waits=([f_intq[hc]] + (list(barrier_B) if hc == 0 else []) +
                                         ([f_dq[hc - 1]] if (hc - 1) in f_dq else []))
                                        if (b8 == 0 and u == 0) else
                                        ([f_dq[("ev", hc, b8 - 1)]] if u == 0 else []))
                            f_tr = IL("tensor", "t")
                            E("vector", lambda e, hc=hc, b8=b8, nb=nb: e.tensor_copy(
                                wdT[:, :].rearrange("p (i n) -> p i n", i=NIC)
                                   [:, b8 * 8: b8 * 8 + nb, 128 * hc:128 * (hc + 1)],
                                psBb[:, 3072:3072 + nb * 128].rearrange("p (i n) -> p i n", i=nb)),
                              waits=[f_tr])
                            f_dq[("ev", hc, b8)] = IL("vector", "v")
                        f_dq[hc] = f_dq[("ev", hc, 2)]

                    emit_wdchain(0)
                    for hc in range(NCH):
                        if hc + 1 < NCH:
                            emit_wdchain(hc + 1)
                        emit_wdtr(hc)
                    f_wdT_done = f_dq[NCH - 1]

                    if HAVE_DBG:
                        EV(lambda e: e.tensor_reduce(tiny[:, 12:13],
                                                     wdT[:, 0:2048].bitcast(F32),
                                                     AX.X, ALU.max,
                                                     apply_absolute_value=True),
                           waits=[f_wdT_done])
                        for di_, ap_ in enumerate((rowA[0:1, 0:256], gmx2[0:1, 0:128],
                                                   sdv[0:1, 0:8], rsh_b[0:1, 0:1],
                                                   sh_b[0:1, 0:1], tiny[0:1, 12:13])):
                            E("gpsimd", lambda e, di_=di_, ap_=ap_: e.dma_start(
                                out=d_dbg[di_:di_ + 1, 0:ap_.shape[1]],
                                in_=ap_, allow_slow_non_contiguous=True),
                              waits=[cc("v"), cc("s"), cc("t")])
                            IL("gpsimd", "dou0", 16)

                    f_ev = {}
                    f_rbd = {}
                    f_ln2 = {}
                    dbg_fc = {}
                    f_r2s = {}
                    f_vars = {}
                    _ord = list(DOWN_ORDER or range(SEQ * TT))
                    for tt8i, tt8 in enumerate(_ord):
                        MK(f"C_tt8_{tt8i}")
                        p_tt8 = _ord[tt8i - 1] if tt8i > 0 else None
                        s_ = tt8 // TT
                        tt = tt8 % TT
                        t0 = s_ * T

                        def emit_rb(rs_, tt8=tt8):
                            n = 2048 if rs_ < 5 else 1024
                            slot = (rs_ % 2) * 2048
                            w_ = list(f_spill_all)
                            if p_tt8 is None:
                                w_.append(f_wdT_done)
                            if rs_ in (2, 3) and (tt8, "rb") in dbg_fc:
                                w_.append(dbg_fc[(tt8, "rb")])
                            if rs_ >= 2:
                                w_.append(f_ev[(tt8, rs_ * 4 - 5)])
                            if p_tt8 is not None:
                                w_.append(f_ev.get((p_tt8, 19 if rs_ == 0 else NIC - 1)))
                            E("sync", lambda e, tt8=tt8, rs_=rs_, n=n, slot=slot: e.dma_start(
                                out=rb[:, slot:slot + n],
                                in_=d_spill[tt8, :, rs_ * 2048: rs_ * 2048 + n]), waits=w_)
                            f_rbd[(tt8, rs_)] = IL("sync", "drb0" if rs_ % 2 == 0 else "drb1", 16)

                        xpar = tt8i % 2
                        dsemx = "dxf0" if xpar == 0 else "dxf1"
                        w_xr = list(f_xh_all)
                        if tt8i >= 2:
                            w_xr.append(f_r2s[tt8i - 2])
                            if tt8i == 2 and (_ord[0], "xr") in dbg_fc:
                                w_xr.append(dbg_fc[(_ord[0], "xr")])
                        E("sync", lambda e, s_=s_, tt=tt, xpar=xpar: e.dma_start(
                            out=xr[:, xpar * 4096:(xpar + 1) * 4096]
                                .rearrange("p (a n) -> p a n", a=8),
                            in_=d_xh[s_, :, tt * 512:(tt + 1) * 512]
                                .rearrange("(a p) n -> p a n", p=128)), waits=w_xr)
                        f_xrd = IL("sync", dsemx, 16)
                        emit_rb(0)
                        emit_rb(1)
                        if HAVE_DBG and tt8i == 0:
                            E("gpsimd", lambda e: e.dma_start(
                                out=d_dbg2[:, 0:2048], in_=rb[:, 0:2048]),
                              waits=[f_rbd[(tt8, 0)]])
                            dbg_fc[(tt8, "rb")] = IL("gpsimd", "dgw", 16)
                            E("gpsimd", lambda e: e.dma_start(
                                out=d_dbg2[:, 3072:7168], in_=xr[:, 0:4096]),
                              waits=[f_xrd])
                            dbg_fc[(tt8, "xr")] = IL("gpsimd", "dgw", 16)
                        for ic in range(NIC):
                            if ic >= 4 and (ic - 4) % 4 == 0 and (ic - 4) // 4 + 2 < 6 \
                                    and (tt8, (ic - 4) // 4 + 2) not in f_rbd:
                                emit_rb((ic - 4) // 4 + 2)
                            rs_ = (ic * 512) // 2048
                            off = (ic * 512) % 2048 + ((rs_ % 2) * 2048)
                            hslot = (ic % 2) * 512
                            qs = hslot
                            E("scalar", lambda e, off=off, qs=qs: e.activation(
                                rndc.ap().bitcast(F32)[:, qs:qs + 512], rb[:, off:off + 512],
                                AFT.Identity, bias=cbias[:, :], scale=rsh_b[:, :]),
                              waits=[f_rbd[(tt8, rs_)], f_wdT_done] +
                                    ([f_ev[(tt8, ic - 2)]] if (tt8, ic - 2) in f_ev else []) +
                                    ([f_ev.get((p_tt8, NIC - 1))] if (ic < 2 and p_tt8 is not None)
                                     else []) +
                                    ([dbg_fc[(tt8, "hq")]] if (ic == 2 and (tt8, "hq") in dbg_fc)
                                     else []))
                            f_rd = IL("scalar", "s")
                            E("vector", lambda e, qs=qs: e.tensor_scalar(
                                rndc.ap().bitcast(F32)[:, qs:qs + 512],
                                rndc.ap().bitcast(F32)[:, qs:qs + 512],
                                C_RND + 127.0, C_RND - 128.0, ALU.min, ALU.max), waits=[f_rd])
                            E("vector", lambda e, hslot=hslot, qs=qs: e.tensor_scalar(
                                hq[:, hslot:hslot + 512], rndc.ap().bitcast(F32)[:, qs:qs + 512],
                                C_RND, None, ALU.subtract))
                            f_hq = IL("vector", "v")
                            if HAVE_DBG and tt8i == 0 and ic == 0:
                                E("gpsimd", lambda e: e.dma_start(
                                    out=d_dbg2[:, 2048:2560], in_=hq[:, 0:512]),
                                  waits=[f_hq])
                                dbg_fc[(tt8, "hq")] = IL("gpsimd", "dgw", 16)
                                E("gpsimd", lambda e: e.dma_start(
                                    out=d_dbg2[:, 2560:2568], in_=sdv[:, 0:8]),
                                  waits=[f_hq])
                                IL("gpsimd", "dgw", 16)
                            hc_order = (4, 5, 6, 7, 0, 1, 2, 3) if ic == 0 else range(NCH)
                            for hi, hc in enumerate(hc_order):
                                ps = psA if hc < 4 else psB
                                pb = 512 * (hc % 4)
                                w_ = [f_hq] if hi == 0 else []
                                if ic == 0 and p_tt8 is not None:
                                    if hc == 4:
                                        w_.append(f_r2s[tt8i - 1])
                                    elif hc == 0:
                                        w_.append(f_vars[tt8i - 1])
                                    elif hc == 2:
                                        w_.append(f_ln2[("apl", p_tt8)])
                                E("tensor", lambda e, ic=ic, hc=hc, ps=ps, pb=pb, hslot=hslot:
                                  e.matmul(ps[:, pb:pb + 512],
                                           wdT[:, ic * 1024 + hc * 128: ic * 1024 + hc * 128 + 128],
                                           hq[:, hslot:hslot + 512],
                                           start=(ic == 0), stop=(ic == NIC - 1)),
                                  waits=w_)
                            f_ev[(tt8, ic)] = IL("tensor", "t")
                        f_dn = f_ev[(tt8, NIC - 1)]
                        for hc in range(NCH):
                            ps = psA if hc < 4 else psB
                            pb = 512 * (hc % 4)
                            w_e = [f_dn, f_xrd] if hc == 0 else []
                            if hc == 0 and p_tt8 is not None:
                                w_e.append(f_ln2[("apl", p_tt8)])
                                if (p_tt8, "r2f") in dbg_fc:
                                    w_e.append(dbg_fc[(p_tt8, "r2f")])
                            E("vector", lambda e, hc=hc, ps=ps, pb=pb, xpar=xpar:
                              e.scalar_tensor_tensor(
                                  r2f[:, hc * 512:(hc + 1) * 512], ps[:, pb:pb + 512],
                                  sdv[:, hc:hc + 1],
                                  xr[:, xpar * 4096 + hc * 512: xpar * 4096 + (hc + 1) * 512],
                                  ALU.mult, ALU.add), waits=w_e)
                        f_r2 = IL("vector", "v")
                        f_r2s[tt8i] = f_r2
                        if HAVE_DBG and tt8i == 0:
                            E("gpsimd", lambda e: e.dma_start(
                                out=d_dbg2[:, 7168:11264], in_=r2f[:, 0:4096]),
                              waits=[f_r2])
                            dbg_fc[(tt8, "r2f")] = IL("gpsimd", "dgw", 16)

                        f_sq2 = {}
                        for chv in range(NCH):
                            slot = (chv % 2) * 512
                            E("vector", lambda e, chv=chv, slot=slot: e.tensor_copy(
                                r2b[:, slot:slot + 512], r2f[:, chv * 512:(chv + 1) * 512]),
                              waits=([f_sq2[chv - 2]] if chv >= 2 else []))
                            f_cp = IL("vector", "v")
                            E("scalar", lambda e, slot=slot: e.activation(
                                sqc[:, slot:slot + 512], r2b[:, slot:slot + 512], AFT.Square),
                              waits=[f_cp])
                            f_sqs = IL("scalar", "s")
                            E("tensor", lambda e, chv=chv, slot=slot: e.matmul(
                                psA[0:1, 0:512], ones_s[:, 0:1], r2b[:, slot:slot + 512],
                                start=(chv == 0), stop=(chv == NCH - 1)),
                              waits=[f_cp, f_sqs])
                            E("tensor", lambda e, chv=chv, slot=slot: e.matmul(
                                psA[0:1, 512:1024], ones_s[:, 0:1], sqc[:, slot:slot + 512],
                                start=(chv == 0), stop=(chv == NCH - 1)))
                            f_sq2[chv] = IL("tensor", "t")
                        f_st2 = f_sq2[NCH - 1]
                        E("vector", lambda e: e.tensor_scalar(
                            strowc[0:1, 0:512], psA[0:1, 0:512], 1.0 / C, None, ALU.mult),
                          waits=[f_st2] + ([dbg_fc[(p_tt8, "strow")]]
                                           if (p_tt8, "strow") in dbg_fc else []))
                        E("vector", lambda e: e.tensor_scalar(
                            strowc[0:1, 512:1024], psA[0:1, 512:1024], 1.0 / C, None, ALU.mult))
                        E("vector", lambda e: e.tensor_mul(
                            strowc[0:1, 1024:1536], strowc[0:1, 0:512], strowc[0:1, 0:512]))
                        E("vector", lambda e: e.tensor_sub(
                            strowc[0:1, 512:1024], strowc[0:1, 512:1024], strowc[0:1, 1024:1536]))
                        f_var = IL("vector", "v")
                        f_vars[tt8i] = f_var
                        E("scalar", lambda e: e.activation(
                            strowc[0:1, 512:1024], strowc[0:1, 512:1024], AFT.Sqrt,
                            bias=epsb[0:1, 0:1]), waits=[f_var])
                        f_sd2 = IL("scalar", "s")
                        E("vector", lambda e: e.reciprocal(
                            strowc[0:1, 512:1024], strowc[0:1, 512:1024]), waits=[f_sd2])
                        E("vector", lambda e: e.tensor_mul(
                            strowc[0:1, 1024:1536], strowc[0:1, 0:512], strowc[0:1, 512:1024]))
                        E("vector", lambda e: e.tensor_copy(strowc_b[0:1, 0:512],
                                                            strowc[0:1, 512:1024]))
                        E("vector", lambda e: e.tensor_copy(strowc_b[0:1, 512:1024],
                                                            strowc[0:1, 1024:1536]))
                        f_rows2 = IL("vector", "v")
                        if HAVE_DBG and tt8i == 0:
                            E("gpsimd", lambda e: e.dma_start(
                                out=d_dbg2[0:1, 11264:13312], in_=strowc[0:1, 0:2048]),
                              waits=[f_rows2])
                            dbg_fc[(tt8, "strow")] = IL("gpsimd", "dgw", 16)
                        E("tensor", lambda e: e.matmul(
                            psA[:, 1024:1536], ones_s[0:1, :], strowc_b[0:1, 0:512],
                            start=True, stop=True), waits=[f_rows2])
                        E("tensor", lambda e: e.matmul(
                            psA[:, 1536:2048], ones_s[0:1, :], strowc_b[0:1, 512:1024],
                            start=True, stop=True))
                        f_bc2 = IL("tensor", "t")
                        for chv in range(NCH):
                            oslot = (chv % 2) * 512
                            w_o = []
                            if chv == 0:
                                w_o = [f_bc2, f_ln2.get(("od", p_tt8, 6))]
                            elif chv == 1:
                                w_o = [f_ln2.get(("od", p_tt8, 7))]
                            else:
                                w_o = [f_ln2.get(("od", tt8, chv - 2))]
                                if chv == 2 and (tt8, "outs") in dbg_fc:
                                    w_o.append(dbg_fc[(tt8, "outs")])
                            E("vector", lambda e, chv=chv, oslot=oslot: e.tensor_mul(
                                outs[:, oslot:oslot + 512], r2f[:, chv * 512:(chv + 1) * 512],
                                psA[:, 1024:1536]), waits=w_o)
                            E("vector", lambda e, chv=chv, oslot=oslot: e.tensor_sub(
                                outs[:, oslot:oslot + 512], outs[:, oslot:oslot + 512],
                                psA[:, 1536:2048]))
                            f_o = IL("vector", "v")
                            if HAVE_DBG and tt8i == 0 and chv == 0:
                                E("gpsimd", lambda e: e.dma_start(
                                    out=d_dbg2[:, 13312:13824], in_=outs[:, 0:512]),
                                  waits=[f_o])
                                dbg_fc[(tt8, "outs")] = IL("gpsimd", "dgw", 16)
                            E("sync", lambda e, chv=chv, oslot=oslot, s_=s_, tt=tt: e.dma_start(
                                out=d_out[s_, chv * 128:(chv + 1) * 128, tt * 512:(tt + 1) * 512],
                                in_=outs[:, oslot:oslot + 512]), waits=[f_o])
                            f_ln2[("od", tt8, chv)] = IL(
                                "sync", "dou0" if chv % 2 == 0 else "dou1", 16)
                        f_ln2[("apl", tt8)] = cc("v")
                        for hc in range(NCH):
                            f_ln2[(tt8, hc)] = cc("v")

                    MK("C_end")
                    E("sync", lambda e: e.nop(),
                      waits=[cc("dou0"), cc("dou1")] +
                            ([cc("dgw")] if HAVE_DBG else []))

        with nc.Block() as block:
            def mk(eng_name):
                def body(eng):
                    hw = {}
                    for waits, fn, inc_sem, inc_k in S.ops[eng_name]:
                        for s_n, v_ in waits:
                            if hw.get(s_n, -1) >= v_:
                                continue
                            eng.wait_ge(sems[s_n], v_)
                            hw[s_n] = v_
                        inst = fn(eng)
                        if inc_sem is not None:
                            inst.then_inc(sems[inc_sem], inc_k)
                return body
            block.sync(mk("sync"))
            block.tensor(mk("tensor"))
            block.vector(mk("vector"))
            block.scalar(mk("scalar"))
            block.gpsimd(mk("gpsimd"))
    return nc


def _host_prepare(inputs):
    bf16 = ml_dtypes.bfloat16
    x = np.asarray(inputs["L0_attn_x"], dtype=np.float32)
    wqd = np.asarray(inputs["L0_attn_Wq_down"], dtype=np.float32)
    wkd = np.asarray(inputs["L0_attn_Wk_down"], dtype=np.float32)
    wvd = np.asarray(inputs["L0_attn_Wv_down"], dtype=np.float32)
    wquc = np.asarray(inputs["L0_attn_Wq_up_c"], dtype=np.float32)
    wque = np.asarray(inputs["L0_attn_Wq_up_e"], dtype=np.float32)
    wkuc = np.asarray(inputs["L0_attn_Wk_up_c"], dtype=np.float32)
    wkue = np.asarray(inputs["L0_attn_Wk_up_e"], dtype=np.float32)
    wvu = np.asarray(inputs["L0_attn_Wv_up"], dtype=np.float32)
    wc = np.asarray(inputs["L0_attn_Wc"], dtype=np.float32)
    clipx = np.asarray(inputs["L0_ffn_clip_x"], dtype=np.float32).reshape(1, 1)
    clipd = np.asarray(inputs["L0_ffn_clip_down"], dtype=np.float32).reshape(1, 1)
    wg = np.asarray(inputs["L0_ffn_w_gate"], dtype=np.float32)
    wu = np.asarray(inputs["L0_ffn_w_up"], dtype=np.float32)
    wd = np.asarray(inputs["L0_ffn_w_down"], dtype=np.float32)

    ball = np.zeros((C, 4096), dtype=np.float32)
    perm = np.arange(32)
    perm = perm + 1 - 2 * (perm % 2)
    for g in range(4):
        for j in range(4):
            h = 4 * g + j
            r0 = 256 * g + 64 * j
            c0 = 1024 * g
            ball[r0:r0 + 64, c0 + 32 * j:c0 + 32 * j + 32] = wquc
            ball[r0:r0 + 64, c0 + 128 + 32 * j:c0 + 128 + 32 * j + 32] = wque
            ball[r0:r0 + 64, c0 + 256 + 32 * j:c0 + 256 + 32 * j + 32] = wque[:, perm]
            ball[r0:r0 + 64, c0 + 384 + 32 * j:c0 + 384 + 32 * j + 32] = wkuc
            ball[r0:r0 + 64, c0 + 512 + 32 * j:c0 + 512 + 32 * j + 32] = wkue
            ball[r0:r0 + 64, c0 + 640 + 32 * j:c0 + 640 + 32 * j + 32] = wkue[:, perm]
            ball[r0:r0 + 64, c0 + 768 + 64 * j:c0 + 768 + 64 * j + 64] = wvu

    tpos = np.arange(T, dtype=np.float64)
    invf = 1.0 / (10000.0 ** (np.arange(0, 32, 2, dtype=np.float64) / 32.0))
    ang = np.outer(tpos, invf)
    co = np.cos(ang); si = np.sin(ang)
    cos32 = np.concatenate([co, co], axis=1).T
    sin32 = np.concatenate([si, si], axis=1).T
    sgn = np.where(np.arange(32) % 2 == 0, -1.0, 1.0)
    sin32 = sin32 * sgn[:, None]
    cos_t = np.ascontiguousarray(np.tile(cos32, (4, 1))).astype(bf16)
    sin_t = np.ascontiguousarray(np.tile(sin32, (4, 1))).astype(bf16)

    dmask = np.zeros((128, 4 * 512), dtype=np.float32)
    p = np.arange(128)[:, None]
    f = np.arange(512)[None, :]
    for r in range(4):
        dmask[:, 512 * r:512 * (r + 1)] = (128 * r + p <= f).astype(np.float32)
    dmask = dmask.astype(bf16)
    ident = np.eye(128, dtype=np.float32).astype(bf16)

    shared = {
        "wqdT": np.ascontiguousarray(wqd.T), "wkdT": np.ascontiguousarray(wkd.T),
        "wvdT": np.ascontiguousarray(wvd.T), "ball": ball, "wc": wc,
        "wg": wg, "wu": wu, "wd": wd, "cos_t": cos_t, "sin_t": sin_t,
        "dmask": dmask, "ident": ident, "clipx": clipx, "clipd": clipd,
    }
    in_maps = []
    for cid in range(NCORE):
        xc = np.ascontiguousarray(np.transpose(x[2 * cid:2 * cid + 2], (0, 2, 1)))
        m = dict(shared)
        m["x"] = xc
        in_maps.append(m)
    return in_maps


def kernel(**inputs):
    in_maps = _host_prepare(inputs)
    nc = build_core()
    res = run_bass_kernel_spmd(nc, in_maps, core_ids=list(range(NCORE)))
    out = np.empty((B, T, C), dtype=np.float32)
    for cid in range(NCORE):
        o = np.asarray(res.results[cid]["out"])
        out[2 * cid:2 * cid + 2] = np.transpose(o, (0, 2, 1))
    return out


# version 14
# speedup vs baseline: 1.0875x; 1.0145x over previous
import numpy as np
import ml_dtypes
import concourse.bass as bass
import concourse.mybir as mybir
from concourse.bass_utils import run_bass_kernel_spmd
from contextlib import ExitStack

F32 = mybir.dt.float32
BF16 = mybir.dt.bfloat16
I32 = mybir.dt.int32
ALU = mybir.AluOpType
AFT = mybir.ActivationFunctionType
AX = mybir.AxisListType

B, T, C = 16, 2048, 1024
I_FF = 2816
NCORE = 8
SEQ = 2
TT = 4
NCH = 8
NIC = 22
SCALE = 0.125
C_RND = float(np.float32(3 * 2.0 ** 22))
EPS = 1e-5

STOP_AFTER = None
DOWN_ORDER = None
MARKS = []

ENGS = ("sync", "tensor", "vector", "scalar", "gpsimd")
SEMS = ("t", "v", "s", "g", "cc", "dc", "dxx", "dcl", "dgw", "dwc", "dar",
        "dwg", "dwu", "dwd0", "dwd1", "dsp0", "dsp1", "drb0", "drb1",
        "dou0", "dou1", "dxf0", "dxf1", "dxh", "dxw")
SEM_ENG = {"t": "tensor", "v": "vector", "s": "scalar", "g": "gpsimd"}


class Sched:
    def __init__(self):
        self.ops = {e: [] for e in ENGS}
        self.cnt = {s: 0 for s in SEMS}

    def emit(self, eng, fn, waits=()):
        w = {}
        for f in waits:
            if f is None:
                continue
            s, vv = f
            if vv <= 0:
                continue
            w[s] = max(w.get(s, 0), vv)
        self.ops[eng].append([tuple(w.items()), fn, None, 0])

    def inc_last(self, eng, sem, k=1):
        op = self.ops[eng][-1]
        assert op[2] is None
        op[2] = sem
        op[3] = k
        self.cnt[sem] += k
        return (sem, self.cnt[sem])

    def c(self, sem):
        return (sem, self.cnt[sem])


def build_core():
    nc = bass.Bass("TRN2", target_bir_lowering=False, debug=False,
                   detect_race_conditions=False)
    S = Sched()
    E = S.emit
    IL = S.inc_last
    cc = S.c
    MARKS.clear()
    MK = lambda lbl: MARKS.append((lbl, S.cnt["t"], S.cnt["v"], S.cnt["s"]))

    d_x = nc.dram_tensor("x", [SEQ, C, T], F32, kind="ExternalInput").ap()
    d_wqdT = nc.dram_tensor("wqdT", [C, C], F32, kind="ExternalInput").ap()
    d_wkdT = nc.dram_tensor("wkdT", [C, C], F32, kind="ExternalInput").ap()
    d_wvdT = nc.dram_tensor("wvdT", [C, C], F32, kind="ExternalInput").ap()
    d_ball = nc.dram_tensor("ball", [C, 4096], F32, kind="ExternalInput").ap()
    d_wc = nc.dram_tensor("wc", [C, C], F32, kind="ExternalInput").ap()
    d_wg = nc.dram_tensor("wg", [I_FF, C], F32, kind="ExternalInput").ap()
    d_wu = nc.dram_tensor("wu", [I_FF, C], F32, kind="ExternalInput").ap()
    d_wd = nc.dram_tensor("wd", [C, I_FF], F32, kind="ExternalInput").ap()
    d_cos = nc.dram_tensor("cos_t", [128, T], BF16, kind="ExternalInput").ap()
    d_sin = nc.dram_tensor("sin_t", [128, T], BF16, kind="ExternalInput").ap()
    d_dmask = nc.dram_tensor("dmask", [128, 4 * 512], BF16, kind="ExternalInput").ap()
    d_ident = nc.dram_tensor("ident", [128, 128], BF16, kind="ExternalInput").ap()
    d_clipx = nc.dram_tensor("clipx", [1, 1], F32, kind="ExternalInput").ap()
    d_clipd = nc.dram_tensor("clipd", [1, 1], F32, kind="ExternalInput").ap()
    d_out = nc.dram_tensor("out", [SEQ, C, T], F32, kind="ExternalOutput").ap()
    d_dbg = nc.dram_tensor("dbg", [16, 256], F32, kind="ExternalOutput").ap()
    HAVE_DBG = STOP_AFTER == "c1"
    if HAVE_DBG:
        d_dbg2 = nc.dram_tensor("dbg2", [128, 16384], F32, kind="ExternalOutput").ap()

    d_spill = nc.dram_tensor("h_spill", [SEQ * TT, 128, NIC * 512], F32)
    d_xh = nc.dram_tensor("xh", [SEQ, C, T], F32).ap()
    d_ar1i = nc.dram_tensor("ar1i", [1, 128], F32)
    d_ar1o = nc.dram_tensor("ar1o", [1, 128], F32)
    d_ar2i = nc.dram_tensor("ar2i", [1, 128], F32)
    d_ar2o = nc.dram_tensor("ar2o", [1, 128], F32)

    ctx = ExitStack()
    with ctx:
        sbt = lambda name, shape, dt=BF16: ctx.enter_context(nc.sbuf_tensor(name, shape, dt))
        cos_s = sbt("cos_s", [128, T])
        sin_s = sbt("sin_s", [128, T])
        dmask_s = sbt("dmask_s", [128, 4 * 512])
        ident_s = sbt("ident_s", [128, 128])
        ones_s = sbt("ones_s", [128, 128])
        onesf = sbt("onesf", [128, 1], F32)
        cbias = sbt("cbias", [128, 1], F32)
        epsb = sbt("epsb", [128, 1], F32)
        rsx_b = sbt("rsx_b", [128, 1], F32)
        rsh_b = sbt("rsh_b", [128, 1], F32)
        sh_b = sbt("sh_b", [128, 1], F32)
        sxs_b = sbt("sxs_b", [128, 1], F32)
        biasg = sbt("biasg", [128, 1], F32)
        tiny = sbt("tiny", [128, 16], F32)
        mbt = sbt("mbt", [128, 4])
        rowA = sbt("rowA", [1, 256], F32)
        rowC = sbt("rowC", [1, 16])
        gmx1 = sbt("gmx1", [1, 128], F32)
        gmx2 = sbt("gmx2", [1, 128], F32)
        sdv = sbt("sdv", [128, 8], F32)
        sgv = sbt("sgv", [128, NIC], F32)
        suv = sbt("suv", [128, NIC], F32)
        hred = sbt("hred", [128, 8 * NIC], F32)

        psA = ctx.enter_context(nc.psum_tensor("psA", [128, 2048], F32))
        psB = ctx.enter_context(nc.psum_tensor("psB", [128, 2048], F32))
        psAb = psA.ap().bitcast(BF16)
        psBb = psB.ap().bitcast(BF16)
        psAb = psA.ap().bitcast(BF16)

        sems = {}
        for s_ in SEMS:
            sems[s_] = ctx.enter_context(nc.semaphore("sem_" + s_))

        def xsv(chv, lo, n):
            return xs[:, chv * 4096 + lo: chv * 4096 + lo + n]

        E("gpsimd", lambda e: e.memset(ones_s[:, :], 1.0))
        E("gpsimd", lambda e: e.memset(onesf[:, :], 1.0))
        E("gpsimd", lambda e: e.memset(cbias[:, :], C_RND))
        E("gpsimd", lambda e: e.memset(epsb[:, :], EPS))
        E("gpsimd", lambda e: e.memset(rowA[:, :], 1.0))
        E("gpsimd", lambda e: e.memset(tiny[:, :], 0.0))
        IL("gpsimd", "g")
        g_setup = cc("g")
        E("sync", lambda e: e.dma_start(out=cos_s[:, :], in_=d_cos[:, :]))
        IL("sync", "dc", 16)
        E("sync", lambda e: e.dma_start(out=sin_s[:, :], in_=d_sin[:, :]))
        IL("sync", "dc", 16)
        E("sync", lambda e: e.dma_start(out=dmask_s[:, :], in_=d_dmask[:, :]))
        IL("sync", "dc", 16)
        E("sync", lambda e: e.dma_start(out=ident_s[:, :], in_=d_ident[:, :]))
        IL("sync", "dc", 16)
        E("gpsimd", lambda e: e.dma_start(out=rowA[0:1, 0:1], in_=d_clipx[:, :]), waits=[g_setup])
        IL("gpsimd", "dcl", 16)
        E("gpsimd", lambda e: e.dma_start(out=rowA[0:1, 1:2], in_=d_clipd[:, :]))
        IL("gpsimd", "dcl", 16)
        for s_ in range(SEQ):
            for chv in range(NCH):
                E("gpsimd", lambda e, s_=s_, chv=chv: e.dma_start(
                    out=xsv(chv, s_ * T, T),
                    in_=d_x[s_, chv * 128:(chv + 1) * 128, :]))
                IL("gpsimd", "dxx", 16)
        f_dx_x = cc("dxx")
        f_dx_const = cc("dc")

        def _recip_bf(e, out, in_):
            with nc.allow_low_precision(reason="1/rowsum to bf16; same rounding as fp32 recip + bf16 copy"):
                return e.reciprocal(out, in_)

        def EV(fn, waits=()):
            E("vector", fn, waits=list(waits) + [cc("v")])
            return IL("vector", "v")

        def bcast128(col, dst, waits_v=()):
            EV(lambda e: e.tensor_copy(rowC[0:1, 8:9], rowA[0:1, col:col + 1]),
               waits=waits_v)
            EV(lambda e: e.tensor_copy(tiny[0:1, 15:16], rowC[0:1, 8:9]))
            EV(lambda e: e.tensor_sub(tiny[0:1, 14:15], rowA[0:1, col:col + 1],
                                      tiny[0:1, 15:16]))
            f_v = EV(lambda e: e.tensor_copy(rowC[0:1, 9:10], tiny[0:1, 14:15]))
            E("tensor", lambda e: e.matmul(psB[:, 1536:1537], ones_s[0:1, :], rowC[0:1, 8:9],
                                           start=True, stop=False), waits=[f_v])
            E("tensor", lambda e: e.matmul(psB[:, 1536:1537], ones_s[0:1, :], rowC[0:1, 9:10],
                                           start=False, stop=True))
            f_t = IL("tensor", "t")
            return EV(lambda e: e.tensor_copy(dst[:, :], psB[:, 1536:1537]), waits=[f_t])

        def pmax128(tc, dstc, mc, waits_v=()):
            f_v = EV(lambda e: e.tensor_copy(mbt[:, mc:mc + 1], tiny[:, tc:tc + 1]),
                     waits=waits_v)
            E("tensor", lambda e: e.matmul(psB[0:1, 1536:1664], mbt[:, mc:mc + 1],
                                           ident_s[:, :], start=True, stop=True),
              waits=[f_v])
            f_t = IL("tensor", "t")
            EV(lambda e: e.tensor_copy(rowA[0:1, 128:256], psB[0:1, 1536:1664]),
               waits=[f_t])
            return EV(lambda e: e.tensor_reduce(rowA[0:1, dstc:dstc + 1],
                                                rowA[0:1, 128:256], AX.X, ALU.max))

        f_A_end = {}
        with ExitStack() as actx:
            sa = lambda name, shape, dt=BF16: actx.enter_context(nc.sbuf_tensor(name, shape, dt))
            xs = sa("xs", [128, NCH * 4096])
            eff = sa("eff", [128, NCH * 1024])
            qcb = sa("qcb", [128, T]); qeb = sa("qeb", [128, T])
            kcb = sa("kcb", [128, T]); keb = sa("keb", [128, T])
            erb = sa("erb", [128, T])
            vsb = sa("vsb", [128, 16 * 260])
            attu = sa("attu", [128, 2 * 2048])
            ysb = sa("ysb", [128, NCH * 2048])
            zr = sa("zr", [128, 2048], F32)
            zrb = sa("zrb", [128, 2048])
            wdt3 = sa("wdt3", [128, 3 * 2048])
            xf = wdt3[:, 0:2048].bitcast(F32)
            r1f = wdt3[:, 2048:4096].bitcast(F32)
            ballg = sa("ballg", [128, 2048])
            wcs = sa("wcs", [128, 2 * 1024])
            strow = sa("strow", [1, 1536], F32)
            strow_b = sa("strow_b", [1, 1536])

            E("gpsimd", lambda e: e.memset(zr[:, :], 1.0), waits=[g_setup])
            IL("gpsimd", "g")

            f_exp0_last = None
            f_exp1_last = None
            f_avnorm_last = None
            f_proj_prev = None
            f_vsb_free = None
            f_ln1_v = None

            for s_ in range(SEQ):
                t0 = s_ * T
                for g in range(4):
                    MK(f"s{s_}g{g}_start")
                    hl0 = 256 * g
                    for wi, dsrc in enumerate((d_wqdT, d_wkdT, d_wvdT)):
                        E("gpsimd", lambda e, wi=wi, dsrc=dsrc, hl0=hl0: e.dma_start(
                            out=wdt3[:, wi * 2048: (wi + 1) * 2048]
                                .rearrange("p (a n) -> p a n", a=2),
                            in_=dsrc[hl0:hl0 + 256, :].rearrange("(a p) n -> p a n", p=128)),
                          waits=[f_proj_prev, f_dx_x if (s_ == 0 and g == 0) else None,
                                 f_ln1_v, cc("dxw") if g == 0 else None])
                        IL("gpsimd", "dgw", 16)
                    E("gpsimd", lambda e, hl0=hl0, g=g: e.dma_start(
                        out=ballg[:].rearrange("p (a n) -> p a n", a=2),
                        in_=d_ball[hl0:hl0 + 256, 1024 * g:1024 * (g + 1)]
                            .rearrange("(a p) n -> p a n", p=128)))
                    IL("gpsimd", "dgw", 16)
                    f_gdma = cc("dgw")

                    f_effmm = [None] * NCH
                    f_effev = [None] * NCH
                    for chv in range(NCH):
                        cols = ((0, 384, psA), (512, 384, psA), (1024, 256, psA)) if chv % 2 == 0 \
                            else ((1536, 384, psA), (0, 384, psB), (512, 256, psB))
                        for ri, (pb, nn, ps) in enumerate(cols):
                            r0 = (0, 384, 768)[ri]
                            for a in range(2):
                                w_ = []
                                if chv == 0 and ri == 0 and a == 0:
                                    w_ = [f_gdma, f_dx_const, f_exp0_last,
                                          f_exp1_last, f_avnorm_last, f_ln1_v]
                                if chv >= 2 and ri == 0 and a == 0:
                                    w_ = [f_effev[chv - 2]]
                                E("tensor", lambda e, chv=chv, ri=ri, r0=r0, pb=pb, nn=nn,
                                  ps=ps, a=a: e.matmul(
                                    ps[:, pb:pb + nn],
                                    wdt3[:, ri * 2048 + a * 1024 + chv * 128:
                                         ri * 2048 + a * 1024 + chv * 128 + 128],
                                    ballg[:, a * 1024 + r0: a * 1024 + r0 + nn],
                                    start=(a == 0), stop=(a == 1)), waits=w_)
                        f_effmm[chv] = IL("tensor", "t")
                        for ri, (pb, nn, ps) in enumerate(cols):
                            co = (0, 384, 768)[ri]
                            E("vector", lambda e, chv=chv, pb=pb, nn=nn, ps=ps, co=co:
                              e.tensor_copy(eff[:, chv * 1024 + co: chv * 1024 + co + nn],
                                            ps[:, pb:pb + nn]),
                              waits=[f_effmm[chv]] + ([f_proj_prev] if chv == 0 and ri == 0 else []))
                        f_effev[chv] = IL("vector", "v")
                    f_eff_done = f_effev[NCH - 1]

                    dests = ((qcb, 0), (qeb, 1), (erb, 2), (kcb, 3), (keb, 4), (erb, 5))
                    f_pj = {}
                    f_rope_done = None
                    for di, (dst, ci) in enumerate(dests):
                        for tt in range(TT):
                            pb = 512 * (tt % 2)
                            w_ = [f_eff_done] if (di == 0 and tt <= 1) else []
                            prev = f_pj.get((di, tt - 2)) or f_pj.get((di - 1, tt + 2))
                            if prev is not None:
                                w_.append(prev)
                            if di == 5 and tt == 0 and f_rope_done is not None:
                                w_.append(f_rope_done)
                            for chv in range(NCH):
                                E("tensor", lambda e, ci=ci, tt=tt, chv=chv, pb=pb, t0=t0: e.matmul(
                                    psA[:, pb:pb + 512],
                                    eff[:, chv * 1024 + ci * 128: chv * 1024 + ci * 128 + 128],
                                    xsv(chv, t0 + tt * 512, 512),
                                    start=(chv == 0), stop=(chv == NCH - 1)),
                                  waits=w_ if chv == 0 else [])
                            f_mm = IL("tensor", "t")
                            E("vector", lambda e, dst=dst, tt=tt, pb=pb: e.tensor_copy(
                                dst[:, tt * 512:(tt + 1) * 512], psA[:, pb:pb + 512]),
                              waits=[f_mm])
                            f_pj[(di, tt)] = IL("vector", "v")
                        if di == 2:
                            E("vector", lambda e: e.tensor_mul(erb[:, :], erb[:, :], sin_s[:, :]),
                              waits=[f_pj[(di, TT - 1)]])
                            E("vector", lambda e: e.tensor_mul(qeb[:, :], qeb[:, :], cos_s[:, :]))
                            E("vector", lambda e: e.tensor_add(qeb[:, :], qeb[:, :], erb[:, :]))
                            f_rope_done = IL("vector", "v")
                        if di == 5:
                            E("vector", lambda e: e.tensor_mul(erb[:, :], erb[:, :], sin_s[:, :]),
                              waits=[f_pj[(di, TT - 1)]])
                            E("vector", lambda e: e.tensor_mul(keb[:, :], keb[:, :], cos_s[:, :]))
                            E("vector", lambda e: e.tensor_add(keb[:, :], keb[:, :], erb[:, :]))
                            f_ropek = IL("vector", "v")
                    f_qk_done = cc("v")

                    E("vector", lambda e: e.memset(vsb[:, :], 1.0),
                      waits=[f_vsb_free, f_eff_done])
                    f_vsb_ones = IL("vector", "v")
                    f_vp = {}
                    for tc in range(16):
                        pb = 1024 + 512 * (tc % 2)
                        w_ = [f_vsb_ones] if tc == 0 else []
                        if tc >= 2:
                            w_.append(f_vp[tc - 2])
                        for chv in range(NCH):
                            E("tensor", lambda e, tc=tc, chv=chv, pb=pb, t0=t0: e.matmul(
                                psA[:, pb:pb + 256],
                                xsv(chv, t0 + tc * 128, 128),
                                eff[:, chv * 1024 + 768: chv * 1024 + 1024],
                                start=(chv == 0), stop=(chv == NCH - 1)),
                              waits=w_ if chv == 0 else [])
                        f_mm = IL("tensor", "t")
                        E("vector", lambda e, tc=tc, pb=pb: e.tensor_copy(
                            vsb[:, 260 * tc: 260 * tc + 260]
                                .rearrange("p (j d) -> p j d", j=4)[:, :, 0:64],
                            psA[:, pb:pb + 256].rearrange("p (j d) -> p j d", j=4)),
                          waits=[f_mm])
                        f_vp[tc] = IL("vector", "v")
                    f_v_done = cc("v")

                    EV(lambda e: e.tensor_reduce(tiny[:, 0:1], qcb[:, :], AX.X,
                                                 ALU.max, apply_absolute_value=True),
                       waits=[f_qk_done])
                    EV(lambda e: e.tensor_reduce(tiny[:, 1:2], qeb[:, :], AX.X,
                                                 ALU.max, apply_absolute_value=True))
                    EV(lambda e: e.tensor_max(tiny[:, 0:1], tiny[:, 0:1], tiny[:, 1:2]))
                    EV(lambda e: e.tensor_reduce(tiny[:, 2:3], kcb[:, :], AX.X,
                                                 ALU.max, apply_absolute_value=True))
                    EV(lambda e: e.tensor_reduce(tiny[:, 3:4], keb[:, :], AX.X,
                                                 ALU.max, apply_absolute_value=True))
                    EV(lambda e: e.tensor_max(tiny[:, 2:3], tiny[:, 2:3], tiny[:, 3:4]))
                    pmax128(0, 2, 0)
                    pmax128(2, 3, 1)
                    EV(lambda e: e.tensor_mul(rowA[0:1, 4:5], rowA[0:1, 2:3],
                                              rowA[0:1, 3:4]))
                    EV(lambda e: e.tensor_scalar(rowA[0:1, 4:5], rowA[0:1, 4:5],
                                                 -8.0, None, ALU.mult))
                    f_bias_v = EV(lambda e: e.tensor_copy(rowC[0:1, 4:5], rowA[0:1, 4:5]))
                    E("tensor", lambda e: e.matmul(psB[:, 1536:1537], ones_s[0:1, :],
                                                   rowC[0:1, 4:5], start=True, stop=True),
                      waits=[f_bias_v])
                    f_bias_t = IL("tensor", "t")
                    f_bias = EV(lambda e: e.tensor_copy(biasg[:, :], psB[:, 1536:1537]),
                                waits=[f_bias_t])

                    MK(f"s{s_}g{g}_proj")
                    pend = {"qt": None, "f_evac": None}

                    def emit_norm(g=g):
                        qn = pend["qt"]
                        f_ev_ = pend["f_evac"]
                        pend["qt"] = None
                        for j in range(4):
                            if j % 2 == 0:
                                E("tensor", lambda e, j=j: e.matmul(
                                    psB[0:64, 512 * j:512 * j + 512],
                                    ones_s[64:65, 0:64],
                                    zrb[64:65, 512 * j:512 * j + 512],
                                    start=True, stop=True, tile_position=(64, 0)),
                                  waits=[f_ev_] if j == 0 else [])
                            else:
                                E("tensor", lambda e, j=j: e.matmul(
                                    psB[64:128, 512 * j:512 * j + 512],
                                    ones_s[0:1, 0:64],
                                    zrb[0:1, 512 * j:512 * j + 512],
                                    start=True, stop=True, tile_position=(0, 64)))
                        f_outer = IL("tensor", "t")
                        for j in range(4):
                            yc = 2 * g + j // 2
                            pr = 64 * (j % 2)
                            E("vector", lambda e, j=j, yc=yc, pr=pr, qn=qn: e.tensor_mul(
                                ysb[pr:pr + 64, yc * 2048 + qn * 512: yc * 2048 + qn * 512 + 512],
                                ysb[pr:pr + 64, yc * 2048 + qn * 512: yc * 2048 + qn * 512 + 512],
                                psB[pr:pr + 64, 512 * j:512 * j + 512]),
                              waits=[f_outer] if j == 0 else [])
                        return IL("vector", "v")

                    for qt in range(TT):
                        nkt = 4 * qt + 4
                        f_av = {}
                        av_args = {}

                        def emit_av(kt, nkt):
                            a0, f_dep0, f_dep1 = av_args[kt]
                            w_av = [f_dep0]
                            if kt == 0:
                                w_av.append(f_avnorm_last)
                                w_av.append(f_v_done)
                            for j in range(4):
                                w_j = w_av if j == 0 else ([f_dep1] if j == 2 else [])
                                if j % 2 == 0:
                                    E("tensor", lambda e, j=j, kt=kt, a0=a0, nkt=nkt: e.matmul(
                                        psB[0:65, 512 * j:512 * (j + 1)],
                                        vsb[:, 260 * kt + 65 * j: 260 * kt + 65 * j + 65],
                                        attu[:, a0 + 512 * j: a0 + 512 * (j + 1)],
                                        start=(kt == 0), stop=(kt == nkt - 1)),
                                      waits=w_j)
                                else:
                                    E("tensor", lambda e, j=j, kt=kt, a0=a0, nkt=nkt: e.matmul(
                                        psB[64:128, 512 * j:512 * (j + 1)],
                                        vsb[:, 260 * kt + 65 * j: 260 * kt + 65 * j + 64],
                                        attu[:, a0 + 512 * j: a0 + 512 * (j + 1)],
                                        start=(kt == 0), stop=(kt == nkt - 1)))
                                    E("tensor", lambda e, j=j, kt=kt, a0=a0, nkt=nkt: e.matmul(
                                        psB[0:1, 512 * j:512 * (j + 1)],
                                        ones_s[:, 0:1],
                                        attu[:, a0 + 512 * j: a0 + 512 * (j + 1)],
                                        start=(kt == 0), stop=(kt == nkt - 1)))
                            f_av[kt] = IL("tensor", "t")

                        for kt in range(nkt):
                            w_sc0 = [f_exp0_last, f_bias] if kt == 0 else [f_exp0_last]
                            w_sc1 = [f_exp1_last] if kt > 0 else [f_exp1_last, f_bias]
                            if qt == 0 and kt == 0:
                                w_sc0 += [f_qk_done, f_avnorm_last]
                                w_sc1 += [f_qk_done]
                            for half in range(2):
                                js = (0, 1) if half == 0 else (2, 3)
                                w_h = w_sc0 if half == 0 else w_sc1
                                for j in js:
                                    E("tensor", lambda e, j=j, kt=kt, qt=qt: e.matmul(
                                        psA[:, 512 * j:512 * (j + 1)],
                                        kcb[32 * j:32 * (j + 1), kt * 128:(kt + 1) * 128],
                                        qcb[32 * j:32 * (j + 1), qt * 512:(qt + 1) * 512],
                                        start=True, stop=False, tile_position=(32 * j, 0)),
                                      waits=w_h if j == js[0] else [])
                                for j in js:
                                    E("tensor", lambda e, j=j, kt=kt, qt=qt: e.matmul(
                                        psA[:, 512 * j:512 * (j + 1)],
                                        keb[32 * j:32 * (j + 1), kt * 128:(kt + 1) * 128],
                                        qeb[32 * j:32 * (j + 1), qt * 512:(qt + 1) * 512],
                                        start=False, stop=True, tile_position=(32 * j, 0)))
                                if half == 0:
                                    f_sc0 = IL("tensor", "t")
                                else:
                                    f_sc1 = IL("tensor", "t")
                            a0 = (kt % 2) * 2048
                            w_e0 = [f_sc0]
                            w_e1 = [f_sc1]
                            if f_av.get(kt - 2) is not None:
                                w_e0.append(f_av[kt - 2])
                                w_e1.append(f_av[kt - 2])
                            E("scalar", lambda e, a0=a0: e.activation(
                                attu[:, a0:a0 + 1024], psA[:, 0:1024], AFT.Exp,
                                bias=biasg[:, :], scale=SCALE), waits=w_e0)
                            f_exp0_last = IL("scalar", "s")
                            E("scalar", lambda e, a0=a0: e.activation(
                                attu[:, a0 + 1024:a0 + 2048], psA[:, 1024:2048], AFT.Exp,
                                bias=biasg[:, :], scale=SCALE), waits=w_e1)
                            f_exp1_last = IL("scalar", "s")
                            f_dep0, f_dep1 = f_exp0_last, f_exp1_last
                            if kt >= 4 * qt:
                                r = kt - 4 * qt
                                for j in range(4):
                                    E("vector", lambda e, a0=a0, j=j, r=r: e.tensor_mul(
                                        attu[:, a0 + 512 * j: a0 + 512 * (j + 1)],
                                        attu[:, a0 + 512 * j: a0 + 512 * (j + 1)],
                                        dmask_s[:, r * 512:(r + 1) * 512]),
                                      waits=[f_exp0_last] if j == 0 else
                                            ([f_exp1_last] if j == 2 else []))
                                f_dep1 = IL("vector", "v")
                                f_dep0 = f_dep1
                            av_args[kt] = (a0, f_dep0, f_dep1)
                            if kt == 1 and pend["qt"] is not None:
                                f_avnorm_last = emit_norm()
                            if kt >= 1:
                                emit_av(kt - 1, nkt)
                        emit_av(nkt - 1, nkt)
                        f_av_fin = f_av[nkt - 1]

                        for j in range(4):
                            yc = 2 * g + j // 2
                            prow = 64 * (j % 2)
                            if j % 2 == 0:
                                E("vector", lambda e, j=j, yc=yc, qt=qt: e.tensor_copy(
                                    ysb[0:64, yc * 2048 + qt * 512: yc * 2048 + qt * 512 + 512],
                                    psB[0:64, 512 * j:512 * j + 512]),
                                  waits=[f_av_fin] if j == 0 else [])
                                E("vector", lambda e, j=j: _recip_bf(
                                    e, zrb[64:65, 512 * j:512 * j + 512],
                                    psB[64:65, 512 * j:512 * j + 512]))
                            else:
                                E("vector", lambda e, j=j, yc=yc, qt=qt: e.tensor_copy(
                                    ysb[64:128, yc * 2048 + qt * 512: yc * 2048 + qt * 512 + 512],
                                    psB[64:128, 512 * j:512 * j + 512]))
                                E("vector", lambda e, j=j: _recip_bf(
                                    e, zrb[0:1, 512 * j:512 * j + 512],
                                    psB[0:1, 512 * j:512 * j + 512]))
                        f_evac = IL("vector", "v")
                        pend["qt"] = qt
                        pend["f_evac"] = f_evac
                        if qt == TT - 1:
                            f_avnorm_last = emit_norm()
                    f_proj_prev = cc("t")
                    f_vsb_free = cc("t")

                MK(f"s{s_}_attn_end")
                f_groups_done = cc("t")
                f_y_all = f_avnorm_last
                f_wc = {}
                f_wcd = {}
                f_xhw = {}
                xfs3w = [xf[:, 0:512], xf[:, 512:1024],
                         wdt3[:, 4096:6144].bitcast(F32)[:, 0:512]]
                f_wcn = {}
                f_xhn = {}
                for oc in range(NCH):
                    slot = (oc % 2) * 1024
                    E("gpsimd", lambda e, oc=oc, slot=slot: e.dma_start(
                        out=wcs[:, slot:slot + 1024].rearrange("p (a n) -> p a n", a=8),
                        in_=d_wc[:, oc * 128:(oc + 1) * 128]
                            .rearrange("(a p) n -> p a n", p=128)),
                      waits=[f_wc[oc - 2]] if oc >= 2 else
                            ([f_A_end.get("wcs_prev")] if f_A_end.get("wcs_prev") else []))
                    f_wcd[oc] = IL("gpsimd", "dwc", 16)
                    for tt in range(TT):
                        n_ = oc * TT + tt
                        pb = 512 * (tt % 4)
                        parx = n_ % 3
                        parr = n_ % 2
                        dsem = ("dxf0", "dxf1", "dxw")[parx]
                        prev = f_wc.get((oc, tt - 4)) or f_wc.get((oc - 1, tt))
                        prev3 = f_wcn.get(n_ - 3)
                        E("sync", lambda e, oc=oc, tt=tt, parx=parx, t0=t0, s_=s_: e.dma_start(
                            out=xfs3w[parx],
                            in_=d_x[s_, oc * 128:(oc + 1) * 128, tt * 512:(tt + 1) * 512]),
                          waits=[prev3, f_groups_done] if prev3 else [f_groups_done])
                        f_xfd = IL("sync", dsem, 16)
                        w_ = [f_wcd[oc]]
                        if tt == 0 and oc == 0:
                            w_.append(f_y_all)
                        if prev:
                            w_.append(prev)
                        for chv in range(NCH):
                            E("tensor", lambda e, oc=oc, tt=tt, chv=chv, pb=pb, slot=slot:
                              e.matmul(psA[:, pb:pb + 512],
                                       wcs[:, slot + chv * 128: slot + chv * 128 + 128],
                                       ysb[:, chv * 2048 + tt * 512: chv * 2048 + tt * 512 + 512],
                                       start=(chv == 0), stop=(chv == NCH - 1)),
                              waits=w_ if chv == 0 else [])
                        f_mm = IL("tensor", "t")
                        w_stt = [f_mm, f_xfd]
                        pw = f_xhn.get(n_ - 2)
                        if pw:
                            w_stt.append(pw)
                        E("vector", lambda e, parx=parx, parr=parr, pb=pb:
                          e.scalar_tensor_tensor(
                            r1f[:, parr * 512:(parr + 1) * 512], psA[:, pb:pb + 512],
                            onesf[:, :], xfs3w[parx],
                            ALU.mult, ALU.add),
                          waits=w_stt)
                        E("vector", lambda e, oc=oc, tt=tt, parr=parr, t0=t0: e.tensor_copy(
                            xsv(oc, t0 + tt * 512, 512), r1f[:, parr * 512:(parr + 1) * 512]))
                        f_r1v = IL("vector", "v")
                        E("scalar", lambda e, oc=oc, tt=tt, parr=parr, s_=s_: e.dma_start(
                            out=d_xh[s_, oc * 128:(oc + 1) * 128, tt * 512:(tt + 1) * 512],
                            in_=r1f[:, parr * 512:(parr + 1) * 512]), waits=[f_r1v])
                        f_xhw[(oc, tt)] = IL("scalar", "dxh", 16)
                        f_xhn[n_] = f_xhw[(oc, tt)]
                        f_wc[(oc, tt)] = f_r1v
                        f_wcn[n_] = f_r1v
                    f_wc[oc] = f_wc[(oc, TT - 1)]
                f_r1w_all = cc("dxh")
                f_A_end["wcs_prev"] = cc("t")
                f_r1 = cc("v")

                MK(f"s{s_}_wc_end")
                f_sq = {}
                f_st = {}
                for chv in range(NCH):
                    slot = 0
                    w_ = [f_r1] if chv == 0 else [f_st[chv - 1]]
                    E("scalar", lambda e, chv=chv, slot=slot, t0=t0: e.activation(
                        attu[:, slot:slot + 2048], xsv(chv, t0, 2048), AFT.Square), waits=w_)
                    f_sq[chv] = IL("scalar", "s")
                    for tt in range(TT):
                        E("tensor", lambda e, chv=chv, tt=tt, t0=t0: e.matmul(
                            psB[0:1, tt * 512:(tt + 1) * 512], ones_s[:, 0:1],
                            xsv(chv, t0 + tt * 512, 512),
                            start=(chv == 0), stop=(chv == NCH - 1)),
                          waits=[f_sq[chv], f_r1, f_avnorm_last] if (chv == 0 and tt == 0)
                                else [f_sq[chv]] if tt == 0 else [])
                        E("tensor", lambda e, chv=chv, tt=tt, slot=slot: e.matmul(
                            psA[0:1, tt * 512:(tt + 1) * 512], ones_s[:, 0:1],
                            attu[:, slot + tt * 512: slot + (tt + 1) * 512],
                            start=(chv == 0), stop=(chv == NCH - 1)))
                    f_st[chv] = IL("tensor", "t")
                f_stats = f_st[NCH - 1]

                f_ap = {}
                f_xw = {}
                f_apn = {}
                f_xwn = {}
                xfs3 = [xf[:, 0:512], xf[:, 512:1024],
                        wdt3[:, 4096:6144].bitcast(F32)[:, 0:512]]
                for tt in range(TT):
                    E("vector", lambda e, tt=tt: e.tensor_scalar(
                        strow[0:1, 0:512], psB[0:1, tt * 512:(tt + 1) * 512],
                        1.0 / C, None, ALU.mult), waits=[f_stats])
                    E("vector", lambda e, tt=tt: e.tensor_scalar(
                        strow[0:1, 512:1024], psA[0:1, tt * 512:(tt + 1) * 512],
                        1.0 / C, None, ALU.mult))
                    E("vector", lambda e: e.tensor_mul(
                        strow[0:1, 1024:1536], strow[0:1, 0:512], strow[0:1, 0:512]))
                    E("vector", lambda e: e.tensor_sub(
                        strow[0:1, 512:1024], strow[0:1, 512:1024], strow[0:1, 1024:1536]))
                    f_var = IL("vector", "v")
                    E("scalar", lambda e: e.activation(
                        strow[0:1, 512:1024], strow[0:1, 512:1024], AFT.Sqrt,
                        bias=epsb[0:1, 0:1]), waits=[f_var])
                    f_sd = IL("scalar", "s")
                    E("vector", lambda e: e.reciprocal(
                        strow[0:1, 512:1024], strow[0:1, 512:1024]), waits=[f_sd])
                    E("vector", lambda e: e.tensor_mul(
                        strow[0:1, 1024:1536], strow[0:1, 0:512], strow[0:1, 512:1024]))
                    E("vector", lambda e: e.tensor_copy(strow_b[0:1, 0:512],
                                                        strow[0:1, 512:1024]))
                    E("vector", lambda e: e.tensor_copy(strow_b[0:1, 512:1024],
                                                        strow[0:1, 1024:1536]))
                    E("vector", lambda e: e.tensor_copy(strow[0:1, 0:512],
                                                        strow_b[0:1, 0:512]))
                    E("vector", lambda e: e.tensor_sub(strow[0:1, 0:512],
                                                       strow[0:1, 512:1024],
                                                       strow[0:1, 0:512]))
                    E("vector", lambda e: e.tensor_copy(strow_b[0:1, 1024:1536],
                                                        strow[0:1, 0:512]))
                    f_rows = IL("vector", "v")
                    E("tensor", lambda e, tt=tt: e.matmul(
                        psB[:, tt * 512:(tt + 1) * 512], ones_s[0:1, :],
                        strow_b[0:1, 0:512], start=True, stop=False), waits=[f_rows])
                    E("tensor", lambda e, tt=tt: e.matmul(
                        psB[:, tt * 512:(tt + 1) * 512], ones_s[0:1, :],
                        strow_b[0:1, 1024:1536], start=False, stop=True))
                    E("tensor", lambda e, tt=tt: e.matmul(
                        psA[:, tt * 512:(tt + 1) * 512], ones_s[0:1, :],
                        strow_b[0:1, 512:1024], start=True, stop=True))
                    f_bc = IL("tensor", "t")
                    for oc in range(NCH):
                        n_ = tt * NCH + oc
                        parx = n_ % 3
                        parr = oc % 2
                        dsem = ("dxf0", "dxf1", "dxh")[parx]
                        w_d = [f_r1w_all]
                        if (n_ - 3) in f_apn:
                            w_d.append(f_apn[n_ - 3])
                        E("sync", lambda e, oc=oc, tt=tt, parx=parx, s_=s_: e.dma_start(
                            out=xfs3[parx],
                            in_=d_xh[s_, oc * 128:(oc + 1) * 128, tt * 512:(tt + 1) * 512]),
                          waits=w_d)
                        f_xfd = IL("sync", dsem, 16)
                        w_t = [f_xfd, f_bc]
                        pw = f_xwn.get(n_ - 2)
                        if pw:
                            w_t.append(pw)
                        E("vector", lambda e, parx=parx, parr=parr, tt=tt: e.tensor_mul(
                            r1f[:, parr * 512:(parr + 1) * 512],
                            xfs3[parx],
                            psB[:, tt * 512:(tt + 1) * 512]), waits=w_t)
                        E("vector", lambda e, parr=parr, tt=tt: e.tensor_sub(
                            r1f[:, parr * 512:(parr + 1) * 512],
                            r1f[:, parr * 512:(parr + 1) * 512],
                            psA[:, tt * 512:(tt + 1) * 512]))
                        E("vector", lambda e, parr=parr, col=tt * NCH + oc: e.tensor_reduce(
                            zr[:, col:col + 1], r1f[:, parr * 512:(parr + 1) * 512],
                            AX.X, ALU.max, apply_absolute_value=True))
                        f_ap[(tt, oc)] = IL("vector", "v")
                        f_apn[n_] = f_ap[(tt, oc)]
                        E("scalar", lambda e, oc=oc, tt=tt, parr=parr, s_=s_: e.dma_start(
                            out=d_xh[s_, oc * 128:(oc + 1) * 128, tt * 512:(tt + 1) * 512],
                            in_=r1f[:, parr * 512:(parr + 1) * 512]),
                          waits=[f_ap[(tt, oc)]])
                        f_xw[(tt, oc)] = IL("scalar", "dxw", 16)
                        f_xwn[n_] = f_xw[(tt, oc)]
                EV(lambda e: e.tensor_reduce(tiny[:, 4:5], zr[:, 0:32],
                                             AX.X, ALU.max))
                if s_ == 0:
                    EV(lambda e: e.tensor_copy(tiny[:, 5:6], tiny[:, 4:5]))
                else:
                    EV(lambda e: e.tensor_max(tiny[:, 5:6], tiny[:, 5:6],
                                              tiny[:, 4:5]))
                f_ln1_v = cc("v")

            f_xh_all = [cc("dxh"), cc("dxw")]
            if STOP_AFTER == "ln1":
                E("gpsimd", lambda e: e.dma_start(out=d_out[0, :, :], in_=d_xh[0, :, :]),
                  waits=[cc("v"), cc("t"), cc("s")] + f_xh_all)
                IL("gpsimd", "dou0", 16)
                for di_, ap_ in enumerate((ones_s[0:1, 0:128], epsb[:, 0:1],
                                           tiny[:, 4:5], tiny[:, 5:6],
                                           rowA[0:1, 0:256], cbias[:, 0:1],
                                           gmx1[0:1, 0:128], biasg[:, 0:1])):
                    E("gpsimd", lambda e, di_=di_, ap_=ap_: e.dma_start(
                        out=d_out[1, di_ * 128:di_ * 128 + ap_.shape[0], 0:ap_.shape[1]],
                        in_=ap_, allow_slow_non_contiguous=True))
                    IL("gpsimd", "dou0", 16)
                E("sync", lambda e: e.nop(), waits=[cc("dou0")])

            if STOP_AFTER in (None, "h", "c1"):
                MK("ln1_end")
                f_xmax = pmax128(5, 5, 2, waits_v=[f_ln1_v])
                EV(lambda e: e.memset(gmx1[:, :], 0.0), waits=[f_xmax])
                f_gm = EV(lambda e: e.tensor_copy(gmx1[0:1, 0:1], rowA[0:1, 5:6]))
                E("gpsimd", lambda e: e.dma_start(out=d_ar1i[:, :], in_=gmx1[:, :]),
                  waits=[f_gm])
                IL("gpsimd", "dar", 16)
                f_ar_in = cc("dar")
                E("gpsimd", lambda e: e.collective_compute(
                    "AllReduce", ALU.max, replica_groups=[list(range(NCORE))],
                    ins=[d_ar1i[:, :]], outs=[d_ar1o[:, :]]), waits=[f_ar_in])
                IL("gpsimd", "cc")
                f_cc1 = cc("cc")
                E("gpsimd", lambda e: e.dma_start(out=gmx1[:, :], in_=d_ar1o[:, :]),
                  waits=[f_cc1])
                IL("gpsimd", "dar", 16)
                f_gmax1 = cc("dar")

                E("scalar", lambda e: e.activation(rowA[0:1, 6:7], rowA[0:1, 0:1], AFT.Sigmoid),
                  waits=[cc("dcl"), f_gmax1])
                f_sig = IL("scalar", "s")
                EV(lambda e: e.tensor_mul(rowA[0:1, 7:8], rowA[0:1, 6:7],
                                          gmx1[0:1, 0:1]), waits=[f_sig, f_gmax1])
                EV(lambda e: e.tensor_scalar(rowA[0:1, 8:9], rowA[0:1, 7:8],
                                             1.0 / 127.0, None, ALU.mult))
                EV(lambda e: e.reciprocal(rowA[0:1, 9:10], rowA[0:1, 8:9]))
                bcast128(9, rsx_b)
                bcast128(8, sxs_b)
                f_sx = cc("v")

                E("tensor", lambda e: e.nop(), waits=[])
                f_A_t = IL("tensor", "t")
                E("scalar", lambda e: e.copy(tiny[0:1, 13:14], tiny[0:1, 13:14]), waits=[f_sx])
                f_A_s = IL("scalar", "s")
                E("vector", lambda e: e.tensor_copy(tiny[0:1, 12:13], tiny[0:1, 12:13]))
                f_A_v = IL("vector", "v")
                E("gpsimd", lambda e: e.memset(tiny[0:1, 11:12], 0.0))
                f_A_g = IL("gpsimd", "g")
        if STOP_AFTER in (None, "h", "c1"):
            barrier_A = [f_A_t, f_A_s, f_A_v, f_A_g, f_sx]

            with ExitStack() as bctx:
                sbq = lambda name, shape, dt=BF16: bctx.enter_context(nc.sbuf_tensor(name, shape, dt))
                xq = sbq("xq", [128, NCH * 4096])
                hst = sbq("hst", [128, 2 * 4096], F32)
                rndq = bctx.enter_context(nc.sbuf_tensor("rndq", [128, 2 * 2048], I32))
                rndw = bctx.enter_context(nc.sbuf_tensor("rndw", [128, 1024], I32))
                xhq = sbq("xhq", [128, 3 * 2048], F32)
                wgo = sbq("wgo", [128, 1024], F32)
                wuo = sbq("wuo", [128, 1024], F32)
                wgT = sbq("wgT", [128, 2 * 1024])
                wuT = sbq("wuT", [128, 2 * 1024])
                wqi = sbq("wqi", [128, 4 * 1024])
                gbf = sbq("gbf", [128, 3 * 512])

                def xqv(chv, lo, n):
                    return xq[:, chv * 4096 + lo: chv * 4096 + lo + n]

                f_xq = {}
                f_xq_seq = {}
                nq = 0
                for s2 in range(SEQ):
                    for chv in range(NCH):
                        par = nq % 3
                        dsem = ("dxf0", "dxf1", "dxw")[par]
                        w_d = list(barrier_A) + f_xh_all
                        if (nq - 3) in f_xq:
                            w_d.append(f_xq[nq - 3])
                        eng_x = "scalar" if par == 2 else "sync"
                        E(eng_x, lambda e, chv=chv, s2=s2, par=par: e.dma_start(
                            out=xhq[:, par * 2048:(par + 1) * 2048],
                            in_=d_xh[s2, chv * 128:(chv + 1) * 128, :]), waits=w_d)
                        f_xd = IL(eng_x, dsem, 16)
                        qpar = nq % 2
                        w_s = [f_xd]
                        if (nq - 2) in f_xq:
                            w_s.append(f_xq[nq - 2])
                        else:
                            w_s += list(barrier_A)
                        E("scalar", lambda e, par=par, qpar=qpar: e.activation(
                            rndq.ap().bitcast(F32)[:, qpar * 2048:(qpar + 1) * 2048],
                            xhq[:, par * 2048:(par + 1) * 2048],
                            AFT.Identity, bias=cbias[:, :], scale=rsx_b[:, :]),
                          waits=w_s)
                        f_rd = IL("scalar", "s")
                        E("vector", lambda e, qpar=qpar: e.tensor_scalar(
                            rndq.ap().bitcast(F32)[:, qpar * 2048:(qpar + 1) * 2048],
                            rndq.ap().bitcast(F32)[:, qpar * 2048:(qpar + 1) * 2048],
                            C_RND + 127.0, C_RND - 128.0, ALU.min, ALU.max), waits=[f_rd])
                        E("vector", lambda e, chv=chv, s2=s2, qpar=qpar: e.tensor_scalar(
                            xqv(chv, s2 * T, T),
                            rndq.ap().bitcast(F32)[:, qpar * 2048:(qpar + 1) * 2048],
                            C_RND, None, ALU.subtract))
                        f_xq[nq] = IL("vector", "v")
                        nq += 1
                    f_xq_seq[s2] = cc("v")
                f_xq_done = cc("v")

                MK("B_xq_end")
                f_wq = {}
                f_int = {}
                f_rnd = {}
                f_trf = {}
                f_sp = {}
                f_hred_last = None

                def emit_chain(ic):
                    for wi, (dsrc, wo, svec) in enumerate(
                            ((d_wg, wgo, sgv), (d_wu, wuo, suv))):
                        w_d = list(barrier_A) if ic == 0 and wi == 0 else []
                        if (ic - 1, wi) in f_rnd:
                            w_d.append(f_rnd[(ic - 1, wi)])
                        E("scalar", lambda e, dsrc=dsrc, wo=wo, ic=ic: e.dma_start(
                            out=wo[:, :], in_=dsrc[ic * 128:(ic + 1) * 128, :]), waits=w_d)
                        f_wd = IL("scalar", "dwg" if wi == 0 else "dwu", 16)
                        w_r = [f_wd]
                        if wi == 1:
                            w_r.append(f_rnd[(ic, 0)])
                        EV(lambda e, wo=wo: e.tensor_reduce(
                            tiny[:, 6:7], wo[:, :], AX.X, ALU.max,
                            apply_absolute_value=True), waits=w_r)
                        EV(lambda e: e.reciprocal(tiny[:, 7:8], tiny[:, 6:7]))
                        EV(lambda e: e.tensor_scalar(tiny[:, 7:8], tiny[:, 7:8],
                                                     127.0, None, ALU.mult))
                        EV(lambda e: e.tensor_scalar(
                            tiny[:, 8:9], tiny[:, 6:7], 1.0 / 127.0, None, ALU.mult))
                        f_rsw = EV(lambda e, svec=svec, ic=ic: e.tensor_mul(
                            svec[:, ic:ic + 1], tiny[:, 8:9], sxs_b[:, :]))
                        E("scalar", lambda e, wo=wo: e.activation(
                            rndw.ap().bitcast(F32), wo[:, :], AFT.Identity,
                            bias=cbias[:, :], scale=tiny[:, 7:8]),
                          waits=[f_rsw, cc("v")])
                        f_rnd[(ic, wi)] = IL("scalar", "s")
                        E("vector", lambda e: e.tensor_scalar(
                            rndw.ap().bitcast(F32), rndw.ap().bitcast(F32),
                            C_RND + 127.0, C_RND - 128.0, ALU.min, ALU.max),
                          waits=[f_rnd[(ic, wi)]])
                        qslot = ((ic % 2) * 2 + wi) * 1024
                        E("vector", lambda e, qslot=qslot: e.tensor_scalar(
                            wqi[:, qslot:qslot + 1024], rndw.ap().bitcast(F32),
                            C_RND, None, ALU.subtract),
                          waits=[f_trf.get((ic - 2, wi))])
                        f_int[(ic, wi)] = IL("vector", "v")

                def emit_tr(ic, wi):
                    if True:
                        wT = wgT if wi == 0 else wuT
                        pb = psAb if wi == 0 else psBb
                        qslot = ((ic % 2) * 2 + wi) * 1024
                        for b in range(8):
                            E("tensor", lambda e, b=b, pb=pb, qslot=qslot: e.transpose(
                                pb[:, 3072 + 128 * b: 3072 + 128 * (b + 1)],
                                wqi[:, qslot + 128 * b: qslot + 128 * (b + 1)],
                                ident_s[:, :]),
                              waits=[f_int[(ic, wi)], f_wq.get((ic - 1, wi))]
                                    if b == 0 else [])
                        f_trf[(ic, wi)] = IL("tensor", "t")
                        slot = (ic % 2) * 1024
                        E("scalar", lambda e, wT=wT, slot=slot, pb=pb: e.copy(
                            wT[:, slot:slot + 1024], pb[:, 3072:4096]),
                          waits=[f_trf[(ic, wi)]])
                        f_wq[(ic, wi)] = IL("scalar", "s")

                def emit_mms(ic):
                    slot = (ic % 2) * 1024
                    hslot = (ic % 2) * 4096
                    for tt8 in range(SEQ * TT):
                        pg = 512 * (tt8 % 3)
                        w_ = []
                        if tt8 == 0:
                            w_ += [f_wq[(ic, 0)], f_wq[(ic, 1)], f_xq_seq[0]]
                            if (ic - 2) in f_sp:
                                w_.append(f_sp[ic - 2])
                        if tt8 == 4:
                            w_.append(f_xq_seq[1])
                        prev = f_wq.get(("evac", ic, tt8 - 3)) or f_wq.get(("evac", ic - 1, tt8 + 5))
                        if prev:
                            w_.append(prev)
                        for b in range(8):
                            E("tensor", lambda e, b=b, tt8=tt8, slot=slot, pg=pg: e.matmul(
                                psA[:, pg:pg + 512],
                                wgT[:, slot + 128 * b: slot + 128 * (b + 1)],
                                xqv(b, tt8 * 512, 512),
                                start=(b == 0), stop=(b == 7)), waits=w_ if b == 0 else [])
                        f_g = IL("tensor", "t")
                        for b in range(8):
                            E("tensor", lambda e, b=b, tt8=tt8, slot=slot, pg=pg: e.matmul(
                                psB[:, pg:pg + 512],
                                wuT[:, slot + 128 * b: slot + 128 * (b + 1)],
                                xqv(b, tt8 * 512, 512),
                                start=(b == 0), stop=(b == 7)))
                        f_u = IL("tensor", "t")
                        gslot = 512 * (tt8 % 3)
                        w_silu = [f_g]
                        if prev:
                            w_silu.append(prev)
                        E("scalar", lambda e, pg=pg, gslot=gslot, ic=ic: e.activation(
                            gbf[:, gslot:gslot + 512], psA[:, pg:pg + 512], AFT.Silu,
                            scale=sgv[:, ic:ic + 1]), waits=w_silu)
                        f_silu = IL("scalar", "s")
                        E("vector", lambda e, pg=pg, gslot=gslot, ic=ic, hslot=hslot, tt8=tt8:
                          e.scalar_tensor_tensor(
                              hst[:, hslot + tt8 * 512: hslot + (tt8 + 1) * 512],
                              psB[:, pg:pg + 512], suv[:, ic:ic + 1],
                              gbf[:, gslot:gslot + 512], ALU.mult, ALU.mult),
                          waits=[f_u, f_silu])
                        E("vector", lambda e, hslot=hslot, tt8=tt8, ic=ic: e.tensor_reduce(
                            hred[:, ic * 8 + tt8: ic * 8 + tt8 + 1],
                            hst[:, hslot + tt8 * 512: hslot + (tt8 + 1) * 512],
                            AX.X, ALU.max, apply_absolute_value=True))
                        f_wq[("evac", ic, tt8)] = IL("vector", "v")
                        if ic + 1 < NIC and tt8 in (3, 6):
                            emit_tr(ic + 1, 0 if tt8 == 3 else 1)
                    f_hr = f_wq[("evac", ic, SEQ * TT - 1)]
                    E("sync", lambda e, ic=ic, hslot=hslot: e.dma_start(
                        out=d_spill[:, :, ic * 512:(ic + 1) * 512].transpose([1, 0, 2]),
                        in_=hst[:, hslot:hslot + 4096].rearrange("p (a n) -> p a n", a=8)),
                      waits=[f_hr])
                    f_sp[ic] = IL("sync", "dsp0" if ic % 2 == 0 else "dsp1", 16)
                    return f_hr

                emit_chain(0)
                emit_tr(0, 0)
                emit_tr(0, 1)
                emit_chain(1)
                for ic in range(NIC):
                    MK(f"B_ic{ic}")
                    if ic + 2 < NIC:
                        emit_chain(ic + 2)
                    f_hred_last = emit_mms(ic)
                f_spill_all = [cc("dsp0"), cc("dsp1")]

                if STOP_AFTER == "h":
                    for a_ in range(6):
                        n_ = 2048 if a_ < 5 else NIC * 512 - 5 * 2048
                        E("gpsimd", lambda e, a_=a_, n_=n_: e.dma_start(
                            out=d_out[0, a_ * 128:(a_ + 1) * 128, 0:n_],
                            in_=d_spill[0, :, a_ * 2048: a_ * 2048 + n_]),
                          waits=([cc("v"), cc("t"), cc("s")] + f_spill_all) if a_ == 0 else [])
                        IL("gpsimd", "dou0", 16)
                    for di_, ap_ in enumerate((gmx1[0:1, 0:128], rowA[0:1, 0:256],
                                               rsx_b[:, 0:1], sxs_b[:, 0:1],
                                               sgv[:, 0:1], suv[:, 0:1],
                                               tiny[:, 5:6], hred[:, 0:1])):
                        E("gpsimd", lambda e, di_=di_, ap_=ap_: e.dma_start(
                            out=d_out[1, di_ * 128:di_ * 128 + ap_.shape[0],
                                      0:ap_.shape[1]], in_=ap_,
                            allow_slow_non_contiguous=True))
                        IL("gpsimd", "dou0", 16)
                    E("sync", lambda e: e.nop(), waits=[cc("dou0")])

                MK("B_mm_end")
                EV(lambda e: e.tensor_reduce(tiny[:, 9:10], hred[:, :], AX.X,
                                             ALU.max), waits=[f_hred_last])
                f_hm = pmax128(9, 12, 3)
                EV(lambda e: e.memset(gmx2[:, :], 0.0), waits=[f_hm])
                f_gm2 = EV(lambda e: e.tensor_copy(gmx2[0:1, 0:1], rowA[0:1, 12:13]))
                E("gpsimd", lambda e: e.dma_start(out=d_ar2i[:, :], in_=gmx2[:, :]),
                  waits=[f_gm2])
                IL("gpsimd", "dar", 16)
                f_ar2_in = cc("dar")
                E("gpsimd", lambda e: e.collective_compute(
                    "AllReduce", ALU.max, replica_groups=[list(range(NCORE))],
                    ins=[d_ar2i[:, :]], outs=[d_ar2o[:, :]]), waits=[f_ar2_in])
                IL("gpsimd", "cc")
                f_cc2 = cc("cc")
                E("gpsimd", lambda e: e.dma_start(out=gmx2[:, :], in_=d_ar2o[:, :]),
                  waits=[f_cc2])
                IL("gpsimd", "dar", 16)
                f_gmax2 = cc("dar")

                E("scalar", lambda e: e.activation(rowA[0:1, 13:14], rowA[0:1, 1:2], AFT.Sigmoid),
                  waits=[f_gmax2])
                f_sig2 = IL("scalar", "s")
                EV(lambda e: e.tensor_mul(rowA[0:1, 14:15], rowA[0:1, 13:14],
                                          gmx2[0:1, 0:1]), waits=[f_sig2, f_gmax2])
                EV(lambda e: e.tensor_scalar(rowA[0:1, 15:16], rowA[0:1, 14:15],
                                             1.0 / 127.0, None, ALU.mult))
                EV(lambda e: e.reciprocal(rowA[0:1, 16:17], rowA[0:1, 15:16]))
                bcast128(16, rsh_b)
                bcast128(15, sh_b)
                f_sh = cc("v")

                E("tensor", lambda e: e.nop(), waits=[])
                f_B_t = IL("tensor", "t")
                E("scalar", lambda e: e.copy(tiny[0:1, 13:14], tiny[0:1, 13:14]), waits=[f_sh])
                f_B_s = IL("scalar", "s")
                E("vector", lambda e: e.tensor_copy(tiny[0:1, 12:13], tiny[0:1, 12:13]))
                f_B_v = IL("vector", "v")
                E("gpsimd", lambda e: e.memset(tiny[0:1, 11:12], 0.0))
                f_B_g = IL("gpsimd", "g")
            barrier_B = [f_B_t, f_B_s, f_B_v, f_B_g, f_sh] + f_spill_all
            SKIP_C = STOP_AFTER == "h"

            if not SKIP_C:
                with ExitStack() as cctx:
                    sbc = lambda name, shape, dt=BF16: cctx.enter_context(nc.sbuf_tensor(name, shape, dt))
                    wdT = sbc("wdT", [128, NIC * 1024])
                    wdo = sbc("wdo", [128, 2 * I_FF], F32)
                    rndc = cctx.enter_context(nc.sbuf_tensor("rndc", [128, I_FF], I32))
                    wdi = sbc("wdi", [128, I_FF])
                    rb = sbc("rb", [128, 2 * 2048], F32)
                    xr = sbc("xr", [128, 2 * 8 * 512], F32)
                    r2f = sbc("r2f", [128, 8 * 512], F32)
                    r2b = sbc("r2b", [128, 2 * 512])
                    hq = sbc("hq", [128, 2 * 512])
                    outs = sbc("outs", [128, 2 * 512], F32)
                    sqc = sbc("sqc", [128, 2 * 512])
                    strowc = sbc("strowc", [1, 2048], F32)
                    strowc_b = sbc("strowc_b", [1, 1024])

                    MK("C_start")
                    rndc2 = rb.ap().bitcast(I32)[:, 0:I_FF]
                    wdi2 = r2f.ap().bitcast(BF16)[:, 0:I_FF]
                    f_dq = {}
                    f_intq = {}

                    def emit_wdchain(hc):
                        slot = (hc % 2) * I_FF
                        rnd_t = rndc.ap().bitcast(F32) if hc % 2 == 0 else rndc2.bitcast(F32)
                        wdi_t = wdi if hc % 2 == 0 else wdi2
                        w_ = list(barrier_B) if hc == 0 else []
                        if ("rnd", hc - 2) in f_dq:
                            w_.append(f_dq[("rnd", hc - 2)])
                        E("sync", lambda e, hc=hc, slot=slot: e.dma_start(
                            out=wdo[:, slot:slot + I_FF], in_=d_wd[hc * 128:(hc + 1) * 128, :]),
                          waits=w_)
                        f_wd = IL("sync", "dwd0" if hc % 2 == 0 else "dwd1", 16)
                        EV(lambda e, slot=slot: e.tensor_reduce(
                            tiny[:, 6:7], wdo[:, slot:slot + I_FF], AX.X, ALU.max,
                            apply_absolute_value=True),
                           waits=[f_wd, f_dq.get(("rnd", hc - 1))] +
                                 (list(barrier_B) if hc == 0 else []))
                        EV(lambda e: e.reciprocal(tiny[:, 7:8], tiny[:, 6:7]))
                        EV(lambda e: e.tensor_scalar(tiny[:, 7:8], tiny[:, 7:8],
                                                     127.0, None, ALU.mult))
                        EV(lambda e: e.tensor_scalar(tiny[:, 8:9], tiny[:, 6:7],
                                                     1.0 / 127.0, None, ALU.mult))
                        f_rsw = EV(lambda e, hc=hc: e.tensor_mul(sdv[:, hc:hc + 1], tiny[:, 8:9],
                                                                 sh_b[:, :]))
                        E("scalar", lambda e, slot=slot, rnd_t=rnd_t: e.activation(
                            rnd_t, wdo[:, slot:slot + I_FF], AFT.Identity,
                            bias=cbias[:, :], scale=tiny[:, 7:8]),
                          waits=[f_rsw] + (list(barrier_B) if hc == 0 else []))
                        f_rd = IL("scalar", "s")
                        f_dq[("rnd", hc)] = f_rd
                        E("vector", lambda e, rnd_t=rnd_t: e.tensor_scalar(
                            rnd_t, rnd_t,
                            C_RND + 127.0, C_RND - 128.0, ALU.min, ALU.max), waits=[f_rd])
                        E("vector", lambda e, rnd_t=rnd_t, wdi_t=wdi_t: e.tensor_scalar(
                            wdi_t[:, 0:I_FF], rnd_t, C_RND, None, ALU.subtract))
                        f_intq[hc] = IL("vector", "v")

                    def emit_wdtr(hc):
                        wdi_t = wdi if hc % 2 == 0 else wdi2
                        for b8 in range(3):
                            nb = 8 if b8 < 2 else 6
                            for u in range(nb):
                                icx = b8 * 8 + u
                                E("tensor", lambda e, icx=icx, u=u, wdi_t=wdi_t: e.transpose(
                                    psBb[:, 3072 + 128 * u: 3072 + 128 * (u + 1)],
                                    wdi_t[:, 128 * icx:128 * (icx + 1)], ident_s[:, :]),
                                  waits=([f_intq[hc]] + (list(barrier_B) if hc == 0 else []) +
                                         ([f_dq[hc - 1]] if (hc - 1) in f_dq else []))
                                        if (b8 == 0 and u == 0) else
                                        ([f_dq[("ev", hc, b8 - 1)]] if u == 0 else []))
                            f_tr = IL("tensor", "t")
                            E("vector", lambda e, hc=hc, b8=b8, nb=nb: e.tensor_copy(
                                wdT[:, :].rearrange("p (i n) -> p i n", i=NIC)
                                   [:, b8 * 8: b8 * 8 + nb, 128 * hc:128 * (hc + 1)],
                                psBb[:, 3072:3072 + nb * 128].rearrange("p (i n) -> p i n", i=nb)),
                              waits=[f_tr])
                            f_dq[("ev", hc, b8)] = IL("vector", "v")
                        f_dq[hc] = f_dq[("ev", hc, 2)]

                    emit_wdchain(0)
                    for hc in range(NCH):
                        if hc + 1 < NCH:
                            emit_wdchain(hc + 1)
                        emit_wdtr(hc)
                    f_wdT_done = f_dq[NCH - 1]

                    if HAVE_DBG:
                        EV(lambda e: e.tensor_reduce(tiny[:, 12:13],
                                                     wdT[:, 0:2048].bitcast(F32),
                                                     AX.X, ALU.max,
                                                     apply_absolute_value=True),
                           waits=[f_wdT_done])
                        for di_, ap_ in enumerate((rowA[0:1, 0:256], gmx2[0:1, 0:128],
                                                   sdv[0:1, 0:8], rsh_b[0:1, 0:1],
                                                   sh_b[0:1, 0:1], tiny[0:1, 12:13])):
                            E("gpsimd", lambda e, di_=di_, ap_=ap_: e.dma_start(
                                out=d_dbg[di_:di_ + 1, 0:ap_.shape[1]],
                                in_=ap_, allow_slow_non_contiguous=True),
                              waits=[cc("v"), cc("s"), cc("t")])
                            IL("gpsimd", "dou0", 16)

                    f_ev = {}
                    f_rbd = {}
                    f_ln2 = {}
                    dbg_fc = {}
                    f_r2s = {}
                    f_vars = {}
                    _ord = list(DOWN_ORDER or range(SEQ * TT))
                    for tt8i, tt8 in enumerate(_ord):
                        MK(f"C_tt8_{tt8i}")
                        p_tt8 = _ord[tt8i - 1] if tt8i > 0 else None
                        s_ = tt8 // TT
                        tt = tt8 % TT
                        t0 = s_ * T

                        def emit_rb(rs_, tt8=tt8):
                            n = 2048 if rs_ < 5 else 1024
                            slot = (rs_ % 2) * 2048
                            w_ = list(f_spill_all)
                            if p_tt8 is None:
                                w_.append(f_wdT_done)
                            if rs_ in (2, 3) and (tt8, "rb") in dbg_fc:
                                w_.append(dbg_fc[(tt8, "rb")])
                            if rs_ >= 2:
                                w_.append(f_ev[(tt8, rs_ * 4 - 5)])
                            if p_tt8 is not None:
                                w_.append(f_ev.get((p_tt8, 19 if rs_ == 0 else NIC - 1)))
                            E("sync", lambda e, tt8=tt8, rs_=rs_, n=n, slot=slot: e.dma_start(
                                out=rb[:, slot:slot + n],
                                in_=d_spill[tt8, :, rs_ * 2048: rs_ * 2048 + n]), waits=w_)
                            f_rbd[(tt8, rs_)] = IL("sync", "drb0" if rs_ % 2 == 0 else "drb1", 16)

                        xpar = tt8i % 2
                        dsemx = "dxf0" if xpar == 0 else "dxf1"
                        w_xr = list(f_xh_all)
                        if tt8i >= 2:
                            w_xr.append(f_r2s[tt8i - 2])
                            if tt8i == 2 and (_ord[0], "xr") in dbg_fc:
                                w_xr.append(dbg_fc[(_ord[0], "xr")])
                        E("scalar", lambda e, s_=s_, tt=tt, xpar=xpar: e.dma_start(
                            out=xr[:, xpar * 4096:(xpar + 1) * 4096]
                                .rearrange("p (a n) -> p a n", a=8),
                            in_=d_xh[s_, :, tt * 512:(tt + 1) * 512]
                                .rearrange("(a p) n -> p a n", p=128)), waits=w_xr)
                        f_xrd = IL("scalar", dsemx, 16)
                        emit_rb(0)
                        emit_rb(1)
                        if HAVE_DBG and tt8i == 0:
                            E("gpsimd", lambda e: e.dma_start(
                                out=d_dbg2[:, 0:2048], in_=rb[:, 0:2048]),
                              waits=[f_rbd[(tt8, 0)]])
                            dbg_fc[(tt8, "rb")] = IL("gpsimd", "dgw", 16)
                            E("gpsimd", lambda e: e.dma_start(
                                out=d_dbg2[:, 3072:7168], in_=xr[:, 0:4096]),
                              waits=[f_xrd])
                            dbg_fc[(tt8, "xr")] = IL("gpsimd", "dgw", 16)
                        for ic in range(NIC):
                            if ic >= 4 and (ic - 4) % 4 == 0 and (ic - 4) // 4 + 2 < 6 \
                                    and (tt8, (ic - 4) // 4 + 2) not in f_rbd:
                                emit_rb((ic - 4) // 4 + 2)
                            rs_ = (ic * 512) // 2048
                            off = (ic * 512) % 2048 + ((rs_ % 2) * 2048)
                            hslot = (ic % 2) * 512
                            qs = hslot
                            E("scalar", lambda e, off=off, qs=qs: e.activation(
                                rndc.ap().bitcast(F32)[:, qs:qs + 512], rb[:, off:off + 512],
                                AFT.Identity, bias=cbias[:, :], scale=rsh_b[:, :]),
                              waits=[f_rbd[(tt8, rs_)], f_wdT_done] +
                                    ([f_ev[(tt8, ic - 2)]] if (tt8, ic - 2) in f_ev else []) +
                                    ([f_ev.get((p_tt8, NIC - 1))] if (ic < 2 and p_tt8 is not None)
                                     else []) +
                                    ([dbg_fc[(tt8, "hq")]] if (ic == 2 and (tt8, "hq") in dbg_fc)
                                     else []))
                            f_rd = IL("scalar", "s")
                            E("vector", lambda e, qs=qs: e.tensor_scalar(
                                rndc.ap().bitcast(F32)[:, qs:qs + 512],
                                rndc.ap().bitcast(F32)[:, qs:qs + 512],
                                C_RND + 127.0, C_RND - 128.0, ALU.min, ALU.max), waits=[f_rd])
                            E("vector", lambda e, hslot=hslot, qs=qs: e.tensor_scalar(
                                hq[:, hslot:hslot + 512], rndc.ap().bitcast(F32)[:, qs:qs + 512],
                                C_RND, None, ALU.subtract))
                            f_hq = IL("vector", "v")
                            if HAVE_DBG and tt8i == 0 and ic == 0:
                                E("gpsimd", lambda e: e.dma_start(
                                    out=d_dbg2[:, 2048:2560], in_=hq[:, 0:512]),
                                  waits=[f_hq])
                                dbg_fc[(tt8, "hq")] = IL("gpsimd", "dgw", 16)
                                E("gpsimd", lambda e: e.dma_start(
                                    out=d_dbg2[:, 2560:2568], in_=sdv[:, 0:8]),
                                  waits=[f_hq])
                                IL("gpsimd", "dgw", 16)
                            hc_order = (4, 5, 6, 7, 0, 1, 2, 3) if ic == 0 else range(NCH)
                            for hi, hc in enumerate(hc_order):
                                ps = psA if hc < 4 else psB
                                pb = 512 * (hc % 4)
                                w_ = [f_hq] if hi == 0 else []
                                if ic == 0 and p_tt8 is not None:
                                    if hc == 4:
                                        w_.append(f_r2s[tt8i - 1])
                                    elif hc == 0:
                                        w_.append(f_vars[tt8i - 1])
                                    elif hc == 2:
                                        w_.append(f_ln2[("apl", p_tt8)])
                                E("tensor", lambda e, ic=ic, hc=hc, ps=ps, pb=pb, hslot=hslot:
                                  e.matmul(ps[:, pb:pb + 512],
                                           wdT[:, ic * 1024 + hc * 128: ic * 1024 + hc * 128 + 128],
                                           hq[:, hslot:hslot + 512],
                                           start=(ic == 0), stop=(ic == NIC - 1)),
                                  waits=w_)
                            f_ev[(tt8, ic)] = IL("tensor", "t")
                        f_dn = f_ev[(tt8, NIC - 1)]
                        for hc in range(NCH):
                            ps = psA if hc < 4 else psB
                            pb = 512 * (hc % 4)
                            w_e = [f_dn, f_xrd] if hc == 0 else []
                            if hc == 0 and p_tt8 is not None:
                                w_e.append(f_ln2[("apl", p_tt8)])
                                if (p_tt8, "r2f") in dbg_fc:
                                    w_e.append(dbg_fc[(p_tt8, "r2f")])
                            E("vector", lambda e, hc=hc, ps=ps, pb=pb, xpar=xpar:
                              e.scalar_tensor_tensor(
                                  r2f[:, hc * 512:(hc + 1) * 512], ps[:, pb:pb + 512],
                                  sdv[:, hc:hc + 1],
                                  xr[:, xpar * 4096 + hc * 512: xpar * 4096 + (hc + 1) * 512],
                                  ALU.mult, ALU.add), waits=w_e)
                        f_r2 = IL("vector", "v")
                        f_r2s[tt8i] = f_r2
                        if HAVE_DBG and tt8i == 0:
                            E("gpsimd", lambda e: e.dma_start(
                                out=d_dbg2[:, 7168:11264], in_=r2f[:, 0:4096]),
                              waits=[f_r2])
                            dbg_fc[(tt8, "r2f")] = IL("gpsimd", "dgw", 16)

                        f_sq2 = {}
                        for chv in range(NCH):
                            slot = (chv % 2) * 512
                            E("vector", lambda e, chv=chv, slot=slot: e.tensor_copy(
                                r2b[:, slot:slot + 512], r2f[:, chv * 512:(chv + 1) * 512]),
                              waits=([f_sq2[chv - 2]] if chv >= 2 else []))
                            f_cp = IL("vector", "v")
                            E("scalar", lambda e, slot=slot: e.activation(
                                sqc[:, slot:slot + 512], r2b[:, slot:slot + 512], AFT.Square),
                              waits=[f_cp])
                            f_sqs = IL("scalar", "s")
                            E("tensor", lambda e, chv=chv, slot=slot: e.matmul(
                                psA[0:1, 0:512], ones_s[:, 0:1], r2b[:, slot:slot + 512],
                                start=(chv == 0), stop=(chv == NCH - 1)),
                              waits=[f_cp, f_sqs])
                            E("tensor", lambda e, chv=chv, slot=slot: e.matmul(
                                psA[0:1, 512:1024], ones_s[:, 0:1], sqc[:, slot:slot + 512],
                                start=(chv == 0), stop=(chv == NCH - 1)))
                            f_sq2[chv] = IL("tensor", "t")
                        f_st2 = f_sq2[NCH - 1]
                        E("vector", lambda e: e.tensor_scalar(
                            strowc[0:1, 0:512], psA[0:1, 0:512], 1.0 / C, None, ALU.mult),
                          waits=[f_st2] + ([dbg_fc[(p_tt8, "strow")]]
                                           if (p_tt8, "strow") in dbg_fc else []))
                        E("vector", lambda e: e.tensor_scalar(
                            strowc[0:1, 512:1024], psA[0:1, 512:1024], 1.0 / C, None, ALU.mult))
                        E("vector", lambda e: e.tensor_mul(
                            strowc[0:1, 1024:1536], strowc[0:1, 0:512], strowc[0:1, 0:512]))
                        E("vector", lambda e: e.tensor_sub(
                            strowc[0:1, 512:1024], strowc[0:1, 512:1024], strowc[0:1, 1024:1536]))
                        f_var = IL("vector", "v")
                        f_vars[tt8i] = f_var
                        E("scalar", lambda e: e.activation(
                            strowc[0:1, 512:1024], strowc[0:1, 512:1024], AFT.Sqrt,
                            bias=epsb[0:1, 0:1]), waits=[f_var])
                        f_sd2 = IL("scalar", "s")
                        E("vector", lambda e: e.reciprocal(
                            strowc[0:1, 512:1024], strowc[0:1, 512:1024]), waits=[f_sd2])
                        E("vector", lambda e: e.tensor_mul(
                            strowc[0:1, 1024:1536], strowc[0:1, 0:512], strowc[0:1, 512:1024]))
                        E("vector", lambda e: e.tensor_copy(strowc_b[0:1, 0:512],
                                                            strowc[0:1, 512:1024]))
                        E("vector", lambda e: e.tensor_copy(strowc_b[0:1, 512:1024],
                                                            strowc[0:1, 1024:1536]))
                        f_rows2 = IL("vector", "v")
                        if HAVE_DBG and tt8i == 0:
                            E("gpsimd", lambda e: e.dma_start(
                                out=d_dbg2[0:1, 11264:13312], in_=strowc[0:1, 0:2048]),
                              waits=[f_rows2])
                            dbg_fc[(tt8, "strow")] = IL("gpsimd", "dgw", 16)
                        E("tensor", lambda e: e.matmul(
                            psA[:, 1024:1536], ones_s[0:1, :], strowc_b[0:1, 0:512],
                            start=True, stop=True), waits=[f_rows2])
                        E("tensor", lambda e: e.matmul(
                            psA[:, 1536:2048], ones_s[0:1, :], strowc_b[0:1, 512:1024],
                            start=True, stop=True))
                        f_bc2 = IL("tensor", "t")
                        for chv in range(NCH):
                            oslot = (chv % 2) * 512
                            w_o = []
                            if chv == 0:
                                w_o = [f_bc2, f_ln2.get(("od", p_tt8, 6))]
                            elif chv == 1:
                                w_o = [f_ln2.get(("od", p_tt8, 7))]
                            else:
                                w_o = [f_ln2.get(("od", tt8, chv - 2))]
                                if chv == 2 and (tt8, "outs") in dbg_fc:
                                    w_o.append(dbg_fc[(tt8, "outs")])
                            E("vector", lambda e, chv=chv, oslot=oslot: e.tensor_mul(
                                outs[:, oslot:oslot + 512], r2f[:, chv * 512:(chv + 1) * 512],
                                psA[:, 1024:1536]), waits=w_o)
                            E("vector", lambda e, chv=chv, oslot=oslot: e.tensor_sub(
                                outs[:, oslot:oslot + 512], outs[:, oslot:oslot + 512],
                                psA[:, 1536:2048]))
                            f_o = IL("vector", "v")
                            if HAVE_DBG and tt8i == 0 and chv == 0:
                                E("gpsimd", lambda e: e.dma_start(
                                    out=d_dbg2[:, 13312:13824], in_=outs[:, 0:512]),
                                  waits=[f_o])
                                dbg_fc[(tt8, "outs")] = IL("gpsimd", "dgw", 16)
                            E("sync", lambda e, chv=chv, oslot=oslot, s_=s_, tt=tt: e.dma_start(
                                out=d_out[s_, chv * 128:(chv + 1) * 128, tt * 512:(tt + 1) * 512],
                                in_=outs[:, oslot:oslot + 512]), waits=[f_o])
                            f_ln2[("od", tt8, chv)] = IL(
                                "sync", "dou0" if chv % 2 == 0 else "dou1", 16)
                        f_ln2[("apl", tt8)] = cc("v")
                        for hc in range(NCH):
                            f_ln2[(tt8, hc)] = cc("v")

                    MK("C_end")
                    E("sync", lambda e: e.nop(),
                      waits=[cc("dou0"), cc("dou1")] +
                            ([cc("dgw")] if HAVE_DBG else []))

        with nc.Block() as block:
            def mk(eng_name):
                def body(eng):
                    hw = {}
                    for waits, fn, inc_sem, inc_k in S.ops[eng_name]:
                        for s_n, v_ in waits:
                            if hw.get(s_n, -1) >= v_:
                                continue
                            eng.wait_ge(sems[s_n], v_)
                            hw[s_n] = v_
                        inst = fn(eng)
                        if inc_sem is not None:
                            inst.then_inc(sems[inc_sem], inc_k)
                return body
            block.sync(mk("sync"))
            block.tensor(mk("tensor"))
            block.vector(mk("vector"))
            block.scalar(mk("scalar"))
            block.gpsimd(mk("gpsimd"))
    return nc


def _host_prepare(inputs):
    bf16 = ml_dtypes.bfloat16
    x = np.asarray(inputs["L0_attn_x"], dtype=np.float32)
    wqd = np.asarray(inputs["L0_attn_Wq_down"], dtype=np.float32)
    wkd = np.asarray(inputs["L0_attn_Wk_down"], dtype=np.float32)
    wvd = np.asarray(inputs["L0_attn_Wv_down"], dtype=np.float32)
    wquc = np.asarray(inputs["L0_attn_Wq_up_c"], dtype=np.float32)
    wque = np.asarray(inputs["L0_attn_Wq_up_e"], dtype=np.float32)
    wkuc = np.asarray(inputs["L0_attn_Wk_up_c"], dtype=np.float32)
    wkue = np.asarray(inputs["L0_attn_Wk_up_e"], dtype=np.float32)
    wvu = np.asarray(inputs["L0_attn_Wv_up"], dtype=np.float32)
    wc = np.asarray(inputs["L0_attn_Wc"], dtype=np.float32)
    clipx = np.asarray(inputs["L0_ffn_clip_x"], dtype=np.float32).reshape(1, 1)
    clipd = np.asarray(inputs["L0_ffn_clip_down"], dtype=np.float32).reshape(1, 1)
    wg = np.asarray(inputs["L0_ffn_w_gate"], dtype=np.float32)
    wu = np.asarray(inputs["L0_ffn_w_up"], dtype=np.float32)
    wd = np.asarray(inputs["L0_ffn_w_down"], dtype=np.float32)

    ball = np.zeros((C, 4096), dtype=np.float32)
    perm = np.arange(32)
    perm = perm + 1 - 2 * (perm % 2)
    for g in range(4):
        for j in range(4):
            h = 4 * g + j
            r0 = 256 * g + 64 * j
            c0 = 1024 * g
            ball[r0:r0 + 64, c0 + 32 * j:c0 + 32 * j + 32] = wquc
            ball[r0:r0 + 64, c0 + 128 + 32 * j:c0 + 128 + 32 * j + 32] = wque
            ball[r0:r0 + 64, c0 + 256 + 32 * j:c0 + 256 + 32 * j + 32] = wque[:, perm]
            ball[r0:r0 + 64, c0 + 384 + 32 * j:c0 + 384 + 32 * j + 32] = wkuc
            ball[r0:r0 + 64, c0 + 512 + 32 * j:c0 + 512 + 32 * j + 32] = wkue
            ball[r0:r0 + 64, c0 + 640 + 32 * j:c0 + 640 + 32 * j + 32] = wkue[:, perm]
            ball[r0:r0 + 64, c0 + 768 + 64 * j:c0 + 768 + 64 * j + 64] = wvu

    tpos = np.arange(T, dtype=np.float64)
    invf = 1.0 / (10000.0 ** (np.arange(0, 32, 2, dtype=np.float64) / 32.0))
    ang = np.outer(tpos, invf)
    co = np.cos(ang); si = np.sin(ang)
    cos32 = np.concatenate([co, co], axis=1).T
    sin32 = np.concatenate([si, si], axis=1).T
    sgn = np.where(np.arange(32) % 2 == 0, -1.0, 1.0)
    sin32 = sin32 * sgn[:, None]
    cos_t = np.ascontiguousarray(np.tile(cos32, (4, 1))).astype(bf16)
    sin_t = np.ascontiguousarray(np.tile(sin32, (4, 1))).astype(bf16)

    dmask = np.zeros((128, 4 * 512), dtype=np.float32)
    p = np.arange(128)[:, None]
    f = np.arange(512)[None, :]
    for r in range(4):
        dmask[:, 512 * r:512 * (r + 1)] = (128 * r + p <= f).astype(np.float32)
    dmask = dmask.astype(bf16)
    ident = np.eye(128, dtype=np.float32).astype(bf16)

    shared = {
        "wqdT": np.ascontiguousarray(wqd.T), "wkdT": np.ascontiguousarray(wkd.T),
        "wvdT": np.ascontiguousarray(wvd.T), "ball": ball, "wc": wc,
        "wg": wg, "wu": wu, "wd": wd, "cos_t": cos_t, "sin_t": sin_t,
        "dmask": dmask, "ident": ident, "clipx": clipx, "clipd": clipd,
    }
    in_maps = []
    for cid in range(NCORE):
        xc = np.ascontiguousarray(np.transpose(x[2 * cid:2 * cid + 2], (0, 2, 1)))
        m = dict(shared)
        m["x"] = xc
        in_maps.append(m)
    return in_maps


def kernel(**inputs):
    in_maps = _host_prepare(inputs)
    nc = build_core()
    res = run_bass_kernel_spmd(nc, in_maps, core_ids=list(range(NCORE)))
    out = np.empty((B, T, C), dtype=np.float32)
    for cid in range(NCORE):
        o = np.asarray(res.results[cid]["out"])
        out[2 * cid:2 * cid + 2] = np.transpose(o, (0, 2, 1))
    return out
